# version 35
# baseline (speedup 1.0000x reference)
import numpy as np
import ml_dtypes
import concourse.bass as bass
import concourse.mybir as mybir
from concourse.bass_utils import run_bass_kernel_spmd
from contextlib import ExitStack

F32 = mybir.dt.float32
F16 = mybir.dt.float16
BF16 = mybir.dt.bfloat16
AX = mybir.AxisListType.X
ALU = mybir.AluOpType
ACTF = mybir.ActivationFunctionType
bfd = ml_dtypes.bfloat16

D = 1024
CH = D // 128
DI = 4096
CHI = DI // 128
NPAIR = 8
PSTR = 130
EPS = 1e-5

ENGS = ("tensor", "vector", "scalar", "sync", "gpsimd")


class Plan:
    def __init__(self):
        self.ops = {e: [] for e in ENGS}
        self.cnt = {}
        self.last_wait = {e: {} for e in ENGS}

    def op(self, eng, fn, waits=(), inc=None, inc_by=1, drain=False):
        w = []
        for sem, val in waits:
            if val is None:
                continue
            lw = self.last_wait[eng].get(sem, 0)
            if val > lw:
                w.append((sem, val))
                self.last_wait[eng][sem] = val
        val = None
        if inc is not None:
            val = self.cnt.get(inc, 0) + inc_by
            self.cnt[inc] = val
        self.ops[eng].append((fn, tuple(w), inc, inc_by, drain))
        return val

    def materialize(self, nc, sems):
        for eng, stream in self.ops.items():
            for fn, waits, inc, inc_by, drain in stream:
                for sem, val in waits:
                    assert val <= self.cnt.get(sem, 0), (
                        f"impossible wait {sem}>={val} on {eng} "
                        f"(total {self.cnt.get(sem, 0)})")
        with nc.Block() as blk:
            for eng in ENGS:
                stream = self.ops[eng]
                if not stream:
                    continue

                def mk(stream):
                    def body(e):
                        for fn, waits, inc, inc_by, drain in stream:
                            for sem, val in waits:
                                e.wait_ge(sems[sem], val)
                            if drain:
                                e.drain()
                            inst = fn(e)
                            if inc is not None:
                                inst.then_inc(sems[inc], inc_by)
                    return body

                getattr(blk, eng)(mk(stream))


def build_nc(TQ=16, TKV=32, dbg=()):
    assert TQ % 4 == 0 or TQ == 2, "stats batching assumes 4-tile groups"
    nc = bass.Bass("TRN2", target_bir_lowering=False, debug=False)
    NQ, NKV = TQ * 128, TKV * 128
    p = Plan()

    x_own = nc.dram_tensor("x_own", [NQ, D], F32, kind="ExternalInput")
    x_oth = nc.dram_tensor("x_oth", [NKV - NQ, D], F32, kind="ExternalInput")
    wf = {n: nc.dram_tensor(n, [D, D], F32, kind="ExternalInput")
          for n in ("wq", "wk", "wv", "wfq", "wfk")}
    bin_ = {n: nc.dram_tensor(n, [D], F32, kind="ExternalInput")
            for n in ("bq", "bk", "bv", "bfq", "bfk")}
    wg_f = nc.dram_tensor("wg", [DI, D], F32, kind="ExternalInput")
    wu_f = nc.dram_tensor("wu", [DI, D], F32, kind="ExternalInput")
    wd_f = nc.dram_tensor("wd", [D, DI], F32, kind="ExternalInput")
    id_b = nc.dram_tensor("id_b", [128, 128], BF16, kind="ExternalInput")
    id_f = nc.dram_tensor("id_f", [128, 128], F32, kind="ExternalInput")
    y_out = nc.dram_tensor("y", [NQ, D], F32, kind="ExternalOutput")
    s_nat = {n: nc.dram_tensor("s_" + n, [D, D], BF16, kind="Internal")
             for n in ("wfq", "wfk", "wv")}
    s_q = {"g": nc.dram_tensor("s_gq", [DI, D], BF16, kind="Internal"),
           "u": nc.dram_tensor("s_uq", [DI, D], BF16, kind="Internal"),
           "d": nc.dram_tensor("s_dq", [D, DI], BF16, kind="Internal")}
    s_x2 = nc.dram_tensor("s_x2", [NQ, D], F32, kind="Internal")
    s_hq4 = nc.dram_tensor("s_hq4", [NQ * (DI // 128), 128], BF16,
                           kind="Internal")
    s_eqT = nc.dram_tensor("s_eqT", [NQ, D], BF16, kind="Internal")

    dbg_t = {}

    def dbgT(name, shape, dt=F32):
        if name in dbg:
            dbg_t[name] = nc.dram_tensor("dbg_" + name, shape, dt,
                                         kind="ExternalOutput")

    dbgT("rows", [1, 6 * D], BF16); dbgT("rm_row", [1, NKV], BF16)
    dbgT("wcqT", [128, CH * D], BF16); dbgT("stat", [128, 6 * TKV])
    dbgT("ek0", [128, D], BF16); dbgT("zk0", [128, 1])
    dbgT("vst0", [128, NPAIR * PSTR], BF16)
    dbgT("kvs", [128, NPAIR * PSTR], BF16); dbgT("eqT0", [128, D], BF16)
    dbgT("x20", [128, D]); dbgT("bcast", [128, 9])
    dbgT("ln20", [128, D]); dbgT("aq20", [128, D], BF16)
    dbgT("hmid0", [128, DI]); dbgT("hq0", [128, DI], BF16)
    dbgT("wgT0", [128, DI], BF16); dbgT("wuT0", [128, DI], BF16)
    dbgT("aqT0", [128, D], BF16); dbgT("silu00", [128, 512])
    dbgT("sgq", [DI, 128], BF16)

    def dbg_store(name, ap, waits):
        if name in dbg_t:
            t = dbg_t[name]
            prev = p.cnt.get("dY", 0)
            p.op("sync", lambda e, t=t, ap=ap: e.dma_start(t.ap(), ap),
                 list(waits) + [("dY", prev if prev else None)], "dY", 16)

    def dbg_w():
        return [("dY", p.cnt["dY"])] if (dbg and p.cnt.get("dY")) else []

    def pe(fn, waits=(), inc=True, drain=False):
        return p.op("tensor", fn, waits, "cPE" if inc else None, 1, drain)

    def dv(fn, waits=(), inc=True, drain=False):
        return p.op("vector", fn, waits, "cDV" if inc else None, 1, drain)

    def ac(fn, waits=(), inc=True, drain=False):
        return p.op("scalar", fn, waits, "cAC" if inc else None, 1, drain)

    def dma(eng, sem, fn, waits=()):
        prev = p.cnt.get(sem, 0)
        return p.op(eng, fn, list(waits) + [(sem, prev if prev > 0 else None)],
                    sem, 16)

    ALLSEMS = ("cPE", "cDV", "cAC", "cGP", "dX", "dX1", "dS2", "dY", "dWP",
               "dWP1", "dWP2", "dWS", "dWS1", "dWS2", "dWL", "dWL1", "dWL2",
               "dWL3", "dGC", "dLA", "dLA1", "dLB", "dLB1", "dLX", "dLX1")

    def fence():
        return [(s, p.cnt[s]) for s in ALLSEMS if p.cnt.get(s)]

    st_base = ExitStack()
    st_long = ExitStack()
    st_attn = ExitStack()
    st_preq = ExitStack()
    st_prek = ExitStack()
    st_sweep = ExitStack()
    st_ffnA = ExitStack()
    st_ffnB = ExitStack()

    def sb(stk, name, shape, dt):
        return stk.enter_context(nc.sbuf_tensor(name, shape, dt))

    glob = sb(st_base, "glob", [1, 16], F32)
    GLO = dict(t0=0, t1=1, t2=2, thg=3, thu=4, thd=5, nthg=6, nthu=7, nthd=8,
               ngg=9, ngu=10, ngd=11)
    bcast = sb(st_base, "bcast", [128, 16], F32)
    m2r2 = sb(st_base, "m2r2", [128, 6 * TQ], F32)
    M2 = {n: i * TQ for i, n in enumerate(("sum", "ssq", "m", "r", "sp", "sp2"))}
    dscale = sb(st_base, "dscale", [128, TQ], F32)
    identb = sb(st_base, "identb", [128, 128], BF16)
    identf = sb(st_base, "identf", [128, 128], F32)
    ptmp = sb(st_base, "ptmp", [128, 8], F32)
    onesf_row = sb(st_base, "onesf_row", [1, 128], F32)

    xt_buf = sb(st_long, "xt_buf", [128, 2 * D], F32)
    kvs = sb(st_long, "kvs", [128, NPAIR * PSTR], BF16)
    recden = sb(st_long, "recden", [128, 16], F32)
    tmp_ctx = sb(st_long, "tmp_ctx", [128, D], F32)
    eqt_ld = sb(st_long, "eqt_ld", [128, 2 * D], BF16)

    wcqT = sb(st_attn, "wcqT", [128, CH * D], BF16)
    wckT = sb(st_attn, "wckT", [128, CH * D], BF16)
    wvT = sb(st_attn, "wvT", [128, CH * D], BF16)
    rows = sb(st_attn, "rows", [1, 6 * D], BF16)
    ROW = {n: i * D for i, n in enumerate(("wcqs", "wcks", "wvs", "bfq", "bfk", "bv"))}
    lhsT2 = sb(st_attn, "lhsT2", [2, NKV], BF16)
    rows2 = sb(st_attn, "rows2", [2, 3 * D], BF16)
    ones_row = sb(st_attn, "ones_row", [1, 128], BF16)
    ones_col = sb(st_attn, "ones_col", [128, 1], BF16)
    stat = sb(st_attn, "stat", [128, 6 * TKV], F32)
    STT = {n: i * TKV for i, n in enumerate(("sum", "ssq", "m", "r", "nrm", "sp"))}
    negrm_bf = sb(st_attn, "negrm_bf", [128, TKV], BF16)
    abscols = sb(st_attn, "abscols", [128, 3 * CHI], F32)
    bq_col = sb(st_attn, "bq_col", [128, CH], BF16)
    bk_col = sb(st_attn, "bk_col", [128, CH], BF16)
    sq_scr = sb(st_attn, "sq_scr", [128, D], F32)

    pbA = st_base.enter_context(nc.psum_tensor("pbA", [128, D], F32))
    pbV = st_base.enter_context(nc.psum_tensor("pbV", [128, D], F32))
    pbK = [st_base.enter_context(nc.psum_tensor(f"pbK{i}", [128, 512], F32))
           for i in range(3)]
    pbtr = st_base.enter_context(nc.psum_tensor("pbtr", [128, D], BF16))

    vIb = dma("sync", "dX", lambda e: e.dma_start(identb[...], id_b.ap()))
    vIf = dma("sync", "dX", lambda e: e.dma_start(identf[...], id_f.ap()))

    wq_nat = sb(st_preq, "wq_nat", [128, CH * D], BF16)
    wfqT = sb(st_preq, "wfqT", [128, CH * D], BF16)
    vGq = dma("gpsimd", "dGC", lambda e: e.dma_start(
        wq_nat[...].rearrange("p (c i) -> p c i", i=D),
        wf["wq"].ap().rearrange("(c p) i -> p c i", p=128)))
    vGn = {}
    for n in ("wfq", "wfk", "wv"):
        vGn[n] = dma("gpsimd", "dGC",
                     lambda e, n=n: e.dma_start(s_nat[n].ap(), wf[n].ap()))
    vGbq = dma("gpsimd", "dGC", lambda e: e.dma_start(
        bq_col[...], bin_["bq"].ap().rearrange("(c p) -> p c", p=128),
        allow_slow_non_contiguous=True))
    vGbk = dma("gpsimd", "dGC", lambda e: e.dma_start(
        bk_col[...], bin_["bk"].ap().rearrange("(c p) -> p c", p=128),
        allow_slow_non_contiguous=True))
    for n in ("bfq", "bfk", "bv"):
        dma("gpsimd", "dGC", lambda e, n=n: e.dma_start(
            rows[0:1, ROW[n]:ROW[n] + D],
            bin_[n].ap().rearrange("(a d) -> a d", a=1)))
    p.op("gpsimd", lambda e: e.memset(ones_row[...], 1.0), (), "cGP", 1)
    p.op("gpsimd", lambda e: e.memset(onesf_row[...], 1.0), (), "cGP", 1)
    p.op("gpsimd", lambda e: e.memset(lhsT2[...], 1.0), (), "cGP", 1)
    vGC_end = p.op("gpsimd", lambda e: e.memset(ones_col[...], 1.0), (), "cGP", 1)

    vWT = {}
    for n, tgt in (("wfq", wfqT), ("wv", wvT)):
        last = None
        for c in range(CH):
            last = dma("scalar", "dWL", lambda e, n=n, tgt=tgt, c=c: e.dma_start(
                tgt[:, c * D:(c + 1) * D],
                s_nat[n].ap()[:, c * 128:(c + 1) * 128], transpose=True),
                waits=[("dGC", vGn[n])])
        vWT[n] = last

    def emit_combine(w_nat, wfT, out, vnat, vwt, prev_ev):
        ev = prev_ev
        for mi in range(CH):
            vlast = None
            for h in range(2):
                for cj in range(CH):
                    vlast = pe(lambda e, mi=mi, h=h, cj=cj, w_nat=w_nat, wfT=wfT:
                               e.matmul(
                        pbA[:, 512 * h:512 * h + 512],
                        w_nat[:, cj * D + mi * 128: cj * D + mi * 128 + 128],
                        wfT[:, cj * D + 512 * h: cj * D + 512 * h + 512],
                        start=(cj == 0), stop=(cj == CH - 1)),
                        waits=[("dGC", vnat), ("dWL", vwt), ("cDV", ev)],
                        inc=(h == 1 and cj == CH - 1))
            ev = dv(lambda e, mi=mi, out=out: e.tensor_copy(
                out[:, mi * D:(mi + 1) * D], pbA[...]), waits=[("cPE", vlast)])
        return ev

    row_ev = [None]

    def emit_sum_row(srcT, dst_off, xw):
        vlast = None
        for h in range(2):
            for c in range(CH):
                vlast = pe(lambda e, h=h, c=c, srcT=srcT: e.matmul(
                    pbV[0:1, 512 * h:512 * h + 512], ones_col[...],
                    srcT[:, c * D + 512 * h: c * D + 512 * h + 512],
                    start=(c == 0), stop=(c == CH - 1)),
                    waits=xw + [("cGP", vGC_end), ("cDV", row_ev[-1])],
                    inc=(h == 1 and c == CH - 1))
        row_ev.append(dv(lambda e, dst_off=dst_off: e.tensor_copy(
            rows[0:1, dst_off:dst_off + D], pbV[0:1, :]), waits=[("cPE", vlast)]))

    def emit_bias_row(bcol, wfT, brow_off, dst_off, xw):
        vlast = None
        for h in range(2):
            for c in range(CH):
                vlast = pe(lambda e, h=h, c=c, bcol=bcol, wfT=wfT: e.matmul(
                    pbV[0:1, 512 * h:512 * h + 512], bcol[:, c:c + 1],
                    wfT[:, c * D + 512 * h: c * D + 512 * h + 512],
                    start=(c == 0), stop=False),
                    waits=xw + [("cGP", vGC_end), ("cDV", row_ev[-1])], inc=False)
            vlast = pe(lambda e, h=h, brow_off=brow_off: e.matmul(
                pbV[0:1, 512 * h:512 * h + 512], ones_row[0:1, 0:1],
                rows[0:1, brow_off + 512 * h: brow_off + 512 * h + 512],
                start=False, stop=True), inc=(h == 1))
        row_ev.append(dv(lambda e, dst_off=dst_off: e.tensor_copy(
            rows[0:1, dst_off:dst_off + D], pbV[0:1, :]), waits=[("cPE", vlast)]))

    vCq = emit_combine(wq_nat, wfqT, wcqT, vGq, vWT["wfq"], None)
    emit_sum_row(wcqT, ROW["wcqs"], [("cDV", vCq)])
    emit_bias_row(bq_col, wfqT, ROW["bfq"], ROW["bfq"], [])
    vQrows = row_ev[-1]
    st_preq.close()
    wk_nat = sb(st_prek, "wk_nat", [128, CH * D], BF16)
    wfkT = sb(st_prek, "wfkT", [128, CH * D], BF16)
    vGk = dma("gpsimd", "dGC", lambda e: e.dma_start(
        wk_nat[...].rearrange("p (c i) -> p c i", i=D),
        wf["wk"].ap().rearrange("(c p) i -> p c i", p=128)),
        waits=[("cDV", vQrows)])
    last = None
    for c in range(CH):
        last = dma("scalar", "dWL", lambda e, c=c: e.dma_start(
            wfkT[:, c * D:(c + 1) * D],
            s_nat["wfk"].ap()[:, c * 128:(c + 1) * 128], transpose=True),
            waits=[("dGC", vGn["wfk"]), ("cDV", vQrows)])
    vWT["wfk"] = last
    vCk = emit_combine(wk_nat, wfkT, wckT, vGk, vWT["wfk"], row_ev[-1])
    emit_sum_row(wckT, ROW["wcks"], [("cDV", vCk)])
    emit_sum_row(wvT, ROW["wvs"], [("dWL", vWT["wv"])])
    emit_bias_row(bk_col, wfkT, ROW["bfk"], ROW["bfk"], [])
    dma("sync", "dX", lambda e: e.dma_start(
        rows2[0:1, :], rows[0:1, 0:3 * D]), waits=[("cDV", row_ev[-1])])
    vRows2 = dma("sync", "dX", lambda e: e.dma_start(
        rows2[1:2, :], rows[0:1, 3 * D:6 * D]), waits=[("cDV", row_ev[-1])])
    vRows = row_ev[-1]
    st_prek.close()
    dbg_store("rows", rows[...], [("cDV", vRows)])
    dbg_store("wcqT", wcqT[...], [("cDV", vCq)])

    def x_tile_ap(t):
        if t < TQ:
            return x_own.ap()[t * 128:(t + 1) * 128, :]
        return x_oth.ap()[(t - TQ) * 128:(t - TQ + 1) * 128, :]

    evtr_state = [None]
    stats_marks = {}

    def emit_stats(t, vL, xsem, xap):
        c_ = STT
        vSum = dv(lambda e, t=t, xap=xap: e.reduce_sum(
            stat[:, c_["sum"] + t:c_["sum"] + t + 1], xap, axis=AX),
            waits=[(xsem, vL)])
        vSq = dv(lambda e, t=t, xap=xap: e.scalar_tensor_tensor(
            sq_scr[...], xap, 0.0, xap, op0=ALU.add, op1=ALU.mult,
            accum_out=stat[:, c_["ssq"] + t:c_["ssq"] + t + 1]), drain=True)
        dv(lambda e, t=t: e.tensor_single_scalar(
            stat[:, c_["m"] + t:c_["m"] + t + 1],
            stat[:, c_["sum"] + t:c_["sum"] + t + 1], 1.0 / D, op=ALU.mult),
            drain=True)
        dv(lambda e, t=t: e.tensor_tensor(
            stat[:, c_["sp"] + t:c_["sp"] + t + 1],
            stat[:, c_["m"] + t:c_["m"] + t + 1],
            stat[:, c_["m"] + t:c_["m"] + t + 1], op=ALU.mult), drain=True)
        dv(lambda e, t=t: e.scalar_tensor_tensor(
            stat[:, c_["sum"] + t:c_["sum"] + t + 1],
            stat[:, c_["ssq"] + t:c_["ssq"] + t + 1], 1.0 / D,
            stat[:, c_["sp"] + t:c_["sp"] + t + 1],
            op0=ALU.mult, op1=ALU.subtract), drain=True)
        vVe = dv(lambda e, t=t: e.tensor_single_scalar(
            stat[:, c_["ssq"] + t:c_["ssq"] + t + 1],
            stat[:, c_["sum"] + t:c_["sum"] + t + 1], EPS, op=ALU.add),
            drain=True)
        vLn = ac(lambda e, t=t: e.activation(
            stat[:, c_["sp"] + t:c_["sp"] + t + 1],
            stat[:, c_["ssq"] + t:c_["ssq"] + t + 1], ACTF.Ln,
            bias=0.0, scale=1.0), waits=[("cDV", vVe)], drain=True)
        vR = ac(lambda e, t=t: e.activation(
            stat[:, c_["r"] + t:c_["r"] + t + 1],
            stat[:, c_["sp"] + t:c_["sp"] + t + 1], ACTF.Exp,
            bias=0.0, scale=-0.5), drain=True)
        vNR = dv(lambda e, t=t: e.scalar_tensor_tensor(
            stat[:, c_["nrm"] + t:c_["nrm"] + t + 1],
            stat[:, c_["r"] + t:c_["r"] + t + 1], -1.0,
            stat[:, c_["m"] + t:c_["m"] + t + 1],
            op0=ALU.mult, op1=ALU.mult), waits=[("cAC", vR)], drain=True)
        vNB = dv(lambda e, t=t: e.tensor_copy(
            negrm_bf[:, t:t + 1], stat[:, c_["nrm"] + t:c_["nrm"] + t + 1]),
            drain=True)
        vTr = pe(lambda e, t=t: e.transpose(
            pbtr[0:1, 0:128], negrm_bf[:, t:t + 1], identb[...]),
            waits=[("cDV", vNB), ("dX", vIb), ("cDV", evtr_state[0])])
        evtr_state[0] = dv(lambda e, t=t: e.tensor_copy(
            lhsT2[0:1, t * 128:(t + 1) * 128], pbtr[0:1, 0:128]),
            waits=[("cPE", vTr)])
        stats_marks[t] = (vSq, vR)

    pchunk = sb(st_sweep, "pchunk", [128, 8 * D], F32)
    pp_buf = sb(st_sweep, "pp", [128, 4 * D], F32)
    pwq = sb(st_sweep, "pwq", [128, 8 * D], BF16)
    ppb = sb(st_sweep, "ppb", [128, 8 * D], BF16)
    UPW = CHI // 4
    prep_chunks = []
    for wi, wdram in enumerate((wg_f, wu_f)):
        for u in range(UPW):
            prep_chunks.append((
                wi, wdram,
                lambda a, u=u: a[u * 512:(u + 1) * 512, :]
                .rearrange("(c p) i -> p c i", p=128),
                lambda b, s: b[:, s * 4 * D:(s + 1) * 4 * D]
                .rearrange("p (c i) -> p c i", i=D)))
    for u in range(UPW):
        prep_chunks.append((
            2, wd_f,
            lambda a, u=u: a[u * 128:(u + 1) * 128, :],
            lambda b, s: b[:, s * 4 * D:(s + 1) * 4 * D]))
    prep1_ac = {}
    prep1_ld = {}

    def emit_prep1_load(k):
        wi, wdram, sl_, dst_ = prep_chunks[k]
        s = k % 2
        sem = ("dWP", "dWP1")[s]
        prep1_ld[k] = dma("scalar", sem,
                          lambda e, wdram=wdram, sl_=sl_, dst_=dst_, s=s:
                          e.dma_start(dst_(pchunk, s), sl_(wdram.ap())),
                          waits=[("cAC", prep1_ac.get(k - 2)), ("cDV", vRows)])

    def emit_prep1_abs(k):
        s = k % 2
        sem = ("dWP", "dWP1")[s]
        prep1_ac[k] = ac(lambda e, s=s, k=k: e.activation(
            pp_buf[...], pchunk[:, s * 4 * D:(s + 1) * 4 * D], ACTF.Abs,
            bias=0.0, scale=1.0, accum_out=abscols[:, k:k + 1]),
            waits=[(sem, prep1_ld[k])], drain=True)

    fin_done = [False]
    prep2_i = [0]
    fin_state = {}

    def emit_finalize():
        fin_done[0] = True
        vAC_prep = prep1_ac[len(prep_chunks) - 1]
        pbtr32 = pbtr[...].bitcast(F32)
        tr_prev = [evtr_state[0]]
        vGlo = None
        tr_ev = None
        for wi, (nel, thn, nthn, ngn) in enumerate((
                (DI * D, "thg", "nthg", "ngg"), (DI * D, "thu", "nthu", "ngu"),
                (D * DI, "thd", "nthd", "ngd"))):
            vA = dv(lambda e, wi=wi: e.reduce_sum(
                ptmp[:, wi:wi + 1], abscols[:, wi * UPW:(wi + 1) * UPW],
                axis=AX), waits=[("cAC", vAC_prep)], drain=True)
            vT = pe(lambda e, wi=wi, pbtr32=pbtr32: e.transpose(
                pbtr32[0:1, wi * 128:wi * 128 + 128], ptmp[:, wi:wi + 1],
                identf[...]),
                waits=[("cDV", vA), ("dX", vIf), ("cDV", tr_ev),
                       ("cDV", tr_prev[0])])
            vS = dv(lambda e, wi=wi, pbtr32=pbtr32: e.reduce_sum(
                glob[0:1, wi:wi + 1], pbtr32[0:1, wi * 128:wi * 128 + 128],
                axis=AX), waits=[("cPE", vT)], drain=True)
            tr_ev = vS
            dv(lambda e, wi=wi, nel=nel: e.tensor_single_scalar(
                glob[0:1, wi:wi + 1], glob[0:1, wi:wi + 1], 1.0 / nel,
                op=ALU.mult), drain=True)
            dv(lambda e, wi=wi: e.tensor_single_scalar(
                glob[0:1, wi:wi + 1], glob[0:1, wi:wi + 1], EPS, op=ALU.max),
                drain=True)
            dv(lambda e, wi=wi, thn=thn: e.tensor_single_scalar(
                glob[0:1, GLO[thn]:GLO[thn] + 1], glob[0:1, wi:wi + 1], 0.5,
                op=ALU.mult), drain=True)
            dv(lambda e, thn=thn, nthn=nthn: e.tensor_single_scalar(
                glob[0:1, GLO[nthn]:GLO[nthn] + 1],
                glob[0:1, GLO[thn]:GLO[thn] + 1], -1.0, op=ALU.mult),
                drain=True)
            vGlo = dv(lambda e, wi=wi, ngn=ngn: e.tensor_single_scalar(
                glob[0:1, GLO[ngn]:GLO[ngn] + 1], glob[0:1, wi:wi + 1], 0.5,
                op=ALU.mult), drain=True)
        vB_ = None
        for i in range(9):
            vB_ = pe(lambda e, i=i, pbtr32=pbtr32: e.matmul(
                pbtr32[:, 384 + i:385 + i], onesf_row[...],
                glob[0:1, 3 + i:4 + i], start=True, stop=True),
                waits=([("cDV", vGlo), ("cGP", vGC_end)]) if i == 0 else (),
                inc=(i == 8))
        fin_state["vB"] = vB_
        vBC_ = dv(lambda e, pbtr32=pbtr32: e.tensor_copy(
            bcast[:, 3:12], pbtr32[:, 384:393]), waits=[("cPE", vB_)])
        fin_state["vBC"] = vBC_
        evtr_state[0] = vBC_
        dbg_store("bcast", bcast[:, 3:12], [("cDV", vBC_)])

    prep2_dv = {}
    prep2_st = {}
    prep2_ac = {}
    prep2_ld = {}
    p2l, p2s, p2a = [0], [0], [0]
    p1l, p1a = [0], [0]
    NP2 = 3 * UPW


    def emit_prep2_load(k):
        wi, wdram, sl_, dst_ = prep_chunks[k]
        s = k % 2
        lsem = ("dWP", "dWP1")[s]
        prep2_ld[k] = dma("scalar", lsem,
                          lambda e, wdram=wdram, sl_=sl_, dst_=dst_, s=s:
                          e.dma_start(dst_(pchunk, s), sl_(wdram.ap())),
                          waits=[("cAC", prep2_ac.get(k - 2)),
                                 ("cAC", prep1_ac[len(prep_chunks) - 2 + s])])

    def emit_prep2_signs(k):
        wi, wdram, sl_, dst_ = prep_chunks[k]
        thn = ("thg", "thu", "thd")[wi]
        nthn = ("nthg", "nthu", "nthd")[wi]
        s = k % 2
        lsem = ("dWP", "dWP1")[s]
        ssem = ("dWS", "dWS1")[s]
        SGP = ppb[:, s * 4 * D:(s + 1) * 4 * D]
        PWQ = pwq[:, s * 4 * D:(s + 1) * 4 * D]
        ac(lambda e, s=s, thn=thn, SGP=SGP: e.activation(
            SGP, pchunk[:, s * 4 * D:(s + 1) * 4 * D], ACTF.Sign,
            bias=bcast[:, GLO[thn]:GLO[thn] + 1], scale=1.0),
            waits=[(lsem, prep2_ld[k]), ("cDV", fin_state["vBC"]),
                   ("cGP", prep2_dv.get(k - 2))], drain=True)
        vSM = ac(lambda e, s=s, nthn=nthn, PWQ=PWQ: e.activation(
            PWQ, pchunk[:, s * 4 * D:(s + 1) * 4 * D], ACTF.Sign,
            bias=bcast[:, GLO[nthn]:GLO[nthn] + 1], scale=1.0),
            waits=[(ssem, prep2_st.get(k - 2))], drain=True)
        prep2_ac[k] = vSM

    def emit_prep2_add(k):
        wi, wdram, sl_, dst_ = prep_chunks[k]
        sdram = (s_q["g"], s_q["u"], s_q["d"])[wi]
        s = k % 2
        ssem = ("dWS", "dWS1")[s]
        SGP = ppb[:, s * 4 * D:(s + 1) * 4 * D]
        PWQ = pwq[:, s * 4 * D:(s + 1) * 4 * D]
        vW = p.op("gpsimd", lambda e, SGP=SGP, PWQ=PWQ: e.tensor_tensor(
            PWQ, SGP, PWQ, op=ALU.add),
            [("cAC", prep2_ac[k])], "cGP", 1)
        prep2_dv[k] = vW
        prep2_st[k] = dma("sync", ssem,
                          lambda e, sdram=sdram, sl_=sl_, dst_=dst_, s=s:
                          e.dma_start(sl_(sdram.ap()), dst_(pwq, s)),
                          waits=[("cGP", vW)])

    p2s_hist = [0]

    def prep2_drain_adds(upto=None):
        tgt = p2s[0] if upto is None else upto
        while p2a[0] < tgt:
            emit_prep2_add(p2a[0])
            p2a[0] += 1

    xr_buf = sb(st_sweep, "xr_buf", [128, 2 * D], BF16)
    xrT = sb(st_sweep, "xrT", [128, 2 * D], BF16)
    ek_buf = sb(st_sweep, "ek_buf", [128, 2 * D], BF16)
    vstage = sb(st_sweep, "vstage", [128, 2 * NPAIR * PSTR], BF16)
    eq_buf = sb(st_sweep, "eq_buf", [128, D], BF16)
    nmax = sb(st_sweep, "nmax", [128, 4], F32)
    zkb = sb(st_sweep, "zkb", [128, 2], F32)
    rzkb = sb(st_sweep, "rzkb", [128, 2], F32)
    eqt_st = sb(st_sweep, "eqt_st", [128, 2 * D], BF16)

    vZst = p.op("gpsimd", lambda e: e.memset(vstage[...], 0.0),
                [("cDV", vRows)], "cGP", 1)

    def emit_proj(pb, XRT, wT, sum_off, bias_off, t, xw):
        vlast = None
        for h in range(2):
            for c in range(CH):
                vlast = pe(lambda e, h=h, c=c, pb=pb, XRT=XRT, wT=wT: e.matmul(
                    pb[:, 512 * h:512 * h + 512],
                    XRT[:, c * 128:(c + 1) * 128],
                    wT[:, c * D + 512 * h: c * D + 512 * h + 512],
                    start=(c == 0), stop=False),
                    waits=xw if (h == 0 and c == 0) else (), inc=False)
            vlast = pe(lambda e, h=h, t=t, sum_off=sum_off, pb=pb: e.matmul(
                pb[:, 512 * h:512 * h + 512],
                lhsT2[0:2, t * 128:(t + 1) * 128],
                rows2[0:2, sum_off + 512 * h: sum_off + 512 * h + 512],
                start=False, stop=True),
                waits=[("dX", vRows2)], inc=(h == 1))
        return vlast

    xr_dv = {}
    pbA_free = []
    pbV_free = []
    kv_last = None
    prep_i = 0
    vEK_last = None
    eqt_store = {}

    x_loads = {}

    def emit_xload(t):
        s = t % 2
        xw_extra = ([("cDV", xr_dv[t - 2]), ("cAC", stats_marks[t - 2][1])]
                    if t >= 2 else [])
        xw_sem = ("dX", "dX1")[s]
        vL = dma("sync", xw_sem, lambda e, t=t, s=s: e.dma_start(
            xt_buf[:, s * D:(s + 1) * D], x_tile_ap(t)),
            waits=xw_extra + dbg_w())
        x_loads[t] = (vL, xw_sem)
        return vL, xw_sem

    vL0, sem0 = emit_xload(0)
    emit_stats(0, vL0, sem0, xt_buf[:, 0:D])

    def emit_next_tile_stats(t):
        if t + 1 < TKV:
            vLn_, semn = emit_xload(t + 1)
            emit_stats(t + 1, vLn_, semn,
                       xt_buf[:, ((t + 1) % 2) * D:(((t + 1) % 2) + 1) * D])

    for t in range(TKV):
        s = t % 2
        vL, xw_sem = x_loads[t]
        xap = xt_buf[:, s * D:(s + 1) * D]
        XR = xr_buf[:, s * D:(s + 1) * D]
        XRT = xrT[:, s * D:(s + 1) * D]
        EK = ek_buf[:, s * D:(s + 1) * D]
        VST = vstage[:, s * NPAIR * PSTR:(s + 1) * NPAIR * PSTR]

        xr_dv[t] = dv(lambda e, xap=xap, t=t, XR=XR: e.tensor_single_scalar(
            XR, xap, stat[:, STT["r"] + t:STT["r"] + t + 1], op=ALU.mult),
            waits=[(xw_sem, vL), ("cAC", stats_marks[t][1])])

        vT = None
        for j in range(CH):
            vT = pe(lambda e, j=j, XR=XR: e.transpose(
                pbtr[:, j * 128:(j + 1) * 128], XR[:, j * 128:(j + 1) * 128],
                identb[...]),
                waits=[("cDV", xr_dv[t]), ("cDV", evtr_state[0])]
                if j == 0 else (),
                inc=(j == CH - 1))
        evtr_state[0] = dv(lambda e, XRT=XRT: e.tensor_copy(XRT, pbtr[...]),
                           waits=[("cPE", vT)])
        vXRT = evtr_state[0]

        vK = emit_proj(pbA, XRT, wckT, ROW["wcks"], ROW["bfk"], t,
                       [("cDV", vXRT)] + pbA_free)
        vNM = dv(lambda e, s=s: e.reduce_max(
            nmax[:, s:s + 1], pbA[...], axis=AX, negate=True),
            waits=[("cPE", vK)])
        vEK = ac(lambda e, s=s, EK=EK: e.activation(
            EK, pbA[...], ACTF.Exp, bias=nmax[:, s:s + 1], scale=1.0,
            accum_out=zkb[:, s:s + 1]), waits=[("cPE", vK), ("cDV", vNM)])
        pbA_free = [("cAC", vEK), ("cDV", vNM)]
        vEK_last = vEK
        vRZ = dv(lambda e, s=s: e.reciprocal(rzkb[:, s:s + 1], zkb[:, s:s + 1]),
                 waits=[("cAC", vEK)])
        vV = emit_proj(pbV, XRT, wvT, ROW["wvs"], ROW["bv"], t, pbV_free)
        vS1 = dv(lambda e, s=s, VST=VST: e.tensor_scalar(
            VST.rearrange("p (c k) -> p c k", k=PSTR)[:, :, 0:128],
            pbV[...].rearrange("p (c k) -> p c k", k=128),
            rzkb[:, s:s + 1], None, op0=ALU.mult),
            waits=[("cPE", vV), ("cGP", vZst)], drain=True)
        vS2 = dv(lambda e, s=s, VST=VST: e.tensor_copy(
            VST.rearrange("p (c k) -> p c k", k=PSTR)[:, :, 128:129],
            rzkb[:, s:s + 1].rearrange("p (c k) -> p c k", c=1)
            .broadcast_to([128, NPAIR, 1])))
        pbV_free = [("cDV", vS1)]
        if t == 0:
            dbg_store("ek0", EK, [("cAC", vEK)])
            dbg_store("zk0", zkb[:, 0:1], [("cAC", vEK)])
            dbg_store("vst0", VST, [("cDV", vS2)])

        vQ = None
        if t < TQ:
            vQ = emit_proj(pbA, XRT, wcqT, ROW["wcqs"], ROW["bfq"], t, pbA_free)
        for pr in range(NPAIR):
            b, slot = pr // 3, PSTR * (pr % 3)
            kv_last = pe(lambda e, pr=pr, b=b, slot=slot, EK=EK, VST=VST,
                         st_=(t == 0 and pr % 3 == 0),
                         sp_=(t == TKV - 1 and (pr % 3 == 2 or pr == NPAIR - 1)):
                         e.matmul(
                pbK[b][:, slot:slot + 129], EK[:, pr * 128:(pr + 1) * 128],
                VST[:, pr * PSTR:pr * PSTR + 129],
                start=st_, stop=sp_),
                waits=[("cAC", vEK), ("cDV", vS2)] if pr == 0 else (),
                inc=(pr == NPAIR - 1))

        emit_next_tile_stats(t)

        if t < TQ:
            vNMq = dv(lambda e, s=s: e.reduce_max(
                nmax[:, 2 + s:3 + s], pbA[...], axis=AX, negate=True),
                waits=[("cPE", vQ)])
            vEQ = ac(lambda e, s=s: e.activation(
                eq_buf[...], pbA[...], ACTF.Exp, bias=nmax[:, 2 + s:3 + s],
                scale=1.0), waits=[("cPE", vQ), ("cDV", vNMq)])
            pbA_free = [("cAC", vEQ), ("cDV", vNMq)]
            vTq = None
            for j in range(CH):
                vTq = pe(lambda e, j=j: e.transpose(
                    pbtr[:, j * 128:(j + 1) * 128],
                    eq_buf[:, j * 128:(j + 1) * 128], identb[...]),
                    waits=[("cAC", vEQ), ("cDV", evtr_state[0])]
                    if j == 0 else (),
                    inc=(j == CH - 1))
            evtr_state[0] = dv(lambda e, t=t, s=s: e.tensor_copy(
                eqt_st[:, s * D:(s + 1) * D], pbtr[...]),
                waits=[("cPE", vTq), ("dS2", eqt_store.get(t - 2))]
                + dbg_w())
            eqt_store[t] = dma("sync", "dS2", lambda e, t=t, s=s: e.dma_start(
                s_eqT.ap()[t * 128:(t + 1) * 128, :],
                eqt_st[:, s * D:(s + 1) * D]),
                waits=[("cDV", evtr_state[0])])
            if t == 0:
                dbg_store("eqT0", eqt_st[:, 0:D], [("cDV", evtr_state[0])])

        prep2_drain_adds(upto=p2s[0])
        for _ in range(2):
            if p1a[0] < p1l[0]:
                emit_prep1_abs(p1a[0])
                p1a[0] += 1
            elif p1a[0] >= len(prep_chunks) and not fin_done[0]:
                emit_finalize()
            elif fin_done[0] and p2s[0] < p2l[0]:
                emit_prep2_signs(p2s[0])
                p2s[0] += 1
        for _ in range(2):
            if prep_i < len(prep_chunks) and prep_i - p1a[0] < 2:
                emit_prep1_load(prep_i)
                prep_i += 1
                p1l[0] += 1
        if fin_done[0]:
            for _ in range(2):
                if p2l[0] < NP2 and p2l[0] - p2s[0] < 2:
                    emit_prep2_load(p2l[0])
                    p2l[0] += 1

    while p1a[0] < len(prep_chunks):
        if prep_i < len(prep_chunks) and prep_i - p1a[0] < 2:
            emit_prep1_load(prep_i)
            prep_i += 1
            p1l[0] += 1
        emit_prep1_abs(p1a[0])
        p1a[0] += 1
    if not fin_done[0]:
        emit_finalize()
    while p2a[0] < NP2:
        prep2_drain_adds()
        if p2s[0] < p2l[0]:
            emit_prep2_signs(p2s[0])
            p2s[0] += 1
        elif p2l[0] < NP2 and p2l[0] - p2s[0] < 2:
            emit_prep2_load(p2l[0])
            p2l[0] += 1
    vEQT_done = evtr_state[0]

    vKZ = p.op("gpsimd", lambda e: e.memset(kvs[...], 0.0),
               [("cDV", vRows)], "cGP", 1)
    kv_evac = None
    for pr in range(NPAIR):
        b, slot = pr // 3, PSTR * (pr % 3)
        w0 = [("cPE", kv_last), ("cGP", vKZ)] if pr == 0 else []
        dv(lambda e, pr=pr, b=b, slot=slot: e.tensor_copy(
            kvs[0:64, pr * PSTR:pr * PSTR + 64], pbK[b][0:64, slot:slot + 64]),
            waits=w0)
        dv(lambda e, pr=pr, b=b, slot=slot: e.tensor_copy(
            kvs[64:128, pr * PSTR + 64:pr * PSTR + 128],
            pbK[b][64:128, slot + 64:slot + 128]))
        dv(lambda e, pr=pr, b=b, slot=slot: e.tensor_copy(
            kvs[0:64, pr * PSTR + 128:pr * PSTR + 129],
            pbK[b][0:64, slot + 128:slot + 129]))
        kv_evac = dv(lambda e, pr=pr, b=b, slot=slot: e.tensor_copy(
            kvs[64:128, pr * PSTR + 129:pr * PSTR + 130],
            pbK[b][64:128, slot + 128:slot + 129]))
    dbg_store("kvs", kvs[...], [("cDV", kv_evac)])

    st_sweep.close()
    st_attn.close()

    wgT = sb(st_ffnA, "wgT", [128, CH * DI], BF16)
    wuT = sb(st_ffnA, "wuT", [128, CH * DI], BF16)
    ln2b = sb(st_ffnA, "ln2b", [128, D], F32)
    tq16 = sb(st_ffnA, "tq16", [128, 2 * D], F16)
    aq2 = sb(st_ffnA, "aq2", [128, D], BF16)
    aq2T = sb(st_ffnA, "aq2T", [128, 2 * D], BF16)
    silu_b = sb(st_ffnA, "silu_b", [128, 2 * 512], F32)
    hmid = sb(st_ffnA, "hmid", [128, DI], F32)
    hqb = sb(st_ffnA, "hqb", [128, DI], BF16)
    smal = sb(st_ffnA, "smal", [128, 32], F32)
    hmx = sb(st_ffnA, "hmx", [128, 16], F32)

    fA = fence()

    def _wdone(last):
        out = []
        for par, sem in enumerate(("dWS", "dWS1")):
            ks = [k for k in range(last + 1) if k % 2 == par and k in prep2_st]
            if ks:
                out.append((sem, prep2_st[max(ks)]))
        return out

    wg_done = _wdone(UPW - 1)
    wu_done = _wdone(2 * UPW - 1)
    wd_done = _wdone(3 * UPW - 1)

    eqt_loads = {}
    ctx_x_loads = {}
    ctx_pe = {}
    x2_st = {}
    ln2_ac = {}
    st1v = {}
    lnA_dv = {}

    def emit_eqt_load(t):
        if t >= TQ:
            return
        s = t % 2
        eqt_loads[t] = dma("sync", ("dLA", "dLA1")[s],
                           lambda e, t=t, s=s: e.dma_start(
            eqt_ld[:, s * D:(s + 1) * D], s_eqT.ap()[t * 128:(t + 1) * 128, :]),
            waits=[("dS2", eqt_store[t]), ("cPE", ctx_pe.get(t - 2))]
            + dbg_w())

    def emit_ctx_xload(t):
        if t >= TQ:
            return
        s = t % 2
        xw = ([("cDV", xr_dv[TKV - 1]), ("cAC", stats_marks[TKV - 1][1])]
              if t < 2 else
              [("cAC", ln2_ac.get(t - 2)), ("dS2", x2_st.get(t - 2)),
               ("cDV", lnA_dv.get(t - 2))])
        ctx_x_loads[t] = dma("sync", ("dX", "dX1")[s],
                             lambda e, t=t, s=s: e.dma_start(
            xt_buf[:, s * D:(s + 1) * D], x_tile_ap(t)), waits=xw + dbg_w())

    emit_ctx_xload(0)
    emit_eqt_load(0)
    emit_eqt_load(1)
    emit_ctx_xload(1)

    wgld, wuld = {}, {}
    for c in range(CH):
        wgld[c] = (("dWL", "dWL1")[c % 2], dma(
            "sync", ("dWL", "dWL1")[c % 2], lambda e, c=c: e.dma_start(
                wgT[:, c * DI:(c + 1) * DI],
                s_q["g"].ap()[:, c * 128:(c + 1) * 128],
                transpose=True), waits=wg_done + fA))
    for c in range(CH):
        wuld[c] = (("dWL", "dWL1")[c % 2], dma(
            "sync", ("dWL", "dWL1")[c % 2], lambda e, c=c: e.dma_start(
                wuT[:, c * DI:(c + 1) * DI],
                s_q["u"].ap()[:, c * 128:(c + 1) * 128],
                transpose=True), waits=wu_done + fA))

    GB = [pbA[:, 0:512], pbA[:, 512:D]]
    UB = [pbV[:, 0:512], pbV[:, 512:D]]
    gb_free = [pbA_free[:], pbA_free[:]]
    ub_free = [pbV_free[:], pbV_free[:]]
    silu_free = [[], []]
    tr_free = [("cDV", vEQT_done)]
    hq_st = {}
    vHQ_last = None
    ctx_banks_free = [("cDV", kv_evac)]
    ctx_vm = {}

    def emit_ctx(t):
        nonlocal ctx_banks_free
        s = t % 2
        X2 = xt_buf[:, s * D:(s + 1) * D]
        vC = None
        for pr in range(NPAIR):
            b, slot = pr // 3, PSTR * (pr % 3)
            vC = pe(lambda e, pr=pr, b=b, slot=slot, t=t, s=s: e.matmul(
                pbK[b][:, slot:slot + 130],
                eqt_ld[:, s * D + pr * 128:s * D + (pr + 1) * 128],
                kvs[:, pr * PSTR:pr * PSTR + 130],
                start=(pr % 3 == 0), stop=(pr % 3 == 2 or pr == NPAIR - 1)),
                waits=(ctx_banks_free + [("cDV", vEQT_done),
                                         (("dLA", "dLA1")[s], eqt_loads[t])])
                if pr == 0 else (),
                inc=(pr == NPAIR - 1))
        ctx_pe[t] = vC
        vRD = None
        for b in range(3):
            npr = 3 if b < 2 else 2
            vRD = dv(lambda e, b=b, npr=npr: e.reciprocal(
                recden[:, 6 * b:6 * b + 2 * npr]
                .rearrange("p (c k) -> p c k", k=2),
                pbK[b][:, 0:npr * PSTR].rearrange("p (c k) -> p c k", k=PSTR)
                [:, :, 128:130]),
                waits=[("cPE", vC)] if b == 0 else ())
        vM = None
        for b in range(3):
            npr = 3 if b < 2 else 2
            vM = dv(lambda e, b=b, npr=npr: e.tensor_tensor(
                tmp_ctx[:, b * 384:b * 384 + npr * 128]
                .rearrange("p (c h k) -> p c h k", h=2, k=64),
                pbK[b][:, 0:npr * PSTR].rearrange("p (c k) -> p c k", k=PSTR)
                [:, :, 0:128].rearrange("p c (h k) -> p c h k", k=64),
                recden[:, 6 * b:6 * b + 2 * npr]
                .rearrange("p (c h k) -> p c h k", h=2, k=1)
                .broadcast_to([128, npr, 2, 64]),
                op=ALU.mult),
                waits=([("cDV", vRD)] + ([("cAC", ln2_ac[t - 1])]
                                         if t >= 1 else []))
                if b == 0 else (),
                drain=(b == 0))
        ctx_vm[t] = vM
        vAdd = dv(lambda e, X2=X2: e.tensor_add(X2, X2, tmp_ctx[...]),
                  waits=[(("dX", "dX1")[s], ctx_x_loads[t])], drain=True)
        ctx_banks_free = [("cDV", vM)]
        x2_st[t] = dma("sync", "dS2", lambda e, t=t, X2=X2: e.dma_start(
            s_x2.ap()[t * 128:(t + 1) * 128, :], X2), waits=[("cDV", vAdd)])
        ac(lambda e, t=t, X2=X2: e.activation(
            tmp_ctx[...], X2, ACTF.Copy, bias=0.0, scale=1.0,
            accum_out=m2r2[:, M2["sum"] + t:M2["sum"] + t + 1]),
            waits=[("cDV", vAdd)], drain=True)
        ln2_ac[t] = ac(lambda e, t=t, X2=X2: e.activation(
            tmp_ctx[...], X2, ACTF.Square, bias=0.0, scale=1.0,
            accum_out=m2r2[:, M2["ssq"] + t:M2["ssq"] + t + 1]),
            waits=[("cDV", vAdd)], drain=True)
        if t == 0:
            dbg_store("x20", X2, [("cDV", vAdd)])

    def emit_ln2fin(t):
        c2 = M2
        dv(lambda e, t=t: e.tensor_single_scalar(
            m2r2[:, c2["m"] + t:c2["m"] + t + 1],
            m2r2[:, c2["sum"] + t:c2["sum"] + t + 1],
            1.0 / D, op=ALU.mult), waits=[("cAC", ln2_ac[t])], drain=True)
        dv(lambda e, t=t: e.tensor_tensor(
            m2r2[:, c2["sp"] + t:c2["sp"] + t + 1],
            m2r2[:, c2["m"] + t:c2["m"] + t + 1],
            m2r2[:, c2["m"] + t:c2["m"] + t + 1], op=ALU.mult), drain=True)
        dv(lambda e, t=t: e.scalar_tensor_tensor(
            m2r2[:, c2["sp2"] + t:c2["sp2"] + t + 1],
            m2r2[:, c2["ssq"] + t:c2["ssq"] + t + 1], 1.0 / D,
            m2r2[:, c2["sp"] + t:c2["sp"] + t + 1],
            op0=ALU.mult, op1=ALU.subtract), drain=True)
        dv(lambda e, t=t: e.tensor_single_scalar(
            m2r2[:, c2["sp"] + t:c2["sp"] + t + 1],
            m2r2[:, c2["sp2"] + t:c2["sp2"] + t + 1],
            EPS, op=ALU.add), drain=True)
        u5 = dv(lambda e, t=t: e.reciprocal(
            m2r2[:, c2["sp2"] + t:c2["sp2"] + t + 1],
            m2r2[:, c2["sp"] + t:c2["sp"] + t + 1]), drain=True)
        st1v[t] = dict(vR2=ac(lambda e, t=t: e.activation(
            m2r2[:, c2["r"] + t:c2["r"] + t + 1],
            m2r2[:, c2["sp2"] + t:c2["sp2"] + t + 1],
            ACTF.Sqrt, bias=0.0, scale=1.0), waits=[("cDV", u5)], drain=True))

    def stage1_front_a(t):
        s = t % 2
        sm = 16 * s
        X2 = xt_buf[:, s * D:(s + 1) * D]
        vLN = dv(lambda e, X2=X2, t=t: e.tensor_scalar(
            ln2b[...], X2, m2r2[:, M2["m"] + t:M2["m"] + t + 1],
            m2r2[:, M2["r"] + t:M2["r"] + t + 1],
            op0=ALU.subtract, op1=ALU.mult),
            waits=[("cAC", st1v[t]["vR2"])] + dbg_w(), drain=True)
        lnA_dv[t] = vLN
        dv(lambda e, sm=sm: e.reduce_max(smal[:, sm:sm + 1], ln2b[...],
                                         axis=AX, apply_absolute_value=True),
           drain=True)
        dv(lambda e, sm=sm: e.tensor_single_scalar(
            smal[:, sm + 1:sm + 2], smal[:, sm:sm + 1], EPS, op=ALU.max),
            drain=True)
        dv(lambda e, sm=sm: e.reciprocal(smal[:, sm + 2:sm + 3],
                                         smal[:, sm + 1:sm + 2]), drain=True)
        vS127 = dv(lambda e, sm=sm: e.tensor_single_scalar(
            smal[:, sm + 3:sm + 4], smal[:, sm + 2:sm + 3], 127.0,
            op=ALU.mult), drain=True)
        st1v[t].update(sm=sm, vS127=vS127, vLN=vLN)

    def stage1_front_b(t):
        s = t % 2
        sm = st1v[t]["sm"]
        dv(lambda e, sm=sm: e.tensor_single_scalar(
            smal[:, sm + 4:sm + 5], smal[:, sm + 1:sm + 2], 1.0 / 127.0,
            op=ALU.mult), drain=True)
        vGS = dv(lambda e, sm=sm: e.tensor_tensor(
            smal[:, sm + 5:sm + 6], smal[:, sm + 4:sm + 5],
            bcast[:, GLO["ngg"]:GLO["ngg"] + 1], op=ALU.mult),
            waits=[("cDV", fin_state["vBC"])], drain=True)
        vUS = dv(lambda e, sm=sm: e.tensor_tensor(
            smal[:, sm + 6:sm + 7], smal[:, sm + 4:sm + 5],
            bcast[:, GLO["ngu"]:GLO["ngu"] + 1], op=ALU.mult), drain=True)
        vTQ = ac(lambda e, s=s, sm=sm: e.activation(
            tq16[:, s * D:(s + 1) * D], ln2b[...], ACTF.Copy,
            bias=1536.0, scale=smal[:, sm + 3:sm + 4]),
            waits=[("cDV", st1v[t]["vS127"])], drain=True)
        vAQ = ac(lambda e, s=s: e.activation(
            aq2[...], tq16[:, s * D:(s + 1) * D], ACTF.Copy,
            bias=-1536.0, scale=1.0),
            waits=([("cPE", last_vTA[0])]
                   if last_vTA[0] is not None else [])
            + dbg_w(), drain=True)
        if t == 0:
            dbg_store("ln20", ln2b[...], [("cDV", st1v[t]["vLN"])])
            dbg_store("aq20", aq2[...], [("cAC", vAQ)])
        st1v[t].update(vAQ=vAQ, vGS=vGS, vUS=vUS)

    last_vTA = [None]

    def stage1_pe(t):
        nonlocal_tr = st1v[t]
        s = t % 2
        vTA = None
        for j in range(CH):
            vTA = pe(lambda e, j=j: e.transpose(
                pbtr[:, j * 128:(j + 1) * 128], aq2[:, j * 128:(j + 1) * 128],
                identb[...]),
                waits=([("cAC", nonlocal_tr["vAQ"])] + tr_free[:])
                if j == 0 else (),
                inc=(j == CH - 1))
        last_vTA[0] = vTA
        ev = dv(lambda e, s=s: e.tensor_copy(aq2T[:, s * D:(s + 1) * D],
                                             pbtr[...]),
                waits=[("cPE", vTA)])
        nonlocal_tr["aqT_ev"] = ev
        tr_free[:] = [("cDV", ev)]

    def main_mm(t, nxt=None):
        nonlocal vHQ_last
        s = t % 2
        sm = st1v[t]["sm"]
        aqT_ev = st1v[t]["aqT_ev"]
        vGS, vUS = st1v[t]["vGS"], st1v[t]["vUS"]
        if t == 0:
            dbg_store("sgq", s_q["g"].ap()[:, 0:128], wg_done)
            dbg_store("wgT0", wgT[:, 0:DI], [wgld[c] for c in range(CH)])
            dbg_store("wuT0", wuT[:, 0:DI], [wuld[c] for c in range(CH)])
            dbg_store("aqT0", aq2T[:, 0:D], [("cDV", aqT_ev)])
        for oc in range(CH):
            if nxt is not None:
                if oc == 2:
                    emit_ln2fin(nxt)
                elif oc == 4:
                    stage1_front_a(nxt)
                elif oc == 5:
                    stage1_front_b(nxt)
            gp, up_ = GB[oc % 2], UB[oc % 2]
            ss = oc % 2
            vG = None
            for c in range(CH):
                vG = pe(lambda e, c=c, oc=oc, gp=gp, s=s: e.matmul(
                    gp[:, 0:512], aq2T[:, s * D + c * 128:s * D + (c + 1) * 128],
                    wgT[:, c * DI + 512 * oc: c * DI + 512 * oc + 512],
                    start=(c == 0), stop=(c == CH - 1)),
                    waits=(([("cDV", aqT_ev)] + gb_free[ss])
                           if c == 0 else []) + [wgld[c]],
                    inc=(c == CH - 1))
            vU = None
            for c in range(CH):
                vU = pe(lambda e, c=c, oc=oc, up_=up_, s=s: e.matmul(
                    up_[:, 0:512], aq2T[:, s * D + c * 128:s * D + (c + 1) * 128],
                    wuT[:, c * DI + 512 * oc: c * DI + 512 * oc + 512],
                    start=(c == 0), stop=(c == CH - 1)),
                    waits=(ub_free[ss] if c == 0 else []) + [wuld[c]],
                    inc=(c == CH - 1))
            vSI = ac(lambda e, ss=ss, gp=gp, sm=sm: e.activation(
                silu_b[:, ss * 512:(ss + 1) * 512], gp[:, 0:512], ACTF.Silu,
                bias=0.0, scale=smal[:, sm + 5:sm + 6]),
                waits=[("cPE", vG), ("cDV", vGS)] + silu_free[ss], drain=True)
            gb_free[ss] = [("cAC", vSI)]
            if t == 0 and oc == 0:
                dbg_store("silu00", silu_b[:, 0:512], [("cAC", vSI)])
            vH = dv(lambda e, oc=oc, ss=ss, up_=up_, sm=sm:
                    e.scalar_tensor_tensor(
                hmid[:, oc * 512:(oc + 1) * 512], up_[:, 0:512],
                smal[:, sm + 6:sm + 7], silu_b[:, ss * 512:(ss + 1) * 512],
                op0=ALU.mult, op1=ALU.mult),
                waits=[("cPE", vU), ("cAC", vSI), ("cDV", vUS)]
                + (dbg_w() if oc == 0 else []), drain=True)
            dv(lambda e, oc=oc, s=s: e.reduce_max(
                hmx[:, s * 8 + oc:s * 8 + oc + 1],
                hmid[:, oc * 512:(oc + 1) * 512], axis=AX,
                apply_absolute_value=True), drain=True)
            ub_free[ss] = [("cDV", vH)]
            silu_free[ss] = [("cDV", vH)]
        if nxt is not None:
            stage1_pe(nxt)
        vHM_final = dv(lambda e, sm=sm, s=s: e.reduce_max(
            smal[:, sm + 7:sm + 8], hmx[:, s * 8:(s + 1) * 8], axis=AX),
            drain=True)
        dv(lambda e, sm=sm: e.tensor_single_scalar(
            smal[:, sm + 8:sm + 9], smal[:, sm + 7:sm + 8], EPS, op=ALU.max),
            drain=True)
        dv(lambda e, sm=sm: e.reciprocal(smal[:, sm + 9:sm + 10],
                                         smal[:, sm + 8:sm + 9]), drain=True)
        vS127h = dv(lambda e, sm=sm: e.tensor_single_scalar(
            smal[:, sm + 7:sm + 8], smal[:, sm + 9:sm + 10], 127.0,
            op=ALU.mult), drain=True)
        vDS_last = dv(lambda e, t=t, sm=sm: e.tensor_single_scalar(
            dscale[:, t:t + 1], smal[:, sm + 8:sm + 9], 1.0 / 127.0,
            op=ALU.mult), drain=True)
        vHQ = None
        for sl_ in range(4):
            vTQh = ac(lambda e, sl_=sl_, s=s, sm=sm: e.activation(
                tq16[:, s * D:(s + 1) * D], hmid[:, sl_ * D:(sl_ + 1) * D],
                ACTF.Copy, bias=1536.0, scale=smal[:, sm + 7:sm + 8]),
                waits=[("cDV", vS127h)] if sl_ == 0 else (), drain=True)
            vHQ = ac(lambda e, sl_=sl_, s=s: e.activation(
                hqb[:, sl_ * D:(sl_ + 1) * D], tq16[:, s * D:(s + 1) * D],
                ACTF.Copy, bias=-1536.0, scale=1.0),
                waits=(([("dS2", hq_st.get(t - 1))] + dbg_w())
                       if sl_ == 0 else ()), drain=True)
        vHQ_last = vHQ
        ffnA_last_ac[0] = vHQ
        ffnA_last_dv[0] = vDS_last
        if t == 0:
            dbg_store("hmid0", hmid[...], [("cDV", vHM_final)])
            dbg_store("hq0", hqb[...], [("cAC", vHQ)])
        hq_st[t] = dma("sync", "dS2", lambda e, t=t: e.dma_start(
            s_hq4.ap()[t * DI:(t + 1) * DI, :]
            .rearrange("(c n) k -> n c k", n=128),
            hqb[...].rearrange("p (c k) -> p c k", k=128),
            allow_slow_non_contiguous=True),
            waits=[("cAC", vHQ)])
        ffnA_pe_last[0] = vU

    ffnA_pe_last = [None]
    ffnA_last_dv = [None]
    ffnA_last_ac = [None]
    for t in range(TQ):
        emit_ctx(t)
        if t >= 1:
            main_mm(t - 1, nxt=t)
        else:
            emit_ln2fin(0)
            stage1_front_a(0)
            stage1_front_b(0)
            stage1_pe(0)
        emit_eqt_load(t + 2)
        emit_ctx_xload(t + 2)
    main_mm(TQ - 1)

    st_ffnA.close()
    st_long.close()
    wdT = sb(st_ffnB, "wdT", [128, CHI * D], BF16)
    hqT = sb(st_ffnB, "hqT", [128, 2 * DI], BF16)
    x2tB = sb(st_ffnB, "x2tB", [128, 2 * D], F32)
    ybuf = sb(st_ffnB, "ybuf", [128, 2 * D], F32)
    dsct = sb(st_ffnB, "dsct", [128, 2], F32)

    fBpe = [("cPE", ffnA_pe_last[0])]
    fBfree = fBpe + [("cAC", ffnA_last_ac[0]), ("cDV", ffnA_last_dv[0])]
    fBwd = fBfree + [("dS2", x2_st[TQ - 1])]

    hq_ld = {}
    hq_cons = {}

    def emit_hq_load(t):
        if t >= TQ:
            return
        s = t % 2
        lbsem = ("dLB", "dLB1")[s]
        hq_ld[t] = dma("scalar", lbsem, lambda e, t=t, s=s: e.dma_start(
            hqT[:, s * DI:(s + 1) * DI],
            s_hq4.ap()[t * DI:(t + 1) * DI, :], transpose=True),
            waits=[("dS2", hq_st[t]), ("cPE", hq_cons.get(t - 2))] + fBpe
            + dbg_w())

    wdld = {}

    def emit_wd_loads(c0, c1):
        for c in range(c0, c1):
            sem = ("dWL2", "dWL3")[c % 2]
            wdld[c] = (sem, dma("scalar", sem, lambda e, c=c: e.dma_start(
                wdT[:, c * D:(c + 1) * D],
                s_q["d"].ap()[:, c * 128:(c + 1) * 128],
                transpose=True), waits=wd_done + fBwd))

    emit_hq_load(0)
    emit_wd_loads(0, 4)
    emit_hq_load(1)
    emit_wd_loads(4, CHI)

    DB = [pbA, pbV]
    db_free = [fBfree, fBfree]
    y_st = {}
    x2_cons = {}
    for t in range(TQ):
        s = t % 2
        HQT = hqT[:, s * DI:(s + 1) * DI]
        X2B = x2tB[:, s * D:(s + 1) * D]
        YB = ybuf[:, s * D:(s + 1) * D]
        lbsem = ("dLB", "dLB1")[s]
        lxsem = ("dLX", "dLX1")[s]
        vLH = hq_ld[t]
        vLX = dma("sync", lxsem, lambda e, t=t, X2B=X2B: e.dma_start(
            X2B, s_x2.ap()[t * 128:(t + 1) * 128, :]),
            waits=[("cDV", x2_cons.get(t - 2)), ("dS2", x2_st[t])] + fBpe)
        vDSC = dv(lambda e, t=t, s=s: e.tensor_tensor(
            dsct[:, s:s + 1], dscale[:, t:t + 1],
            bcast[:, GLO["ngd"]:GLO["ngd"] + 1], op=ALU.mult),
            waits=[("cDV", fin_state["vBC"])] + fBpe, drain=True)
        pb = DB[t % 2]
        vTlast = None
        for jh in range(2):
            for c in range(CHI):
                vTlast = pe(lambda e, c=c, jh=jh, pb=pb, HQT=HQT, s=s:
                            e.matmul(
                    pb[:, 512 * jh:512 * jh + 512],
                    HQT[:, c * 128:(c + 1) * 128],
                    wdT[:, c * D + 512 * jh: c * D + 512 * jh + 512],
                    start=(c == 0), stop=(c == CHI - 1)),
                    waits=(([(lbsem, vLH)] + db_free[t % 2])
                           if (c == 0 and jh == 0) else [])
                    + ([wdld[c]] if jh == 0 else []),
                    inc=(jh == 1 and c == CHI - 1))
        hq_cons[t] = vTlast
        emit_hq_load(t + 2)
        vY = None
        for jh in range(2):
            vY = dv(lambda e, jh=jh, pb=pb, X2B=X2B, YB=YB, s=s:
                    e.scalar_tensor_tensor(
                        YB[:, 512 * jh:512 * jh + 512],
                        pb[:, 512 * jh:512 * jh + 512], dsct[:, s:s + 1],
                        X2B[:, 512 * jh:512 * jh + 512],
                        op0=ALU.mult, op1=ALU.add),
                    waits=[("cPE", vTlast), (lxsem, vLX), ("cDV", vDSC),
                           ("dY", y_st.get(t - 2))], drain=True)
        x2_cons[t] = vY
        db_free[t % 2] = [("cDV", vY)]
        y_st[t] = dma("sync", "dY", lambda e, t=t, YB=YB: e.dma_start(
            y_out.ap()[t * 128:(t + 1) * 128, :], YB), waits=[("cDV", vY)])

    p.op("sync", lambda e: e.nop(), waits=[("dY", y_st[TQ - 1])])

    sems = {}
    sem_stack = ExitStack()
    for name in ("cPE", "cDV", "cAC", "cGP", "dX", "dX1", "dS2", "dY", "dWP",
                 "dWP1", "dWP2", "dWS", "dWS1", "dWS2", "dWL", "dWL1", "dWL2",
                 "dWL3", "dGC", "dLA", "dLA1", "dLB", "dLB1", "dLX", "dLX1"):
        sems[name] = sem_stack.enter_context(nc.semaphore(name))
    p.materialize(nc, sems)
    sem_stack.close()
    st_ffnB.close()
    st_base.close()
    return nc, sorted(dbg_t)


def _shard_inputs(inputs):
    hs = np.asarray(inputs["L0_attn_hidden_states"], np.float32)
    B, S, _ = hs.shape
    half = S // 2
    base = {
        "wq": np.ascontiguousarray(inputs["L0_attn_Wq"], dtype=np.float32),
        "wk": np.ascontiguousarray(inputs["L0_attn_Wk"], dtype=np.float32),
        "wv": np.ascontiguousarray(inputs["L0_attn_Wv"], dtype=np.float32),
        "wfq": np.ascontiguousarray(inputs["L0_attn_Wfq"], dtype=np.float32),
        "wfk": np.ascontiguousarray(inputs["L0_attn_Wfk"], dtype=np.float32),
        "bq": np.ascontiguousarray(inputs["L0_attn_bq"], dtype=np.float32),
        "bk": np.ascontiguousarray(inputs["L0_attn_bk"], dtype=np.float32),
        "bv": np.ascontiguousarray(inputs["L0_attn_bv"], dtype=np.float32),
        "bfq": np.ascontiguousarray(inputs["L0_attn_bfq"], dtype=np.float32),
        "bfk": np.ascontiguousarray(inputs["L0_attn_bfk"], dtype=np.float32),
        "wg": np.ascontiguousarray(inputs["L0_ffn_w_gate"], dtype=np.float32),
        "wu": np.ascontiguousarray(inputs["L0_ffn_w_up"], dtype=np.float32),
        "wd": np.ascontiguousarray(inputs["L0_ffn_w_down"], dtype=np.float32),
        "id_b": np.eye(128, dtype=bfd),
        "id_f": np.eye(128, dtype=np.float32),
    }
    in_maps = []
    for c in range(2 * B):
        b, h = c // 2, c % 2
        m = dict(base)
        m["x_own"] = np.ascontiguousarray(hs[b, h * half:(h + 1) * half])
        m["x_oth"] = np.ascontiguousarray(hs[b, (1 - h) * half:(2 - h) * half])
        in_maps.append(m)
    return in_maps


def kernel(**inputs):
    nc, _ = build_nc(16, 32)
    in_maps = _shard_inputs(inputs)
    res = run_bass_kernel_spmd(nc, in_maps, core_ids=list(range(8)))
    hs = inputs["L0_attn_hidden_states"]
    B, S, E = hs.shape
    out = np.empty((B, S, E), np.float32)
    half = S // 2
    for c in range(2 * B):
        b, h = c // 2, c % 2
        out[b, h * half:(h + 1) * half] = res.results[c]["y"]
    return out


# version 38
# speedup vs baseline: 1.0035x; 1.0035x over previous
import numpy as np
import ml_dtypes
import concourse.bass as bass
import concourse.mybir as mybir
from concourse.bass_utils import run_bass_kernel_spmd
from contextlib import ExitStack

F32 = mybir.dt.float32
F16 = mybir.dt.float16
BF16 = mybir.dt.bfloat16
AX = mybir.AxisListType.X
ALU = mybir.AluOpType
ACTF = mybir.ActivationFunctionType
bfd = ml_dtypes.bfloat16

D = 1024
CH = D // 128
DI = 4096
CHI = DI // 128
NPAIR = 8
PSTR = 130
EPS = 1e-5

ENGS = ("tensor", "vector", "scalar", "sync", "gpsimd")


class Plan:
    def __init__(self):
        self.ops = {e: [] for e in ENGS}
        self.cnt = {}
        self.last_wait = {e: {} for e in ENGS}

    def op(self, eng, fn, waits=(), inc=None, inc_by=1, drain=False):
        w = []
        for sem, val in waits:
            if val is None:
                continue
            lw = self.last_wait[eng].get(sem, 0)
            if val > lw:
                w.append((sem, val))
                self.last_wait[eng][sem] = val
        val = None
        if inc is not None:
            val = self.cnt.get(inc, 0) + inc_by
            self.cnt[inc] = val
        self.ops[eng].append((fn, tuple(w), inc, inc_by, drain))
        return val

    def materialize(self, nc, sems):
        for eng, stream in self.ops.items():
            for fn, waits, inc, inc_by, drain in stream:
                for sem, val in waits:
                    assert val <= self.cnt.get(sem, 0), (
                        f"impossible wait {sem}>={val} on {eng} "
                        f"(total {self.cnt.get(sem, 0)})")
        with nc.Block() as blk:
            for eng in ENGS:
                stream = self.ops[eng]
                if not stream:
                    continue

                def mk(stream):
                    def body(e):
                        for fn, waits, inc, inc_by, drain in stream:
                            for sem, val in waits:
                                e.wait_ge(sems[sem], val)
                            if drain:
                                e.drain()
                            inst = fn(e)
                            if inc is not None:
                                inst.then_inc(sems[inc], inc_by)
                    return body

                getattr(blk, eng)(mk(stream))


def build_nc(TQ=16, TKV=32, dbg=()):
    assert TQ % 4 == 0 or TQ == 2, "stats batching assumes 4-tile groups"
    nc = bass.Bass("TRN2", target_bir_lowering=False, debug=False)
    NQ, NKV = TQ * 128, TKV * 128
    p = Plan()

    x_own = nc.dram_tensor("x_own", [NQ, D], F32, kind="ExternalInput")
    x_oth = nc.dram_tensor("x_oth", [NKV - NQ, D], F32, kind="ExternalInput")
    wf = {n: nc.dram_tensor(n, [D, D], F32, kind="ExternalInput")
          for n in ("wq", "wk", "wv", "wfq", "wfk")}
    bin_ = {n: nc.dram_tensor(n, [D], F32, kind="ExternalInput")
            for n in ("bq", "bk", "bv", "bfq", "bfk")}
    wg_f = nc.dram_tensor("wg", [DI, D], F32, kind="ExternalInput")
    wu_f = nc.dram_tensor("wu", [DI, D], F32, kind="ExternalInput")
    wd_f = nc.dram_tensor("wd", [D, DI], F32, kind="ExternalInput")
    id_b = nc.dram_tensor("id_b", [128, 128], BF16, kind="ExternalInput")
    id_f = nc.dram_tensor("id_f", [128, 128], F32, kind="ExternalInput")
    y_out = nc.dram_tensor("y", [NQ, D], F32, kind="ExternalOutput")
    s_nat = {n: nc.dram_tensor("s_" + n, [D, D], BF16, kind="Internal")
             for n in ("wfq", "wfk", "wv")}
    s_q = {"g": nc.dram_tensor("s_gq", [DI, D], BF16, kind="Internal"),
           "u": nc.dram_tensor("s_uq", [DI, D], BF16, kind="Internal"),
           "d": nc.dram_tensor("s_dq", [D, DI], BF16, kind="Internal")}
    s_x2 = nc.dram_tensor("s_x2", [NQ, D], F32, kind="Internal")
    s_hq4 = nc.dram_tensor("s_hq4", [NQ * (DI // 128), 128], BF16,
                           kind="Internal")
    s_eqT = nc.dram_tensor("s_eqT", [NQ, D], BF16, kind="Internal")

    dbg_t = {}

    def dbgT(name, shape, dt=F32):
        if name in dbg:
            dbg_t[name] = nc.dram_tensor("dbg_" + name, shape, dt,
                                         kind="ExternalOutput")

    dbgT("rows", [1, 6 * D], BF16); dbgT("rm_row", [1, NKV], BF16)
    dbgT("wcqT", [128, CH * D], BF16); dbgT("stat", [128, 6 * TKV])
    dbgT("ek0", [128, D], BF16); dbgT("zk0", [128, 1])
    dbgT("vst0", [128, NPAIR * PSTR], BF16)
    dbgT("kvs", [128, NPAIR * PSTR], BF16); dbgT("eqT0", [128, D], BF16)
    dbgT("x20", [128, D]); dbgT("bcast", [128, 9])
    dbgT("ln20", [128, D]); dbgT("aq20", [128, D], BF16)
    dbgT("hmid0", [128, DI]); dbgT("hq0", [128, DI], BF16)
    dbgT("wgT0", [128, DI], BF16); dbgT("wuT0", [128, DI], BF16)
    dbgT("aqT0", [128, D], BF16); dbgT("silu00", [128, 512])
    dbgT("sgq", [DI, 128], BF16)

    def dbg_store(name, ap, waits):
        if name in dbg_t:
            t = dbg_t[name]
            prev = p.cnt.get("dY", 0)
            p.op("sync", lambda e, t=t, ap=ap: e.dma_start(t.ap(), ap),
                 list(waits) + [("dY", prev if prev else None)], "dY", 16)

    def dbg_w():
        return [("dY", p.cnt["dY"])] if (dbg and p.cnt.get("dY")) else []

    def pe(fn, waits=(), inc=True, drain=False):
        return p.op("tensor", fn, waits, "cPE" if inc else None, 1, drain)

    def dv(fn, waits=(), inc=True, drain=False):
        return p.op("vector", fn, waits, "cDV" if inc else None, 1, drain)

    def ac(fn, waits=(), inc=True, drain=False):
        return p.op("scalar", fn, waits, "cAC" if inc else None, 1, drain)

    def dma(eng, sem, fn, waits=()):
        prev = p.cnt.get(sem, 0)
        return p.op(eng, fn, list(waits) + [(sem, prev if prev > 0 else None)],
                    sem, 16)

    ALLSEMS = ("cPE", "cDV", "cAC", "cGP", "dX", "dX1", "dS2", "dY", "dWP",
               "dWP1", "dWP2", "dWS", "dWS1", "dWS2", "dWL", "dWL1", "dWL2",
               "dWL3", "dGC", "dLA", "dLA1", "dLB", "dLB1", "dLX", "dLX1")

    def fence():
        return [(s, p.cnt[s]) for s in ALLSEMS if p.cnt.get(s)]

    st_base = ExitStack()
    st_long = ExitStack()
    st_attn = ExitStack()
    st_preq = ExitStack()
    st_prek = ExitStack()
    st_sweep = ExitStack()
    st_ffnA = ExitStack()
    st_ffnB = ExitStack()

    def sb(stk, name, shape, dt):
        return stk.enter_context(nc.sbuf_tensor(name, shape, dt))

    glob = sb(st_base, "glob", [1, 16], F32)
    GLO = dict(t0=0, t1=1, t2=2, thg=3, thu=4, thd=5, nthg=6, nthu=7, nthd=8,
               ngg=9, ngu=10, ngd=11)
    bcast = sb(st_base, "bcast", [128, 16], F32)
    m2r2 = sb(st_base, "m2r2", [128, 6 * TQ], F32)
    M2 = {n: i * TQ for i, n in enumerate(("sum", "ssq", "m", "r", "sp", "sp2"))}
    dscale = sb(st_base, "dscale", [128, TQ], F32)
    identb = sb(st_base, "identb", [128, 128], BF16)
    identf = sb(st_base, "identf", [128, 128], F32)
    ptmp = sb(st_base, "ptmp", [128, 8], F32)
    onesf_row = sb(st_base, "onesf_row", [1, 128], F32)

    xt_buf = sb(st_long, "xt_buf", [128, 2 * D], F32)
    kvs = sb(st_long, "kvs", [128, NPAIR * PSTR], BF16)
    recden = sb(st_long, "recden", [128, 16], F32)
    tmp_ctx = sb(st_long, "tmp_ctx", [128, D], F32)
    eqt_ld = sb(st_long, "eqt_ld", [128, 2 * D], BF16)

    wcqT = sb(st_attn, "wcqT", [128, CH * D], BF16)
    wckT = sb(st_attn, "wckT", [128, CH * D], BF16)
    wvT = sb(st_attn, "wvT", [128, CH * D], BF16)
    rows = sb(st_attn, "rows", [1, 6 * D], BF16)
    ROW = {n: i * D for i, n in enumerate(("wcqs", "wcks", "wvs", "bfq", "bfk", "bv"))}
    lhsT2 = sb(st_attn, "lhsT2", [2, NKV], BF16)
    rows2 = sb(st_attn, "rows2", [2, 3 * D], BF16)
    ones_row = sb(st_attn, "ones_row", [1, 128], BF16)
    ones_col = sb(st_attn, "ones_col", [128, 1], BF16)
    stat = sb(st_attn, "stat", [128, 6 * TKV], F32)
    STT = {n: i * TKV for i, n in enumerate(("sum", "ssq", "m", "r", "nrm", "sp"))}
    negrm_bf = sb(st_attn, "negrm_bf", [128, TKV], BF16)
    abscols = sb(st_attn, "abscols", [128, 3 * CHI], F32)
    bq_col = sb(st_attn, "bq_col", [128, CH], BF16)
    bk_col = sb(st_attn, "bk_col", [128, CH], BF16)
    sq_scr = sb(st_attn, "sq_scr", [128, D], F32)

    pbA = st_base.enter_context(nc.psum_tensor("pbA", [128, D], F32))
    pbV = st_base.enter_context(nc.psum_tensor("pbV", [128, D], F32))
    pbK = [st_base.enter_context(nc.psum_tensor(f"pbK{i}", [128, 512], F32))
           for i in range(3)]
    pbtr = st_base.enter_context(nc.psum_tensor("pbtr", [128, D], BF16))

    vIb = dma("sync", "dX", lambda e: e.dma_start(identb[...], id_b.ap()))
    vIf = dma("sync", "dX", lambda e: e.dma_start(identf[...], id_f.ap()))

    wq_nat = sb(st_preq, "wq_nat", [128, CH * D], BF16)
    wfqT = sb(st_preq, "wfqT", [128, CH * D], BF16)
    vGq = dma("gpsimd", "dGC", lambda e: e.dma_start(
        wq_nat[...].rearrange("p (c i) -> p c i", i=D),
        wf["wq"].ap().rearrange("(c p) i -> p c i", p=128)))
    vGn = {}
    for n in ("wfq", "wfk", "wv"):
        vGn[n] = dma("gpsimd", "dGC",
                     lambda e, n=n: e.dma_start(s_nat[n].ap(), wf[n].ap()))
    vGbq = dma("gpsimd", "dGC", lambda e: e.dma_start(
        bq_col[...], bin_["bq"].ap().rearrange("(c p) -> p c", p=128),
        allow_slow_non_contiguous=True))
    vGbk = dma("gpsimd", "dGC", lambda e: e.dma_start(
        bk_col[...], bin_["bk"].ap().rearrange("(c p) -> p c", p=128),
        allow_slow_non_contiguous=True))
    for n in ("bfq", "bfk", "bv"):
        dma("gpsimd", "dGC", lambda e, n=n: e.dma_start(
            rows[0:1, ROW[n]:ROW[n] + D],
            bin_[n].ap().rearrange("(a d) -> a d", a=1)))
    p.op("gpsimd", lambda e: e.memset(ones_row[...], 1.0), (), "cGP", 1)
    p.op("gpsimd", lambda e: e.memset(onesf_row[...], 1.0), (), "cGP", 1)
    p.op("gpsimd", lambda e: e.memset(lhsT2[...], 1.0), (), "cGP", 1)
    vGC_end = p.op("gpsimd", lambda e: e.memset(ones_col[...], 1.0), (), "cGP", 1)

    vWT = {}
    for n, tgt in (("wfq", wfqT), ("wv", wvT)):
        last = None
        for c in range(CH):
            last = dma("scalar", "dWL", lambda e, n=n, tgt=tgt, c=c: e.dma_start(
                tgt[:, c * D:(c + 1) * D],
                s_nat[n].ap()[:, c * 128:(c + 1) * 128], transpose=True),
                waits=[("dGC", vGn[n])])
        vWT[n] = last

    def emit_combine(w_nat, wfT, out, vnat, vwt, prev_ev):
        ev = prev_ev
        for mi in range(CH):
            vlast = None
            for h in range(2):
                for cj in range(CH):
                    vlast = pe(lambda e, mi=mi, h=h, cj=cj, w_nat=w_nat, wfT=wfT:
                               e.matmul(
                        pbA[:, 512 * h:512 * h + 512],
                        w_nat[:, cj * D + mi * 128: cj * D + mi * 128 + 128],
                        wfT[:, cj * D + 512 * h: cj * D + 512 * h + 512],
                        start=(cj == 0), stop=(cj == CH - 1)),
                        waits=[("dGC", vnat), ("dWL", vwt), ("cDV", ev)],
                        inc=(h == 1 and cj == CH - 1))
            ev = dv(lambda e, mi=mi, out=out: e.tensor_copy(
                out[:, mi * D:(mi + 1) * D], pbA[...]), waits=[("cPE", vlast)])
        return ev

    row_ev = [None]

    def emit_sum_row(srcT, dst_off, xw):
        vlast = None
        for h in range(2):
            for c in range(CH):
                vlast = pe(lambda e, h=h, c=c, srcT=srcT: e.matmul(
                    pbV[0:1, 512 * h:512 * h + 512], ones_col[...],
                    srcT[:, c * D + 512 * h: c * D + 512 * h + 512],
                    start=(c == 0), stop=(c == CH - 1)),
                    waits=xw + [("cGP", vGC_end), ("cDV", row_ev[-1])],
                    inc=(h == 1 and c == CH - 1))
        row_ev.append(dv(lambda e, dst_off=dst_off: e.tensor_copy(
            rows[0:1, dst_off:dst_off + D], pbV[0:1, :]), waits=[("cPE", vlast)]))

    def emit_bias_row(bcol, wfT, brow_off, dst_off, xw):
        vlast = None
        for h in range(2):
            for c in range(CH):
                vlast = pe(lambda e, h=h, c=c, bcol=bcol, wfT=wfT: e.matmul(
                    pbV[0:1, 512 * h:512 * h + 512], bcol[:, c:c + 1],
                    wfT[:, c * D + 512 * h: c * D + 512 * h + 512],
                    start=(c == 0), stop=False),
                    waits=xw + [("cGP", vGC_end), ("cDV", row_ev[-1])], inc=False)
            vlast = pe(lambda e, h=h, brow_off=brow_off: e.matmul(
                pbV[0:1, 512 * h:512 * h + 512], ones_row[0:1, 0:1],
                rows[0:1, brow_off + 512 * h: brow_off + 512 * h + 512],
                start=False, stop=True), inc=(h == 1))
        row_ev.append(dv(lambda e, dst_off=dst_off: e.tensor_copy(
            rows[0:1, dst_off:dst_off + D], pbV[0:1, :]), waits=[("cPE", vlast)]))

    vCq = emit_combine(wq_nat, wfqT, wcqT, vGq, vWT["wfq"], None)
    emit_sum_row(wcqT, ROW["wcqs"], [("cDV", vCq)])
    emit_bias_row(bq_col, wfqT, ROW["bfq"], ROW["bfq"], [])
    vQrows = row_ev[-1]
    st_preq.close()
    wk_nat = sb(st_prek, "wk_nat", [128, CH * D], BF16)
    wfkT = sb(st_prek, "wfkT", [128, CH * D], BF16)
    vGk = dma("gpsimd", "dGC", lambda e: e.dma_start(
        wk_nat[...].rearrange("p (c i) -> p c i", i=D),
        wf["wk"].ap().rearrange("(c p) i -> p c i", p=128)),
        waits=[("cDV", vQrows)])
    last = None
    for c in range(CH):
        last = dma("scalar", "dWL", lambda e, c=c: e.dma_start(
            wfkT[:, c * D:(c + 1) * D],
            s_nat["wfk"].ap()[:, c * 128:(c + 1) * 128], transpose=True),
            waits=[("dGC", vGn["wfk"]), ("cDV", vQrows)])
    vWT["wfk"] = last
    vCk = emit_combine(wk_nat, wfkT, wckT, vGk, vWT["wfk"], row_ev[-1])
    emit_sum_row(wckT, ROW["wcks"], [("cDV", vCk)])
    emit_sum_row(wvT, ROW["wvs"], [("dWL", vWT["wv"])])
    emit_bias_row(bk_col, wfkT, ROW["bfk"], ROW["bfk"], [])
    dma("sync", "dX", lambda e: e.dma_start(
        rows2[0:1, :], rows[0:1, 0:3 * D]), waits=[("cDV", row_ev[-1])])
    vRows2 = dma("sync", "dX", lambda e: e.dma_start(
        rows2[1:2, :], rows[0:1, 3 * D:6 * D]), waits=[("cDV", row_ev[-1])])
    vRows = row_ev[-1]
    st_prek.close()
    dbg_store("rows", rows[...], [("cDV", vRows)])
    dbg_store("wcqT", wcqT[...], [("cDV", vCq)])

    def x_tile_ap(t):
        if t < TQ:
            return x_own.ap()[t * 128:(t + 1) * 128, :]
        return x_oth.ap()[(t - TQ) * 128:(t - TQ + 1) * 128, :]

    evtr_state = [None]
    stats_marks = {}

    def emit_stats(t, vL, xsem, xap):
        c_ = STT
        vSum = dv(lambda e, t=t, xap=xap: e.reduce_sum(
            stat[:, c_["sum"] + t:c_["sum"] + t + 1], xap, axis=AX),
            waits=[(xsem, vL)])
        vSq = dv(lambda e, t=t, xap=xap: e.scalar_tensor_tensor(
            sq_scr[...], xap, 0.0, xap, op0=ALU.add, op1=ALU.mult,
            accum_out=stat[:, c_["ssq"] + t:c_["ssq"] + t + 1]), drain=True)
        dv(lambda e, t=t: e.tensor_single_scalar(
            stat[:, c_["m"] + t:c_["m"] + t + 1],
            stat[:, c_["sum"] + t:c_["sum"] + t + 1], 1.0 / D, op=ALU.mult),
            drain=True)
        dv(lambda e, t=t: e.tensor_tensor(
            stat[:, c_["sp"] + t:c_["sp"] + t + 1],
            stat[:, c_["m"] + t:c_["m"] + t + 1],
            stat[:, c_["m"] + t:c_["m"] + t + 1], op=ALU.mult), drain=True)
        dv(lambda e, t=t: e.scalar_tensor_tensor(
            stat[:, c_["sum"] + t:c_["sum"] + t + 1],
            stat[:, c_["ssq"] + t:c_["ssq"] + t + 1], 1.0 / D,
            stat[:, c_["sp"] + t:c_["sp"] + t + 1],
            op0=ALU.mult, op1=ALU.subtract), drain=True)
        vVe = dv(lambda e, t=t: e.tensor_single_scalar(
            stat[:, c_["ssq"] + t:c_["ssq"] + t + 1],
            stat[:, c_["sum"] + t:c_["sum"] + t + 1], EPS, op=ALU.add),
            drain=True)
        vLn = ac(lambda e, t=t: e.activation(
            stat[:, c_["sp"] + t:c_["sp"] + t + 1],
            stat[:, c_["ssq"] + t:c_["ssq"] + t + 1], ACTF.Ln,
            bias=0.0, scale=1.0), waits=[("cDV", vVe)], drain=True)
        vR = ac(lambda e, t=t: e.activation(
            stat[:, c_["r"] + t:c_["r"] + t + 1],
            stat[:, c_["sp"] + t:c_["sp"] + t + 1], ACTF.Exp,
            bias=0.0, scale=-0.5), drain=True)
        vNR = dv(lambda e, t=t: e.scalar_tensor_tensor(
            stat[:, c_["nrm"] + t:c_["nrm"] + t + 1],
            stat[:, c_["r"] + t:c_["r"] + t + 1], -1.0,
            stat[:, c_["m"] + t:c_["m"] + t + 1],
            op0=ALU.mult, op1=ALU.mult), waits=[("cAC", vR)], drain=True)
        vNB = dv(lambda e, t=t: e.tensor_copy(
            negrm_bf[:, t:t + 1], stat[:, c_["nrm"] + t:c_["nrm"] + t + 1]),
            drain=True)
        vTr = pe(lambda e, t=t: e.transpose(
            pbtr[0:1, 0:128], negrm_bf[:, t:t + 1], identb[...]),
            waits=[("cDV", vNB), ("dX", vIb), ("cDV", evtr_state[0])])
        evtr_state[0] = dv(lambda e, t=t: e.tensor_copy(
            lhsT2[0:1, t * 128:(t + 1) * 128], pbtr[0:1, 0:128]),
            waits=[("cPE", vTr)])
        stats_marks[t] = (vSq, vR)

    pchunk = sb(st_sweep, "pchunk", [128, 8 * D], F32)
    pp_buf = sb(st_sweep, "pp", [128, 4 * D], F32)
    pwq = sb(st_sweep, "pwq", [128, 8 * D], BF16)
    ppb = sb(st_sweep, "ppb", [128, 8 * D], BF16)
    UPW = CHI // 4
    prep_chunks = []
    for wi, wdram in enumerate((wg_f, wu_f)):
        for u in range(UPW):
            prep_chunks.append((
                wi, wdram,
                lambda a, u=u: a[u * 512:(u + 1) * 512, :]
                .rearrange("(c p) i -> p c i", p=128),
                lambda b, s: b[:, s * 4 * D:(s + 1) * 4 * D]
                .rearrange("p (c i) -> p c i", i=D)))
    for u in range(UPW):
        prep_chunks.append((
            2, wd_f,
            lambda a, u=u: a[u * 128:(u + 1) * 128, :],
            lambda b, s: b[:, s * 4 * D:(s + 1) * 4 * D]))
    prep1_ac = {}
    prep1_ld = {}

    def emit_prep1_load(k):
        wi, wdram, sl_, dst_ = prep_chunks[k]
        s = k % 2
        sem = ("dWP", "dWP1")[s]
        prep1_ld[k] = dma("scalar", sem,
                          lambda e, wdram=wdram, sl_=sl_, dst_=dst_, s=s:
                          e.dma_start(dst_(pchunk, s), sl_(wdram.ap())),
                          waits=[("cAC", prep1_ac.get(k - 2)), ("cDV", vRows)])

    def emit_prep1_abs(k):
        s = k % 2
        sem = ("dWP", "dWP1")[s]
        prep1_ac[k] = ac(lambda e, s=s, k=k: e.activation(
            pp_buf[...], pchunk[:, s * 4 * D:(s + 1) * 4 * D], ACTF.Abs,
            bias=0.0, scale=1.0, accum_out=abscols[:, k:k + 1]),
            waits=[(sem, prep1_ld[k])], drain=True)

    fin_done = [False]
    prep2_i = [0]
    fin_state = {}

    def emit_finalize():
        fin_done[0] = True
        vAC_prep = prep1_ac[len(prep_chunks) - 1]
        pbtr32 = pbtr[...].bitcast(F32)
        tr_prev = [evtr_state[0]]
        vGlo = None
        tr_ev = None
        for wi, (nel, thn, nthn, ngn) in enumerate((
                (DI * D, "thg", "nthg", "ngg"), (DI * D, "thu", "nthu", "ngu"),
                (D * DI, "thd", "nthd", "ngd"))):
            vA = dv(lambda e, wi=wi: e.reduce_sum(
                ptmp[:, wi:wi + 1], abscols[:, wi * UPW:(wi + 1) * UPW],
                axis=AX), waits=[("cAC", vAC_prep)], drain=True)
            vT = pe(lambda e, wi=wi, pbtr32=pbtr32: e.transpose(
                pbtr32[0:1, wi * 128:wi * 128 + 128], ptmp[:, wi:wi + 1],
                identf[...]),
                waits=[("cDV", vA), ("dX", vIf), ("cDV", tr_ev),
                       ("cDV", tr_prev[0])])
            vS = dv(lambda e, wi=wi, pbtr32=pbtr32: e.reduce_sum(
                glob[0:1, wi:wi + 1], pbtr32[0:1, wi * 128:wi * 128 + 128],
                axis=AX), waits=[("cPE", vT)], drain=True)
            tr_ev = vS
            dv(lambda e, wi=wi, nel=nel: e.tensor_single_scalar(
                glob[0:1, wi:wi + 1], glob[0:1, wi:wi + 1], 1.0 / nel,
                op=ALU.mult), drain=True)
            dv(lambda e, wi=wi: e.tensor_single_scalar(
                glob[0:1, wi:wi + 1], glob[0:1, wi:wi + 1], EPS, op=ALU.max),
                drain=True)
            dv(lambda e, wi=wi, thn=thn: e.tensor_single_scalar(
                glob[0:1, GLO[thn]:GLO[thn] + 1], glob[0:1, wi:wi + 1], 0.5,
                op=ALU.mult), drain=True)
            dv(lambda e, thn=thn, nthn=nthn: e.tensor_single_scalar(
                glob[0:1, GLO[nthn]:GLO[nthn] + 1],
                glob[0:1, GLO[thn]:GLO[thn] + 1], -1.0, op=ALU.mult),
                drain=True)
            vGlo = dv(lambda e, wi=wi, ngn=ngn: e.tensor_single_scalar(
                glob[0:1, GLO[ngn]:GLO[ngn] + 1], glob[0:1, wi:wi + 1], 0.5,
                op=ALU.mult), drain=True)
        vB_ = None
        for i in range(9):
            vB_ = pe(lambda e, i=i, pbtr32=pbtr32: e.matmul(
                pbtr32[:, 384 + i:385 + i], onesf_row[...],
                glob[0:1, 3 + i:4 + i], start=True, stop=True),
                waits=([("cDV", vGlo), ("cGP", vGC_end)]) if i == 0 else (),
                inc=(i == 8))
        fin_state["vB"] = vB_
        vBC_ = dv(lambda e, pbtr32=pbtr32: e.tensor_copy(
            bcast[:, 3:12], pbtr32[:, 384:393]), waits=[("cPE", vB_)])
        fin_state["vBC"] = vBC_
        evtr_state[0] = vBC_
        dbg_store("bcast", bcast[:, 3:12], [("cDV", vBC_)])

    prep2_dv = {}
    prep2_st = {}
    prep2_ac = {}
    prep2_ld = {}
    p2l, p2s, p2a = [0], [0], [0]
    p1l, p1a = [0], [0]
    NP2 = 3 * UPW


    def emit_prep2_load(k):
        wi, wdram, sl_, dst_ = prep_chunks[k]
        s = k % 2
        lsem = ("dWP", "dWP1")[s]
        prep2_ld[k] = dma("scalar", lsem,
                          lambda e, wdram=wdram, sl_=sl_, dst_=dst_, s=s:
                          e.dma_start(dst_(pchunk, s), sl_(wdram.ap())),
                          waits=[("cAC", prep2_ac.get(k - 2)),
                                 ("cAC", prep1_ac[len(prep_chunks) - 2 + s])])

    def emit_prep2_signs(k):
        wi, wdram, sl_, dst_ = prep_chunks[k]
        thn = ("thg", "thu", "thd")[wi]
        nthn = ("nthg", "nthu", "nthd")[wi]
        s = k % 2
        lsem = ("dWP", "dWP1")[s]
        ssem = ("dWS", "dWS1")[s]
        SGP = ppb[:, s * 4 * D:(s + 1) * 4 * D]
        PWQ = pwq[:, s * 4 * D:(s + 1) * 4 * D]
        ac(lambda e, s=s, thn=thn, SGP=SGP: e.activation(
            SGP, pchunk[:, s * 4 * D:(s + 1) * 4 * D], ACTF.Sign,
            bias=bcast[:, GLO[thn]:GLO[thn] + 1], scale=1.0),
            waits=[(lsem, prep2_ld[k]), ("cDV", fin_state["vBC"]),
                   ("cGP", prep2_dv.get(k - 2))], drain=True)
        vSM = ac(lambda e, s=s, nthn=nthn, PWQ=PWQ: e.activation(
            PWQ, pchunk[:, s * 4 * D:(s + 1) * 4 * D], ACTF.Sign,
            bias=bcast[:, GLO[nthn]:GLO[nthn] + 1], scale=1.0),
            waits=[(ssem, prep2_st.get(k - 2))], drain=True)
        prep2_ac[k] = vSM

    def emit_prep2_add(k):
        wi, wdram, sl_, dst_ = prep_chunks[k]
        sdram = (s_q["g"], s_q["u"], s_q["d"])[wi]
        s = k % 2
        ssem = ("dWS", "dWS1")[s]
        SGP = ppb[:, s * 4 * D:(s + 1) * 4 * D]
        PWQ = pwq[:, s * 4 * D:(s + 1) * 4 * D]
        vW = p.op("gpsimd", lambda e, SGP=SGP, PWQ=PWQ: e.tensor_tensor(
            PWQ, SGP, PWQ, op=ALU.add),
            [("cAC", prep2_ac[k])], "cGP", 1)
        prep2_dv[k] = vW
        prep2_st[k] = dma("sync", ssem,
                          lambda e, sdram=sdram, sl_=sl_, dst_=dst_, s=s:
                          e.dma_start(sl_(sdram.ap()), dst_(pwq, s)),
                          waits=[("cGP", vW)])

    p2s_hist = [0]

    def prep2_drain_adds(upto=None):
        tgt = p2s[0] if upto is None else upto
        while p2a[0] < tgt:
            emit_prep2_add(p2a[0])
            p2a[0] += 1

    xr_buf = sb(st_sweep, "xr_buf", [128, 2 * D], BF16)
    xrT = sb(st_sweep, "xrT", [128, 2 * D], BF16)
    ek_buf = sb(st_sweep, "ek_buf", [128, 2 * D], BF16)
    vstage = sb(st_sweep, "vstage", [128, 2 * NPAIR * PSTR], BF16)
    eq_buf = sb(st_sweep, "eq_buf", [128, D], BF16)
    nmax = sb(st_sweep, "nmax", [128, 4], F32)
    zkb = sb(st_sweep, "zkb", [128, 2], F32)
    rzkb = sb(st_sweep, "rzkb", [128, 2], F32)
    eqt_st = sb(st_sweep, "eqt_st", [128, 2 * D], BF16)

    vZst = p.op("gpsimd", lambda e: e.memset(vstage[...], 0.0),
                [("cDV", vRows)], "cGP", 1)

    def emit_proj(pb, XRT, wT, sum_off, bias_off, t, xw):
        vlast = None
        for h in range(2):
            for c in range(CH):
                vlast = pe(lambda e, h=h, c=c, pb=pb, XRT=XRT, wT=wT: e.matmul(
                    pb[:, 512 * h:512 * h + 512],
                    XRT[:, c * 128:(c + 1) * 128],
                    wT[:, c * D + 512 * h: c * D + 512 * h + 512],
                    start=(c == 0), stop=False),
                    waits=xw if (h == 0 and c == 0) else (), inc=False)
            vlast = pe(lambda e, h=h, t=t, sum_off=sum_off, pb=pb: e.matmul(
                pb[:, 512 * h:512 * h + 512],
                lhsT2[0:2, t * 128:(t + 1) * 128],
                rows2[0:2, sum_off + 512 * h: sum_off + 512 * h + 512],
                start=False, stop=True),
                waits=[("dX", vRows2)], inc=(h == 1))
        return vlast

    xr_dv = {}
    pbA_free = []
    pbV_free = []
    kv_last = None
    prep_i = 0
    vEK_last = None
    eqt_store = {}

    x_loads = {}

    def emit_xload(t):
        s = t % 2
        xw_extra = ([("cDV", xr_dv[t - 2]), ("cAC", stats_marks[t - 2][1])]
                    if t >= 2 else [])
        xw_sem = ("dX", "dX1")[s]
        vL = dma("sync", xw_sem, lambda e, t=t, s=s: e.dma_start(
            xt_buf[:, s * D:(s + 1) * D], x_tile_ap(t)),
            waits=xw_extra + dbg_w())
        x_loads[t] = (vL, xw_sem)
        return vL, xw_sem

    vL0, sem0 = emit_xload(0)
    emit_stats(0, vL0, sem0, xt_buf[:, 0:D])

    def emit_next_tile_stats(t):
        if t + 1 < TKV:
            vLn_, semn = emit_xload(t + 1)
            emit_stats(t + 1, vLn_, semn,
                       xt_buf[:, ((t + 1) % 2) * D:(((t + 1) % 2) + 1) * D])

    for t in range(TKV):
        s = t % 2
        vL, xw_sem = x_loads[t]
        xap = xt_buf[:, s * D:(s + 1) * D]
        XR = xr_buf[:, s * D:(s + 1) * D]
        XRT = xrT[:, s * D:(s + 1) * D]
        EK = ek_buf[:, s * D:(s + 1) * D]
        VST = vstage[:, s * NPAIR * PSTR:(s + 1) * NPAIR * PSTR]

        xr_dv[t] = dv(lambda e, xap=xap, t=t, XR=XR: e.tensor_single_scalar(
            XR, xap, stat[:, STT["r"] + t:STT["r"] + t + 1], op=ALU.mult),
            waits=[(xw_sem, vL), ("cAC", stats_marks[t][1])])

        vT = None
        for j in range(CH):
            vT = pe(lambda e, j=j, XR=XR: e.transpose(
                pbtr[:, j * 128:(j + 1) * 128], XR[:, j * 128:(j + 1) * 128],
                identb[...]),
                waits=[("cDV", xr_dv[t]), ("cDV", evtr_state[0])]
                if j == 0 else (),
                inc=(j == CH - 1))
        evtr_state[0] = dv(lambda e, XRT=XRT: e.tensor_copy(XRT, pbtr[...]),
                           waits=[("cPE", vT)])
        vXRT = evtr_state[0]

        vK = emit_proj(pbA, XRT, wckT, ROW["wcks"], ROW["bfk"], t,
                       [("cDV", vXRT)] + pbA_free)
        vNM = dv(lambda e, s=s: e.reduce_max(
            nmax[:, s:s + 1], pbA[...], axis=AX, negate=True),
            waits=[("cPE", vK)])
        vEK = ac(lambda e, s=s, EK=EK: e.activation(
            EK, pbA[...], ACTF.Exp, bias=nmax[:, s:s + 1], scale=1.0,
            accum_out=zkb[:, s:s + 1]), waits=[("cPE", vK), ("cDV", vNM)])
        pbA_free = [("cAC", vEK), ("cDV", vNM)]
        vEK_last = vEK
        vRZ = dv(lambda e, s=s: e.reciprocal(rzkb[:, s:s + 1], zkb[:, s:s + 1]),
                 waits=[("cAC", vEK)])
        vV = emit_proj(pbV, XRT, wvT, ROW["wvs"], ROW["bv"], t, pbV_free)
        vS1 = dv(lambda e, s=s, VST=VST: e.tensor_scalar(
            VST.rearrange("p (c k) -> p c k", k=PSTR)[:, :, 0:128],
            pbV[...].rearrange("p (c k) -> p c k", k=128),
            rzkb[:, s:s + 1], None, op0=ALU.mult),
            waits=[("cPE", vV), ("cGP", vZst)], drain=True)
        vS2 = dv(lambda e, s=s, VST=VST: e.tensor_copy(
            VST.rearrange("p (c k) -> p c k", k=PSTR)[:, :, 128:129],
            rzkb[:, s:s + 1].rearrange("p (c k) -> p c k", c=1)
            .broadcast_to([128, NPAIR, 1])))
        pbV_free = [("cDV", vS1)]
        if t == 0:
            dbg_store("ek0", EK, [("cAC", vEK)])
            dbg_store("zk0", zkb[:, 0:1], [("cAC", vEK)])
            dbg_store("vst0", VST, [("cDV", vS2)])

        vQ = None
        if t < TQ:
            vQ = emit_proj(pbA, XRT, wcqT, ROW["wcqs"], ROW["bfq"], t, pbA_free)
        for pr in range(NPAIR):
            b, slot = pr // 3, PSTR * (pr % 3)
            kv_last = pe(lambda e, pr=pr, b=b, slot=slot, EK=EK, VST=VST,
                         st_=(t == 0 and pr % 3 == 0),
                         sp_=(t == TKV - 1 and (pr % 3 == 2 or pr == NPAIR - 1)):
                         e.matmul(
                pbK[b][:, slot:slot + 129], EK[:, pr * 128:(pr + 1) * 128],
                VST[:, pr * PSTR:pr * PSTR + 129],
                start=st_, stop=sp_),
                waits=[("cAC", vEK), ("cDV", vS2)] if pr == 0 else (),
                inc=(pr == NPAIR - 1))

        emit_next_tile_stats(t)

        if t < TQ:
            vNMq = dv(lambda e, s=s: e.reduce_max(
                nmax[:, 2 + s:3 + s], pbA[...], axis=AX, negate=True),
                waits=[("cPE", vQ)])
            vEQ = ac(lambda e, s=s: e.activation(
                eq_buf[...], pbA[...], ACTF.Exp, bias=nmax[:, 2 + s:3 + s],
                scale=1.0), waits=[("cPE", vQ), ("cDV", vNMq)])
            pbA_free = [("cAC", vEQ), ("cDV", vNMq)]
            vTq = None
            for j in range(CH):
                vTq = pe(lambda e, j=j: e.transpose(
                    pbtr[:, j * 128:(j + 1) * 128],
                    eq_buf[:, j * 128:(j + 1) * 128], identb[...]),
                    waits=[("cAC", vEQ), ("cDV", evtr_state[0])]
                    if j == 0 else (),
                    inc=(j == CH - 1))
            evtr_state[0] = dv(lambda e, t=t, s=s: e.tensor_copy(
                eqt_st[:, s * D:(s + 1) * D], pbtr[...]),
                waits=[("cPE", vTq), ("dS2", eqt_store.get(t - 2))]
                + dbg_w())
            eqt_store[t] = dma("sync", "dS2", lambda e, t=t, s=s: e.dma_start(
                s_eqT.ap()[t * 128:(t + 1) * 128, :],
                eqt_st[:, s * D:(s + 1) * D]),
                waits=[("cDV", evtr_state[0])])
            if t == 0:
                dbg_store("eqT0", eqt_st[:, 0:D], [("cDV", evtr_state[0])])

        prep2_drain_adds(upto=p2s[0])
        for _ in range(2):
            if p1a[0] < p1l[0]:
                emit_prep1_abs(p1a[0])
                p1a[0] += 1
            elif p1a[0] >= len(prep_chunks) and not fin_done[0]:
                emit_finalize()
            elif fin_done[0] and p2s[0] < p2l[0]:
                emit_prep2_signs(p2s[0])
                p2s[0] += 1
        for _ in range(2):
            if prep_i < len(prep_chunks) and prep_i - p1a[0] < 2:
                emit_prep1_load(prep_i)
                prep_i += 1
                p1l[0] += 1
        if fin_done[0]:
            for _ in range(2):
                if p2l[0] < NP2 and p2l[0] - p2s[0] < 2:
                    emit_prep2_load(p2l[0])
                    p2l[0] += 1

    while p1a[0] < len(prep_chunks):
        if prep_i < len(prep_chunks) and prep_i - p1a[0] < 2:
            emit_prep1_load(prep_i)
            prep_i += 1
            p1l[0] += 1
        emit_prep1_abs(p1a[0])
        p1a[0] += 1
    if not fin_done[0]:
        emit_finalize()
    while p2a[0] < NP2:
        prep2_drain_adds()
        if p2s[0] < p2l[0]:
            emit_prep2_signs(p2s[0])
            p2s[0] += 1
        elif p2l[0] < NP2 and p2l[0] - p2s[0] < 2:
            emit_prep2_load(p2l[0])
            p2l[0] += 1
    vEQT_done = evtr_state[0]

    vKZ = p.op("gpsimd", lambda e: e.memset(kvs[...], 0.0),
               [("cDV", vRows)], "cGP", 1)
    kv_evac = None
    for pr in range(NPAIR):
        b, slot = pr // 3, PSTR * (pr % 3)
        w0 = [("cPE", kv_last), ("cGP", vKZ)] if pr == 0 else []
        dv(lambda e, pr=pr, b=b, slot=slot: e.tensor_copy(
            kvs[0:64, pr * PSTR:pr * PSTR + 64], pbK[b][0:64, slot:slot + 64]),
            waits=w0)
        dv(lambda e, pr=pr, b=b, slot=slot: e.tensor_copy(
            kvs[64:128, pr * PSTR + 64:pr * PSTR + 128],
            pbK[b][64:128, slot + 64:slot + 128]))
        dv(lambda e, pr=pr, b=b, slot=slot: e.tensor_copy(
            kvs[0:64, pr * PSTR + 128:pr * PSTR + 129],
            pbK[b][0:64, slot + 128:slot + 129]))
        kv_evac = dv(lambda e, pr=pr, b=b, slot=slot: e.tensor_copy(
            kvs[64:128, pr * PSTR + 129:pr * PSTR + 130],
            pbK[b][64:128, slot + 128:slot + 129]))
    dbg_store("kvs", kvs[...], [("cDV", kv_evac)])

    st_sweep.close()
    st_attn.close()

    wgT = sb(st_ffnA, "wgT", [128, CH * DI], BF16)
    wuT = sb(st_ffnA, "wuT", [128, CH * DI], BF16)
    ln2b = sb(st_ffnA, "ln2b", [128, D], F32)
    tq16 = sb(st_ffnA, "tq16", [128, 2 * D], F16)
    aq2 = sb(st_ffnA, "aq2", [128, D], BF16)
    aq2T = sb(st_ffnA, "aq2T", [128, 2 * D], BF16)
    silu_b = sb(st_ffnA, "silu_b", [128, 2 * 512], F32)
    hmid = sb(st_ffnA, "hmid", [128, DI], F32)
    hqb = sb(st_ffnA, "hqb", [128, DI], BF16)
    smal = sb(st_ffnA, "smal", [128, 32], F32)
    hmx = sb(st_ffnA, "hmx", [128, 16], F32)

    fA = fence()

    def _wdone(last):
        out = []
        for par, sem in enumerate(("dWS", "dWS1")):
            ks = [k for k in range(last + 1) if k % 2 == par and k in prep2_st]
            if ks:
                out.append((sem, prep2_st[max(ks)]))
        return out

    wg_done = _wdone(UPW - 1)
    wu_done = _wdone(2 * UPW - 1)
    wd_done = _wdone(3 * UPW - 1)

    eqt_loads = {}
    ctx_x_loads = {}
    ctx_pe = {}
    x2_st = {}
    ln2_ac = {}
    st1v = {}
    lnA_dv = {}

    def emit_eqt_load(t):
        if t >= TQ:
            return
        s = t % 2
        eqt_loads[t] = dma("sync", ("dLA", "dLA1")[s],
                           lambda e, t=t, s=s: e.dma_start(
            eqt_ld[:, s * D:(s + 1) * D], s_eqT.ap()[t * 128:(t + 1) * 128, :]),
            waits=[("dS2", eqt_store[t]), ("cPE", ctx_pe.get(t - 2))]
            + dbg_w())

    def emit_ctx_xload(t):
        if t >= TQ:
            return
        s = t % 2
        xw = ([("cDV", xr_dv[TKV - 1]), ("cAC", stats_marks[TKV - 1][1])]
              if t < 2 else
              [("cDV", ln2_ac.get(t - 2)), ("dS2", x2_st.get(t - 2)),
               ("cDV", lnA_dv.get(t - 2))])
        ctx_x_loads[t] = dma("sync", ("dX", "dX1")[s],
                             lambda e, t=t, s=s: e.dma_start(
            xt_buf[:, s * D:(s + 1) * D], x_tile_ap(t)), waits=xw + dbg_w())

    emit_ctx_xload(0)
    emit_eqt_load(0)
    emit_eqt_load(1)
    emit_ctx_xload(1)

    wgld, wuld = {}, {}
    for c in range(CH):
        wgld[c] = (("dWL", "dWL1")[c % 2], dma(
            "sync", ("dWL", "dWL1")[c % 2], lambda e, c=c: e.dma_start(
                wgT[:, c * DI:(c + 1) * DI],
                s_q["g"].ap()[:, c * 128:(c + 1) * 128],
                transpose=True), waits=wg_done + fA))
    for c in range(CH):
        wuld[c] = (("dWL", "dWL1")[c % 2], dma(
            "sync", ("dWL", "dWL1")[c % 2], lambda e, c=c: e.dma_start(
                wuT[:, c * DI:(c + 1) * DI],
                s_q["u"].ap()[:, c * 128:(c + 1) * 128],
                transpose=True), waits=wu_done + fA))

    GB = [pbA[:, 0:512], pbA[:, 512:D]]
    UB = [pbV[:, 0:512], pbV[:, 512:D]]
    gb_free = [pbA_free[:], pbA_free[:]]
    ub_free = [pbV_free[:], pbV_free[:]]
    silu_free = [[], []]
    tr_free = [("cDV", vEQT_done)]
    hq_st = {}
    vHQ_last = None
    ctx_banks_free = [("cDV", kv_evac)]
    ctx_vm = {}

    def emit_ctx(t):
        nonlocal ctx_banks_free
        s = t % 2
        X2 = xt_buf[:, s * D:(s + 1) * D]
        vC = None
        for pr in range(NPAIR):
            b, slot = pr // 3, PSTR * (pr % 3)
            vC = pe(lambda e, pr=pr, b=b, slot=slot, t=t, s=s: e.matmul(
                pbK[b][:, slot:slot + 130],
                eqt_ld[:, s * D + pr * 128:s * D + (pr + 1) * 128],
                kvs[:, pr * PSTR:pr * PSTR + 130],
                start=(pr % 3 == 0), stop=(pr % 3 == 2 or pr == NPAIR - 1)),
                waits=(ctx_banks_free + [("cDV", vEQT_done),
                                         (("dLA", "dLA1")[s], eqt_loads[t])])
                if pr == 0 else (),
                inc=(pr == NPAIR - 1))
        ctx_pe[t] = vC
        vRD = None
        for b in range(3):
            npr = 3 if b < 2 else 2
            vRD = dv(lambda e, b=b, npr=npr: e.reciprocal(
                recden[:, 6 * b:6 * b + 2 * npr]
                .rearrange("p (c k) -> p c k", k=2),
                pbK[b][:, 0:npr * PSTR].rearrange("p (c k) -> p c k", k=PSTR)
                [:, :, 128:130]),
                waits=[("cPE", vC)] if b == 0 else ())
        vM = None
        for b in range(3):
            npr = 3 if b < 2 else 2
            vM = dv(lambda e, b=b, npr=npr: e.tensor_tensor(
                tmp_ctx[:, b * 384:b * 384 + npr * 128]
                .rearrange("p (c h k) -> p c h k", h=2, k=64),
                pbK[b][:, 0:npr * PSTR].rearrange("p (c k) -> p c k", k=PSTR)
                [:, :, 0:128].rearrange("p c (h k) -> p c h k", k=64),
                recden[:, 6 * b:6 * b + 2 * npr]
                .rearrange("p (c h k) -> p c h k", h=2, k=1)
                .broadcast_to([128, npr, 2, 64]),
                op=ALU.mult),
                waits=([("cDV", vRD)] + ([("cDV", ln2_ac[t - 1])]
                                         if t >= 1 else []))
                if b == 0 else (),
                drain=(b == 0))
        ctx_vm[t] = vM
        vAdd = dv(lambda e, X2=X2: e.tensor_add(X2, X2, tmp_ctx[...]),
                  waits=[(("dX", "dX1")[s], ctx_x_loads[t])], drain=True)
        ctx_banks_free = [("cDV", vM)]
        x2_st[t] = dma("sync", "dS2", lambda e, t=t, X2=X2: e.dma_start(
            s_x2.ap()[t * 128:(t + 1) * 128, :], X2), waits=[("cDV", vAdd)])
        dv(lambda e, t=t, X2=X2: e.reduce_sum(
            m2r2[:, M2["sum"] + t:M2["sum"] + t + 1], X2, axis=AX),
            drain=True)
        ln2_ac[t] = dv(lambda e, t=t, X2=X2: e.scalar_tensor_tensor(
            tmp_ctx[...], X2, 0.0, X2, op0=ALU.add, op1=ALU.mult,
            accum_out=m2r2[:, M2["ssq"] + t:M2["ssq"] + t + 1]),
            drain=True)
        if t == 0:
            dbg_store("x20", X2, [("cDV", vAdd)])

    def emit_ln2fin(t):
        c2 = M2
        dv(lambda e, t=t: e.tensor_single_scalar(
            m2r2[:, c2["m"] + t:c2["m"] + t + 1],
            m2r2[:, c2["sum"] + t:c2["sum"] + t + 1],
            1.0 / D, op=ALU.mult), waits=[("cDV", ln2_ac[t])], drain=True)
        dv(lambda e, t=t: e.tensor_tensor(
            m2r2[:, c2["sp"] + t:c2["sp"] + t + 1],
            m2r2[:, c2["m"] + t:c2["m"] + t + 1],
            m2r2[:, c2["m"] + t:c2["m"] + t + 1], op=ALU.mult), drain=True)
        dv(lambda e, t=t: e.scalar_tensor_tensor(
            m2r2[:, c2["sp2"] + t:c2["sp2"] + t + 1],
            m2r2[:, c2["ssq"] + t:c2["ssq"] + t + 1], 1.0 / D,
            m2r2[:, c2["sp"] + t:c2["sp"] + t + 1],
            op0=ALU.mult, op1=ALU.subtract), drain=True)
        dv(lambda e, t=t: e.tensor_single_scalar(
            m2r2[:, c2["sp"] + t:c2["sp"] + t + 1],
            m2r2[:, c2["sp2"] + t:c2["sp2"] + t + 1],
            EPS, op=ALU.add), drain=True)
        u5 = dv(lambda e, t=t: e.reciprocal(
            m2r2[:, c2["sp2"] + t:c2["sp2"] + t + 1],
            m2r2[:, c2["sp"] + t:c2["sp"] + t + 1]), drain=True)
        st1v[t] = dict(vR2=ac(lambda e, t=t: e.activation(
            m2r2[:, c2["r"] + t:c2["r"] + t + 1],
            m2r2[:, c2["sp2"] + t:c2["sp2"] + t + 1],
            ACTF.Sqrt, bias=0.0, scale=1.0), waits=[("cDV", u5)], drain=True))

    def stage1_front_a(t):
        s = t % 2
        sm = 16 * s
        X2 = xt_buf[:, s * D:(s + 1) * D]
        vLN = dv(lambda e, X2=X2, t=t: e.tensor_scalar(
            ln2b[...], X2, m2r2[:, M2["m"] + t:M2["m"] + t + 1],
            m2r2[:, M2["r"] + t:M2["r"] + t + 1],
            op0=ALU.subtract, op1=ALU.mult),
            waits=[("cAC", st1v[t]["vR2"])] + dbg_w(), drain=True)
        lnA_dv[t] = vLN
        dv(lambda e, sm=sm: e.reduce_max(smal[:, sm:sm + 1], ln2b[...],
                                         axis=AX, apply_absolute_value=True),
           drain=True)
        dv(lambda e, sm=sm: e.tensor_single_scalar(
            smal[:, sm + 1:sm + 2], smal[:, sm:sm + 1], EPS, op=ALU.max),
            drain=True)
        dv(lambda e, sm=sm: e.reciprocal(smal[:, sm + 2:sm + 3],
                                         smal[:, sm + 1:sm + 2]), drain=True)
        vS127 = dv(lambda e, sm=sm: e.tensor_single_scalar(
            smal[:, sm + 3:sm + 4], smal[:, sm + 2:sm + 3], 127.0,
            op=ALU.mult), drain=True)
        st1v[t].update(sm=sm, vS127=vS127, vLN=vLN)

    def stage1_front_b(t):
        s = t % 2
        sm = st1v[t]["sm"]
        dv(lambda e, sm=sm: e.tensor_single_scalar(
            smal[:, sm + 4:sm + 5], smal[:, sm + 1:sm + 2], 1.0 / 127.0,
            op=ALU.mult), drain=True)
        vGS = dv(lambda e, sm=sm: e.tensor_tensor(
            smal[:, sm + 5:sm + 6], smal[:, sm + 4:sm + 5],
            bcast[:, GLO["ngg"]:GLO["ngg"] + 1], op=ALU.mult),
            waits=[("cDV", fin_state["vBC"])], drain=True)
        vUS = dv(lambda e, sm=sm: e.tensor_tensor(
            smal[:, sm + 6:sm + 7], smal[:, sm + 4:sm + 5],
            bcast[:, GLO["ngu"]:GLO["ngu"] + 1], op=ALU.mult), drain=True)
        vTQ = ac(lambda e, s=s, sm=sm: e.activation(
            tq16[:, s * D:(s + 1) * D], ln2b[...], ACTF.Copy,
            bias=1536.0, scale=smal[:, sm + 3:sm + 4]),
            waits=[("cDV", st1v[t]["vS127"])], drain=True)
        vAQ = ac(lambda e, s=s: e.activation(
            aq2[...], tq16[:, s * D:(s + 1) * D], ACTF.Copy,
            bias=-1536.0, scale=1.0),
            waits=([("cPE", last_vTA[0])]
                   if last_vTA[0] is not None else [])
            + dbg_w(), drain=True)
        if t == 0:
            dbg_store("ln20", ln2b[...], [("cDV", st1v[t]["vLN"])])
            dbg_store("aq20", aq2[...], [("cAC", vAQ)])
        st1v[t].update(vAQ=vAQ, vGS=vGS, vUS=vUS)

    last_vTA = [None]

    def stage1_pe(t):
        nonlocal_tr = st1v[t]
        s = t % 2
        vTA = None
        for j in range(CH):
            vTA = pe(lambda e, j=j: e.transpose(
                pbtr[:, j * 128:(j + 1) * 128], aq2[:, j * 128:(j + 1) * 128],
                identb[...]),
                waits=([("cAC", nonlocal_tr["vAQ"])] + tr_free[:])
                if j == 0 else (),
                inc=(j == CH - 1))
        last_vTA[0] = vTA
        ev = dv(lambda e, s=s: e.tensor_copy(aq2T[:, s * D:(s + 1) * D],
                                             pbtr[...]),
                waits=[("cPE", vTA)])
        nonlocal_tr["aqT_ev"] = ev
        tr_free[:] = [("cDV", ev)]

    def main_mm(t, nxt=None):
        nonlocal vHQ_last
        s = t % 2
        sm = st1v[t]["sm"]
        aqT_ev = st1v[t]["aqT_ev"]
        vGS, vUS = st1v[t]["vGS"], st1v[t]["vUS"]
        if t == 0:
            dbg_store("sgq", s_q["g"].ap()[:, 0:128], wg_done)
            dbg_store("wgT0", wgT[:, 0:DI], [wgld[c] for c in range(CH)])
            dbg_store("wuT0", wuT[:, 0:DI], [wuld[c] for c in range(CH)])
            dbg_store("aqT0", aq2T[:, 0:D], [("cDV", aqT_ev)])
        for oc in range(CH):
            if nxt is not None:
                if oc == 2:
                    emit_ln2fin(nxt)
                elif oc == 4:
                    stage1_front_a(nxt)
                elif oc == 5:
                    stage1_front_b(nxt)
            gp, up_ = GB[oc % 2], UB[oc % 2]
            ss = oc % 2
            vG = None
            for c in range(CH):
                vG = pe(lambda e, c=c, oc=oc, gp=gp, s=s: e.matmul(
                    gp[:, 0:512], aq2T[:, s * D + c * 128:s * D + (c + 1) * 128],
                    wgT[:, c * DI + 512 * oc: c * DI + 512 * oc + 512],
                    start=(c == 0), stop=(c == CH - 1)),
                    waits=(([("cDV", aqT_ev)] + gb_free[ss])
                           if c == 0 else []) + [wgld[c]],
                    inc=(c == CH - 1))
            vU = None
            for c in range(CH):
                vU = pe(lambda e, c=c, oc=oc, up_=up_, s=s: e.matmul(
                    up_[:, 0:512], aq2T[:, s * D + c * 128:s * D + (c + 1) * 128],
                    wuT[:, c * DI + 512 * oc: c * DI + 512 * oc + 512],
                    start=(c == 0), stop=(c == CH - 1)),
                    waits=(ub_free[ss] if c == 0 else []) + [wuld[c]],
                    inc=(c == CH - 1))
            vSI = ac(lambda e, ss=ss, gp=gp, sm=sm: e.activation(
                silu_b[:, ss * 512:(ss + 1) * 512], gp[:, 0:512], ACTF.Silu,
                bias=0.0, scale=smal[:, sm + 5:sm + 6]),
                waits=[("cPE", vG), ("cDV", vGS)] + silu_free[ss], drain=True)
            gb_free[ss] = [("cAC", vSI)]
            if t == 0 and oc == 0:
                dbg_store("silu00", silu_b[:, 0:512], [("cAC", vSI)])
            vH = dv(lambda e, oc=oc, ss=ss, up_=up_, sm=sm:
                    e.scalar_tensor_tensor(
                hmid[:, oc * 512:(oc + 1) * 512], up_[:, 0:512],
                smal[:, sm + 6:sm + 7], silu_b[:, ss * 512:(ss + 1) * 512],
                op0=ALU.mult, op1=ALU.mult),
                waits=[("cPE", vU), ("cAC", vSI), ("cDV", vUS)]
                + (dbg_w() if oc == 0 else []), drain=True)
            dv(lambda e, oc=oc, s=s: e.reduce_max(
                hmx[:, s * 8 + oc:s * 8 + oc + 1],
                hmid[:, oc * 512:(oc + 1) * 512], axis=AX,
                apply_absolute_value=True), drain=True)
            ub_free[ss] = [("cDV", vH)]
            silu_free[ss] = [("cDV", vH)]
        if nxt is not None:
            stage1_pe(nxt)
        vHM_final = dv(lambda e, sm=sm, s=s: e.reduce_max(
            smal[:, sm + 7:sm + 8], hmx[:, s * 8:(s + 1) * 8], axis=AX),
            drain=True)
        dv(lambda e, sm=sm: e.tensor_single_scalar(
            smal[:, sm + 8:sm + 9], smal[:, sm + 7:sm + 8], EPS, op=ALU.max),
            drain=True)
        dv(lambda e, sm=sm: e.reciprocal(smal[:, sm + 9:sm + 10],
                                         smal[:, sm + 8:sm + 9]), drain=True)
        vS127h = dv(lambda e, sm=sm: e.tensor_single_scalar(
            smal[:, sm + 7:sm + 8], smal[:, sm + 9:sm + 10], 127.0,
            op=ALU.mult), drain=True)
        vDS_last = dv(lambda e, t=t, sm=sm: e.tensor_single_scalar(
            dscale[:, t:t + 1], smal[:, sm + 8:sm + 9], 1.0 / 127.0,
            op=ALU.mult), drain=True)
        vHQ = None
        for sl_ in range(4):
            vTQh = ac(lambda e, sl_=sl_, s=s, sm=sm: e.activation(
                tq16[:, s * D:(s + 1) * D], hmid[:, sl_ * D:(sl_ + 1) * D],
                ACTF.Copy, bias=1536.0, scale=smal[:, sm + 7:sm + 8]),
                waits=([("cDV", vS127h)] if sl_ == 0 else [])
                + ([("cDV", vHQ)] if sl_ > 0 else []), drain=True)
            vHQ = dv(lambda e, sl_=sl_, s=s: e.tensor_single_scalar(
                hqb[:, sl_ * D:(sl_ + 1) * D], tq16[:, s * D:(s + 1) * D],
                1536.0, op=ALU.subtract),
                waits=[("cAC", vTQh)] + (([("dS2", hq_st.get(t - 1))]
                                          + dbg_w()) if sl_ == 0 else []),
                drain=True)
        vHQ_last = vHQ
        ffnA_last_ac[0] = vTQh
        ffnA_last_dv[0] = vHQ
        if t == 0:
            dbg_store("hmid0", hmid[...], [("cDV", vHM_final)])
            dbg_store("hq0", hqb[...], [("cDV", vHQ)])
        hq_st[t] = dma("sync", "dS2", lambda e, t=t: e.dma_start(
            s_hq4.ap()[t * DI:(t + 1) * DI, :]
            .rearrange("(c n) k -> n c k", n=128),
            hqb[...].rearrange("p (c k) -> p c k", k=128),
            allow_slow_non_contiguous=True),
            waits=[("cDV", vHQ)])
        ffnA_pe_last[0] = vU

    ffnA_pe_last = [None]
    ffnA_last_dv = [None]
    ffnA_last_ac = [None]
    for t in range(TQ):
        emit_ctx(t)
        if t >= 1:
            main_mm(t - 1, nxt=t)
        else:
            emit_ln2fin(0)
            stage1_front_a(0)
            stage1_front_b(0)
            stage1_pe(0)
        emit_eqt_load(t + 2)
        emit_ctx_xload(t + 2)
    main_mm(TQ - 1)

    st_ffnA.close()
    st_long.close()
    wdT = sb(st_ffnB, "wdT", [128, CHI * D], BF16)
    hqT = sb(st_ffnB, "hqT", [128, 2 * DI], BF16)
    x2tB = sb(st_ffnB, "x2tB", [128, 2 * D], F32)
    ybuf = sb(st_ffnB, "ybuf", [128, 2 * D], F32)
    dsct = sb(st_ffnB, "dsct", [128, 2], F32)

    fBpe = [("cPE", ffnA_pe_last[0])]
    fBfree = fBpe + [("cAC", ffnA_last_ac[0]), ("cDV", ffnA_last_dv[0])]
    fBwd = fBfree + [("dS2", x2_st[TQ - 1])]

    hq_ld = {}
    hq_cons = {}

    def emit_hq_load(t):
        if t >= TQ:
            return
        s = t % 2
        lbsem = ("dLB", "dLB1")[s]
        hq_ld[t] = dma("scalar", lbsem, lambda e, t=t, s=s: e.dma_start(
            hqT[:, s * DI:(s + 1) * DI],
            s_hq4.ap()[t * DI:(t + 1) * DI, :], transpose=True),
            waits=[("dS2", hq_st[t]), ("cPE", hq_cons.get(t - 2))] + fBpe
            + dbg_w())

    wdld = {}

    def emit_wd_loads(c0, c1):
        for c in range(c0, c1):
            sem = ("dWL2", "dWL3")[c % 2]
            wdld[c] = (sem, dma("scalar", sem, lambda e, c=c: e.dma_start(
                wdT[:, c * D:(c + 1) * D],
                s_q["d"].ap()[:, c * 128:(c + 1) * 128],
                transpose=True), waits=wd_done + fBwd))

    emit_hq_load(0)
    emit_wd_loads(0, 8)
    emit_hq_load(1)
    emit_wd_loads(8, CHI)

    DB = [pbA, pbV]
    db_free = [fBfree, fBfree]
    y_st = {}
    x2_cons = {}
    for t in range(TQ):
        s = t % 2
        HQT = hqT[:, s * DI:(s + 1) * DI]
        X2B = x2tB[:, s * D:(s + 1) * D]
        YB = ybuf[:, s * D:(s + 1) * D]
        lbsem = ("dLB", "dLB1")[s]
        lxsem = ("dLX", "dLX1")[s]
        vLH = hq_ld[t]
        vLX = dma("sync", lxsem, lambda e, t=t, X2B=X2B: e.dma_start(
            X2B, s_x2.ap()[t * 128:(t + 1) * 128, :]),
            waits=[("cDV", x2_cons.get(t - 2)), ("dS2", x2_st[t])] + fBpe)
        vDSC = dv(lambda e, t=t, s=s: e.tensor_tensor(
            dsct[:, s:s + 1], dscale[:, t:t + 1],
            bcast[:, GLO["ngd"]:GLO["ngd"] + 1], op=ALU.mult),
            waits=[("cDV", fin_state["vBC"])] + fBpe, drain=True)
        pb = DB[t % 2]
        vTlast = None
        for jh in range(2):
            for c in range(CHI):
                vTlast = pe(lambda e, c=c, jh=jh, pb=pb, HQT=HQT, s=s:
                            e.matmul(
                    pb[:, 512 * jh:512 * jh + 512],
                    HQT[:, c * 128:(c + 1) * 128],
                    wdT[:, c * D + 512 * jh: c * D + 512 * jh + 512],
                    start=(c == 0), stop=(c == CHI - 1)),
                    waits=(([(lbsem, vLH)] + db_free[t % 2])
                           if (c == 0 and jh == 0) else [])
                    + ([wdld[c]] if jh == 0 else []),
                    inc=(jh == 1 and c == CHI - 1))
        hq_cons[t] = vTlast
        emit_hq_load(t + 2)
        vY = None
        for jh in range(2):
            vY = dv(lambda e, jh=jh, pb=pb, X2B=X2B, YB=YB, s=s:
                    e.scalar_tensor_tensor(
                        YB[:, 512 * jh:512 * jh + 512],
                        pb[:, 512 * jh:512 * jh + 512], dsct[:, s:s + 1],
                        X2B[:, 512 * jh:512 * jh + 512],
                        op0=ALU.mult, op1=ALU.add),
                    waits=[("cPE", vTlast), (lxsem, vLX), ("cDV", vDSC),
                           ("dY", y_st.get(t - 2))], drain=True)
        x2_cons[t] = vY
        db_free[t % 2] = [("cDV", vY)]
        y_st[t] = dma("sync", "dY", lambda e, t=t, YB=YB: e.dma_start(
            y_out.ap()[t * 128:(t + 1) * 128, :], YB), waits=[("cDV", vY)])

    p.op("sync", lambda e: e.nop(), waits=[("dY", y_st[TQ - 1])])

    sems = {}
    sem_stack = ExitStack()
    for name in ("cPE", "cDV", "cAC", "cGP", "dX", "dX1", "dS2", "dY", "dWP",
                 "dWP1", "dWP2", "dWS", "dWS1", "dWS2", "dWL", "dWL1", "dWL2",
                 "dWL3", "dGC", "dLA", "dLA1", "dLB", "dLB1", "dLX", "dLX1"):
        sems[name] = sem_stack.enter_context(nc.semaphore(name))
    p.materialize(nc, sems)
    sem_stack.close()
    st_ffnB.close()
    st_base.close()
    return nc, sorted(dbg_t)


def _shard_inputs(inputs):
    hs = np.asarray(inputs["L0_attn_hidden_states"], np.float32)
    B, S, _ = hs.shape
    half = S // 2
    base = {
        "wq": np.ascontiguousarray(inputs["L0_attn_Wq"], dtype=np.float32),
        "wk": np.ascontiguousarray(inputs["L0_attn_Wk"], dtype=np.float32),
        "wv": np.ascontiguousarray(inputs["L0_attn_Wv"], dtype=np.float32),
        "wfq": np.ascontiguousarray(inputs["L0_attn_Wfq"], dtype=np.float32),
        "wfk": np.ascontiguousarray(inputs["L0_attn_Wfk"], dtype=np.float32),
        "bq": np.ascontiguousarray(inputs["L0_attn_bq"], dtype=np.float32),
        "bk": np.ascontiguousarray(inputs["L0_attn_bk"], dtype=np.float32),
        "bv": np.ascontiguousarray(inputs["L0_attn_bv"], dtype=np.float32),
        "bfq": np.ascontiguousarray(inputs["L0_attn_bfq"], dtype=np.float32),
        "bfk": np.ascontiguousarray(inputs["L0_attn_bfk"], dtype=np.float32),
        "wg": np.ascontiguousarray(inputs["L0_ffn_w_gate"], dtype=np.float32),
        "wu": np.ascontiguousarray(inputs["L0_ffn_w_up"], dtype=np.float32),
        "wd": np.ascontiguousarray(inputs["L0_ffn_w_down"], dtype=np.float32),
        "id_b": np.eye(128, dtype=bfd),
        "id_f": np.eye(128, dtype=np.float32),
    }
    in_maps = []
    for c in range(2 * B):
        b, h = c // 2, c % 2
        m = dict(base)
        m["x_own"] = np.ascontiguousarray(hs[b, h * half:(h + 1) * half])
        m["x_oth"] = np.ascontiguousarray(hs[b, (1 - h) * half:(2 - h) * half])
        in_maps.append(m)
    return in_maps


def kernel(**inputs):
    nc, _ = build_nc(16, 32)
    in_maps = _shard_inputs(inputs)
    res = run_bass_kernel_spmd(nc, in_maps, core_ids=list(range(8)))
    hs = inputs["L0_attn_hidden_states"]
    B, S, E = hs.shape
    out = np.empty((B, S, E), np.float32)
    half = S // 2
    for c in range(2 * B):
        b, h = c // 2, c % 2
        out[b, h * half:(h + 1) * half] = res.results[c]["y"]
    return out


# version 40 (final)
# speedup vs baseline: 1.0257x; 1.0221x over previous
import numpy as np
import ml_dtypes
import concourse.bass as bass
import concourse.mybir as mybir
from concourse.bass_utils import run_bass_kernel_spmd
from contextlib import ExitStack

F32 = mybir.dt.float32
F16 = mybir.dt.float16
BF16 = mybir.dt.bfloat16
AX = mybir.AxisListType.X
ALU = mybir.AluOpType
ACTF = mybir.ActivationFunctionType
bfd = ml_dtypes.bfloat16

D = 1024
CH = D // 128
DI = 4096
CHI = DI // 128
NPAIR = 8
PSTR = 130
EPS = 1e-5

ENGS = ("tensor", "vector", "scalar", "sync", "gpsimd")


class Plan:
    def __init__(self):
        self.ops = {e: [] for e in ENGS}
        self.cnt = {}
        self.last_wait = {e: {} for e in ENGS}

    def op(self, eng, fn, waits=(), inc=None, inc_by=1, drain=False):
        w = []
        for sem, val in waits:
            if val is None:
                continue
            lw = self.last_wait[eng].get(sem, 0)
            if val > lw:
                w.append((sem, val))
                self.last_wait[eng][sem] = val
        val = None
        if inc is not None:
            val = self.cnt.get(inc, 0) + inc_by
            self.cnt[inc] = val
        self.ops[eng].append((fn, tuple(w), inc, inc_by, drain))
        return val

    def materialize(self, nc, sems):
        for eng, stream in self.ops.items():
            for fn, waits, inc, inc_by, drain in stream:
                for sem, val in waits:
                    assert val <= self.cnt.get(sem, 0), (
                        f"impossible wait {sem}>={val} on {eng} "
                        f"(total {self.cnt.get(sem, 0)})")
        with nc.Block() as blk:
            for eng in ENGS:
                stream = self.ops[eng]
                if not stream:
                    continue

                def mk(stream):
                    def body(e):
                        for fn, waits, inc, inc_by, drain in stream:
                            for sem, val in waits:
                                e.wait_ge(sems[sem], val)
                            if drain:
                                e.drain()
                            inst = fn(e)
                            if inc is not None:
                                inst.then_inc(sems[inc], inc_by)
                    return body

                getattr(blk, eng)(mk(stream))


def build_nc(TQ=16, TKV=32, dbg=()):
    assert TQ % 4 == 0 or TQ == 2, "stats batching assumes 4-tile groups"
    nc = bass.Bass("TRN2", target_bir_lowering=False, debug=False)
    NQ, NKV = TQ * 128, TKV * 128
    p = Plan()

    x_own = nc.dram_tensor("x_own", [NQ, D], F32, kind="ExternalInput")
    x_oth = nc.dram_tensor("x_oth", [NKV - NQ, D], F32, kind="ExternalInput")
    wf = {n: nc.dram_tensor(n, [D, D], F32, kind="ExternalInput")
          for n in ("wq", "wk", "wv", "wfq", "wfk")}
    bin_ = {n: nc.dram_tensor(n, [D], F32, kind="ExternalInput")
            for n in ("bq", "bk", "bv", "bfq", "bfk")}
    wg_f = nc.dram_tensor("wg", [DI, D], F32, kind="ExternalInput")
    wu_f = nc.dram_tensor("wu", [DI, D], F32, kind="ExternalInput")
    wd_f = nc.dram_tensor("wd", [D, DI], F32, kind="ExternalInput")
    id_b = nc.dram_tensor("id_b", [128, 128], BF16, kind="ExternalInput")
    id_f = nc.dram_tensor("id_f", [128, 128], F32, kind="ExternalInput")
    y_out = nc.dram_tensor("y", [NQ, D], F32, kind="ExternalOutput")
    s_nat = {n: nc.dram_tensor("s_" + n, [D, D], BF16, kind="Internal")
             for n in ("wfq", "wfk", "wv")}
    s_q = {"g": nc.dram_tensor("s_gq", [DI, D], BF16, kind="Internal"),
           "u": nc.dram_tensor("s_uq", [DI, D], BF16, kind="Internal"),
           "d": nc.dram_tensor("s_dq", [D, DI], BF16, kind="Internal")}
    s_x2 = nc.dram_tensor("s_x2", [NQ, D], F32, kind="Internal")
    s_hq4 = nc.dram_tensor("s_hq4", [NQ * (DI // 128), 128], BF16,
                           kind="Internal")
    s_eqT = nc.dram_tensor("s_eqT", [NQ, D], BF16, kind="Internal")

    dbg_t = {}

    def dbgT(name, shape, dt=F32):
        if name in dbg:
            dbg_t[name] = nc.dram_tensor("dbg_" + name, shape, dt,
                                         kind="ExternalOutput")

    dbgT("rows", [1, 6 * D], BF16); dbgT("rm_row", [1, NKV], BF16)
    dbgT("wcqT", [128, CH * D], BF16); dbgT("stat", [128, 6 * TKV])
    dbgT("ek0", [128, D], BF16); dbgT("zk0", [128, 1])
    dbgT("vst0", [128, NPAIR * PSTR], BF16)
    dbgT("kvs", [128, NPAIR * PSTR], BF16); dbgT("eqT0", [128, D], BF16)
    dbgT("x20", [128, D]); dbgT("bcast", [128, 9])
    dbgT("ln20", [128, D]); dbgT("aq20", [128, D], BF16)
    dbgT("hmid0", [128, DI]); dbgT("hq0", [128, DI], BF16)
    dbgT("wgT0", [128, DI], BF16); dbgT("wuT0", [128, DI], BF16)
    dbgT("aqT0", [128, D], BF16); dbgT("silu00", [128, 512])
    dbgT("sgq", [DI, 128], BF16)

    def dbg_store(name, ap, waits):
        if name in dbg_t:
            t = dbg_t[name]
            prev = p.cnt.get("dY", 0)
            p.op("sync", lambda e, t=t, ap=ap: e.dma_start(t.ap(), ap),
                 list(waits) + [("dY", prev if prev else None)], "dY", 16)

    def dbg_w():
        return [("dY", p.cnt["dY"])] if (dbg and p.cnt.get("dY")) else []

    def pe(fn, waits=(), inc=True, drain=False):
        return p.op("tensor", fn, waits, "cPE" if inc else None, 1, drain)

    def dv(fn, waits=(), inc=True, drain=False):
        return p.op("vector", fn, waits, "cDV" if inc else None, 1, drain)

    def ac(fn, waits=(), inc=True, drain=False):
        return p.op("scalar", fn, waits, "cAC" if inc else None, 1, drain)

    def dma(eng, sem, fn, waits=()):
        prev = p.cnt.get(sem, 0)
        return p.op(eng, fn, list(waits) + [(sem, prev if prev > 0 else None)],
                    sem, 16)

    ALLSEMS = ("cPE", "cDV", "cAC", "cGP", "dX", "dX1", "dS2", "dY", "dWP",
               "dWP1", "dWP2", "dWS", "dWS1", "dWS2", "dWL", "dWL1", "dWL2",
               "dWL3", "dGC", "dLA", "dLA1", "dLB", "dLB1", "dLX", "dLX1")

    def fence():
        return [(s, p.cnt[s]) for s in ALLSEMS if p.cnt.get(s)]

    st_base = ExitStack()
    st_long = ExitStack()
    st_attn = ExitStack()
    st_preq = ExitStack()
    st_prek = ExitStack()
    st_sweep = ExitStack()
    st_ffnA = ExitStack()
    st_ffnB = ExitStack()

    def sb(stk, name, shape, dt):
        return stk.enter_context(nc.sbuf_tensor(name, shape, dt))

    glob = sb(st_base, "glob", [1, 16], F32)
    GLO = dict(t0=0, t1=1, t2=2, thg=3, thu=4, thd=5, nthg=6, nthu=7, nthd=8,
               ngg=9, ngu=10, ngd=11)
    bcast = sb(st_base, "bcast", [128, 16], F32)
    m2r2 = sb(st_base, "m2r2", [128, 6 * TQ], F32)
    M2 = {n: i * TQ for i, n in enumerate(("sum", "ssq", "m", "r", "sp", "sp2"))}
    dscale = sb(st_base, "dscale", [128, TQ], F32)
    identb = sb(st_base, "identb", [128, 128], BF16)
    identf = sb(st_base, "identf", [128, 128], F32)
    ptmp = sb(st_base, "ptmp", [128, 8], F32)
    onesf_row = sb(st_base, "onesf_row", [1, 128], F32)

    xt_buf = sb(st_long, "xt_buf", [128, 2 * D], F32)
    kvs = sb(st_long, "kvs", [128, NPAIR * PSTR], BF16)
    recden = sb(st_long, "recden", [128, 16], F32)
    tmp_ctx = sb(st_long, "tmp_ctx", [128, D], F32)
    eqt_ld = sb(st_long, "eqt_ld", [128, 2 * D], BF16)

    wcqT = sb(st_attn, "wcqT", [128, CH * D], BF16)
    wckT = sb(st_attn, "wckT", [128, CH * D], BF16)
    wvT = sb(st_attn, "wvT", [128, CH * D], BF16)
    rows = sb(st_attn, "rows", [1, 6 * D], BF16)
    ROW = {n: i * D for i, n in enumerate(("wcqs", "wcks", "wvs", "bfq", "bfk", "bv"))}
    lhsT2 = sb(st_attn, "lhsT2", [2, NKV], BF16)
    rows2 = sb(st_attn, "rows2", [2, 3 * D], BF16)
    ones_row = sb(st_attn, "ones_row", [1, 128], BF16)
    ones_col = sb(st_attn, "ones_col", [128, 1], BF16)
    stat = sb(st_attn, "stat", [128, 6 * TKV], F32)
    STT = {n: i * TKV for i, n in enumerate(("sum", "ssq", "m", "r", "nrm", "sp"))}
    negrm_bf = sb(st_attn, "negrm_bf", [128, TKV], BF16)
    abscols = sb(st_attn, "abscols", [128, 3 * CHI], F32)
    bq_col = sb(st_attn, "bq_col", [128, CH], BF16)
    bk_col = sb(st_attn, "bk_col", [128, CH], BF16)
    sq_scr = sb(st_attn, "sq_scr", [128, D], F32)

    pbA = st_base.enter_context(nc.psum_tensor("pbA", [128, D], F32))
    pbV = st_base.enter_context(nc.psum_tensor("pbV", [128, D], F32))
    pbK = [st_base.enter_context(nc.psum_tensor(f"pbK{i}", [128, 512], F32))
           for i in range(3)]
    pbtr = st_base.enter_context(nc.psum_tensor("pbtr", [128, D], BF16))

    vIb = dma("sync", "dX", lambda e: e.dma_start(identb[...], id_b.ap()))
    vIf = dma("sync", "dX", lambda e: e.dma_start(identf[...], id_f.ap()))

    wq_nat = sb(st_preq, "wq_nat", [128, CH * D], BF16)
    wfqT = sb(st_preq, "wfqT", [128, CH * D], BF16)
    vGq = dma("gpsimd", "dGC", lambda e: e.dma_start(
        wq_nat[...].rearrange("p (c i) -> p c i", i=D),
        wf["wq"].ap().rearrange("(c p) i -> p c i", p=128)))
    vGn = {}
    for n in ("wfq", "wfk", "wv"):
        vGn[n] = dma("gpsimd", "dGC",
                     lambda e, n=n: e.dma_start(s_nat[n].ap(), wf[n].ap()))
    vGbq = dma("gpsimd", "dGC", lambda e: e.dma_start(
        bq_col[...], bin_["bq"].ap().rearrange("(c p) -> p c", p=128),
        allow_slow_non_contiguous=True))
    vGbk = dma("gpsimd", "dGC", lambda e: e.dma_start(
        bk_col[...], bin_["bk"].ap().rearrange("(c p) -> p c", p=128),
        allow_slow_non_contiguous=True))
    for n in ("bfq", "bfk", "bv"):
        dma("gpsimd", "dGC", lambda e, n=n: e.dma_start(
            rows[0:1, ROW[n]:ROW[n] + D],
            bin_[n].ap().rearrange("(a d) -> a d", a=1)))
    p.op("gpsimd", lambda e: e.memset(ones_row[...], 1.0), (), "cGP", 1)
    p.op("gpsimd", lambda e: e.memset(onesf_row[...], 1.0), (), "cGP", 1)
    p.op("gpsimd", lambda e: e.memset(lhsT2[...], 1.0), (), "cGP", 1)
    vGC_end = p.op("gpsimd", lambda e: e.memset(ones_col[...], 1.0), (), "cGP", 1)

    vWT = {}
    for n, tgt in (("wfq", wfqT), ("wv", wvT)):
        last = None
        for c in range(CH):
            last = dma("scalar", "dWL", lambda e, n=n, tgt=tgt, c=c: e.dma_start(
                tgt[:, c * D:(c + 1) * D],
                s_nat[n].ap()[:, c * 128:(c + 1) * 128], transpose=True),
                waits=[("dGC", vGn[n])])
        vWT[n] = last

    def emit_combine(w_nat, wfT, out, vnat, vwt, prev_ev):
        ev = prev_ev
        for mi in range(CH):
            vlast = None
            for h in range(2):
                for cj in range(CH):
                    vlast = pe(lambda e, mi=mi, h=h, cj=cj, w_nat=w_nat, wfT=wfT:
                               e.matmul(
                        pbA[:, 512 * h:512 * h + 512],
                        w_nat[:, cj * D + mi * 128: cj * D + mi * 128 + 128],
                        wfT[:, cj * D + 512 * h: cj * D + 512 * h + 512],
                        start=(cj == 0), stop=(cj == CH - 1)),
                        waits=[("dGC", vnat), ("dWL", vwt), ("cDV", ev)],
                        inc=(h == 1 and cj == CH - 1))
            ev = dv(lambda e, mi=mi, out=out: e.tensor_copy(
                out[:, mi * D:(mi + 1) * D], pbA[...]), waits=[("cPE", vlast)])
        return ev

    row_ev = [None]

    def emit_sum_row(srcT, dst_off, xw):
        vlast = None
        for h in range(2):
            for c in range(CH):
                vlast = pe(lambda e, h=h, c=c, srcT=srcT: e.matmul(
                    pbV[0:1, 512 * h:512 * h + 512], ones_col[...],
                    srcT[:, c * D + 512 * h: c * D + 512 * h + 512],
                    start=(c == 0), stop=(c == CH - 1)),
                    waits=xw + [("cGP", vGC_end), ("cDV", row_ev[-1])],
                    inc=(h == 1 and c == CH - 1))
        row_ev.append(dv(lambda e, dst_off=dst_off: e.tensor_copy(
            rows[0:1, dst_off:dst_off + D], pbV[0:1, :]), waits=[("cPE", vlast)]))

    def emit_bias_row(bcol, wfT, brow_off, dst_off, xw):
        vlast = None
        for h in range(2):
            for c in range(CH):
                vlast = pe(lambda e, h=h, c=c, bcol=bcol, wfT=wfT: e.matmul(
                    pbV[0:1, 512 * h:512 * h + 512], bcol[:, c:c + 1],
                    wfT[:, c * D + 512 * h: c * D + 512 * h + 512],
                    start=(c == 0), stop=False),
                    waits=xw + [("cGP", vGC_end), ("cDV", row_ev[-1])], inc=False)
            vlast = pe(lambda e, h=h, brow_off=brow_off: e.matmul(
                pbV[0:1, 512 * h:512 * h + 512], ones_row[0:1, 0:1],
                rows[0:1, brow_off + 512 * h: brow_off + 512 * h + 512],
                start=False, stop=True), inc=(h == 1))
        row_ev.append(dv(lambda e, dst_off=dst_off: e.tensor_copy(
            rows[0:1, dst_off:dst_off + D], pbV[0:1, :]), waits=[("cPE", vlast)]))

    vCq = emit_combine(wq_nat, wfqT, wcqT, vGq, vWT["wfq"], None)
    emit_sum_row(wcqT, ROW["wcqs"], [("cDV", vCq)])
    emit_bias_row(bq_col, wfqT, ROW["bfq"], ROW["bfq"], [])
    vQrows = row_ev[-1]
    st_preq.close()
    wk_nat = sb(st_prek, "wk_nat", [128, CH * D], BF16)
    wfkT = sb(st_prek, "wfkT", [128, CH * D], BF16)
    vGk = dma("gpsimd", "dGC", lambda e: e.dma_start(
        wk_nat[...].rearrange("p (c i) -> p c i", i=D),
        wf["wk"].ap().rearrange("(c p) i -> p c i", p=128)),
        waits=[("cDV", vQrows)])
    last = None
    for c in range(CH):
        last = dma("scalar", "dWL", lambda e, c=c: e.dma_start(
            wfkT[:, c * D:(c + 1) * D],
            s_nat["wfk"].ap()[:, c * 128:(c + 1) * 128], transpose=True),
            waits=[("dGC", vGn["wfk"]), ("cDV", vQrows)])
    vWT["wfk"] = last
    vCk = emit_combine(wk_nat, wfkT, wckT, vGk, vWT["wfk"], row_ev[-1])
    emit_sum_row(wckT, ROW["wcks"], [("cDV", vCk)])
    emit_sum_row(wvT, ROW["wvs"], [("dWL", vWT["wv"])])
    emit_bias_row(bk_col, wfkT, ROW["bfk"], ROW["bfk"], [])
    dma("sync", "dX", lambda e: e.dma_start(
        rows2[0:1, :], rows[0:1, 0:3 * D]), waits=[("cDV", row_ev[-1])])
    vRows2 = dma("sync", "dX", lambda e: e.dma_start(
        rows2[1:2, :], rows[0:1, 3 * D:6 * D]), waits=[("cDV", row_ev[-1])])
    vRows = row_ev[-1]
    st_prek.close()
    dbg_store("rows", rows[...], [("cDV", vRows)])
    dbg_store("wcqT", wcqT[...], [("cDV", vCq)])

    def x_tile_ap(t):
        if t < TQ:
            return x_own.ap()[t * 128:(t + 1) * 128, :]
        return x_oth.ap()[(t - TQ) * 128:(t - TQ + 1) * 128, :]

    evtr_state = [None]
    stats_marks = {}

    def emit_stats(t, vL, xsem, xap):
        c_ = STT
        vSum = dv(lambda e, t=t, xap=xap: e.reduce_sum(
            stat[:, c_["sum"] + t:c_["sum"] + t + 1], xap, axis=AX),
            waits=[(xsem, vL)])
        vSq = dv(lambda e, t=t, xap=xap: e.scalar_tensor_tensor(
            sq_scr[...], xap, 0.0, xap, op0=ALU.add, op1=ALU.mult,
            accum_out=stat[:, c_["ssq"] + t:c_["ssq"] + t + 1]), drain=True)
        dv(lambda e, t=t: e.tensor_single_scalar(
            stat[:, c_["m"] + t:c_["m"] + t + 1],
            stat[:, c_["sum"] + t:c_["sum"] + t + 1], 1.0 / D, op=ALU.mult),
            drain=True)
        dv(lambda e, t=t: e.tensor_tensor(
            stat[:, c_["sp"] + t:c_["sp"] + t + 1],
            stat[:, c_["m"] + t:c_["m"] + t + 1],
            stat[:, c_["m"] + t:c_["m"] + t + 1], op=ALU.mult), drain=True)
        dv(lambda e, t=t: e.scalar_tensor_tensor(
            stat[:, c_["sum"] + t:c_["sum"] + t + 1],
            stat[:, c_["ssq"] + t:c_["ssq"] + t + 1], 1.0 / D,
            stat[:, c_["sp"] + t:c_["sp"] + t + 1],
            op0=ALU.mult, op1=ALU.subtract), drain=True)
        vVe = dv(lambda e, t=t: e.tensor_single_scalar(
            stat[:, c_["ssq"] + t:c_["ssq"] + t + 1],
            stat[:, c_["sum"] + t:c_["sum"] + t + 1], EPS, op=ALU.add),
            drain=True)
        vLn = ac(lambda e, t=t: e.activation(
            stat[:, c_["sp"] + t:c_["sp"] + t + 1],
            stat[:, c_["ssq"] + t:c_["ssq"] + t + 1], ACTF.Ln,
            bias=0.0, scale=1.0), waits=[("cDV", vVe)], drain=True)
        vR = ac(lambda e, t=t: e.activation(
            stat[:, c_["r"] + t:c_["r"] + t + 1],
            stat[:, c_["sp"] + t:c_["sp"] + t + 1], ACTF.Exp,
            bias=0.0, scale=-0.5), drain=True)
        vNR = dv(lambda e, t=t: e.scalar_tensor_tensor(
            stat[:, c_["nrm"] + t:c_["nrm"] + t + 1],
            stat[:, c_["r"] + t:c_["r"] + t + 1], -1.0,
            stat[:, c_["m"] + t:c_["m"] + t + 1],
            op0=ALU.mult, op1=ALU.mult), waits=[("cAC", vR)], drain=True)
        vNB = dv(lambda e, t=t: e.tensor_copy(
            negrm_bf[:, t:t + 1], stat[:, c_["nrm"] + t:c_["nrm"] + t + 1]),
            drain=True)
        vTr = pe(lambda e, t=t: e.transpose(
            pbtr[0:1, 0:128], negrm_bf[:, t:t + 1], identb[...]),
            waits=[("cDV", vNB), ("dX", vIb), ("cDV", evtr_state[0])])
        evtr_state[0] = dv(lambda e, t=t: e.tensor_copy(
            lhsT2[0:1, t * 128:(t + 1) * 128], pbtr[0:1, 0:128]),
            waits=[("cPE", vTr)])
        stats_marks[t] = (vSq, vR)

    pchunk = sb(st_sweep, "pchunk", [128, 8 * D], F32)
    pp_buf = sb(st_sweep, "pp", [128, 4 * D], F32)
    pwq = sb(st_sweep, "pwq", [128, 8 * D], BF16)
    ppb = sb(st_sweep, "ppb", [128, 8 * D], BF16)
    UPW = CHI // 4
    prep_chunks = []
    for wi, wdram in enumerate((wg_f, wu_f)):
        for u in range(UPW):
            prep_chunks.append((
                wi, wdram,
                lambda a, u=u: a[u * 512:(u + 1) * 512, :]
                .rearrange("(c p) i -> p c i", p=128),
                lambda b, s: b[:, s * 4 * D:(s + 1) * 4 * D]
                .rearrange("p (c i) -> p c i", i=D)))
    for u in range(UPW):
        prep_chunks.append((
            2, wd_f,
            lambda a, u=u: a[u * 128:(u + 1) * 128, :],
            lambda b, s: b[:, s * 4 * D:(s + 1) * 4 * D]))
    prep1_ac = {}
    prep1_ld = {}

    def emit_prep1_load(k):
        wi, wdram, sl_, dst_ = prep_chunks[k]
        s = k % 2
        sem = ("dWP", "dWP1")[s]
        prep1_ld[k] = dma("scalar", sem,
                          lambda e, wdram=wdram, sl_=sl_, dst_=dst_, s=s:
                          e.dma_start(dst_(pchunk, s), sl_(wdram.ap())),
                          waits=[("cAC", prep1_ac.get(k - 2)), ("cDV", vRows)])

    def emit_prep1_abs(k):
        s = k % 2
        sem = ("dWP", "dWP1")[s]
        prep1_ac[k] = ac(lambda e, s=s, k=k: e.activation(
            pp_buf[...], pchunk[:, s * 4 * D:(s + 1) * 4 * D], ACTF.Abs,
            bias=0.0, scale=1.0, accum_out=abscols[:, k:k + 1]),
            waits=[(sem, prep1_ld[k])], drain=True)

    fin_done = [False]
    prep2_i = [0]
    fin_state = {}

    def emit_finalize():
        fin_done[0] = True
        vAC_prep = prep1_ac[len(prep_chunks) - 1]
        pbtr32 = pbtr[...].bitcast(F32)
        tr_prev = [evtr_state[0]]
        vGlo = None
        tr_ev = None
        for wi, (nel, thn, nthn, ngn) in enumerate((
                (DI * D, "thg", "nthg", "ngg"), (DI * D, "thu", "nthu", "ngu"),
                (D * DI, "thd", "nthd", "ngd"))):
            vA = dv(lambda e, wi=wi: e.reduce_sum(
                ptmp[:, wi:wi + 1], abscols[:, wi * UPW:(wi + 1) * UPW],
                axis=AX), waits=[("cAC", vAC_prep)], drain=True)
            vT = pe(lambda e, wi=wi, pbtr32=pbtr32: e.transpose(
                pbtr32[0:1, wi * 128:wi * 128 + 128], ptmp[:, wi:wi + 1],
                identf[...]),
                waits=[("cDV", vA), ("dX", vIf), ("cDV", tr_ev),
                       ("cDV", tr_prev[0])])
            vS = dv(lambda e, wi=wi, pbtr32=pbtr32: e.reduce_sum(
                glob[0:1, wi:wi + 1], pbtr32[0:1, wi * 128:wi * 128 + 128],
                axis=AX), waits=[("cPE", vT)], drain=True)
            tr_ev = vS
            dv(lambda e, wi=wi, nel=nel: e.tensor_single_scalar(
                glob[0:1, wi:wi + 1], glob[0:1, wi:wi + 1], 1.0 / nel,
                op=ALU.mult), drain=True)
            dv(lambda e, wi=wi: e.tensor_single_scalar(
                glob[0:1, wi:wi + 1], glob[0:1, wi:wi + 1], EPS, op=ALU.max),
                drain=True)
            dv(lambda e, wi=wi, thn=thn: e.tensor_single_scalar(
                glob[0:1, GLO[thn]:GLO[thn] + 1], glob[0:1, wi:wi + 1], 0.5,
                op=ALU.mult), drain=True)
            dv(lambda e, thn=thn, nthn=nthn: e.tensor_single_scalar(
                glob[0:1, GLO[nthn]:GLO[nthn] + 1],
                glob[0:1, GLO[thn]:GLO[thn] + 1], -1.0, op=ALU.mult),
                drain=True)
            vGlo = dv(lambda e, wi=wi, ngn=ngn: e.tensor_single_scalar(
                glob[0:1, GLO[ngn]:GLO[ngn] + 1], glob[0:1, wi:wi + 1], 0.5,
                op=ALU.mult), drain=True)
        vB_ = None
        for i in range(9):
            vB_ = pe(lambda e, i=i, pbtr32=pbtr32: e.matmul(
                pbtr32[:, 384 + i:385 + i], onesf_row[...],
                glob[0:1, 3 + i:4 + i], start=True, stop=True),
                waits=([("cDV", vGlo), ("cGP", vGC_end)]) if i == 0 else (),
                inc=(i == 8))
        fin_state["vB"] = vB_
        vBC_ = dv(lambda e, pbtr32=pbtr32: e.tensor_copy(
            bcast[:, 3:12], pbtr32[:, 384:393]), waits=[("cPE", vB_)])
        fin_state["vBC"] = vBC_
        evtr_state[0] = vBC_
        dbg_store("bcast", bcast[:, 3:12], [("cDV", vBC_)])

    prep2_dv = {}
    prep2_st = {}
    prep2_ac = {}
    prep2_ld = {}
    p2l, p2s, p2a = [0], [0], [0]
    p1l, p1a = [0], [0]
    NP2 = 3 * UPW


    def emit_prep2_load(k):
        wi, wdram, sl_, dst_ = prep_chunks[k]
        s = k % 2
        lsem = ("dWP", "dWP1")[s]
        prep2_ld[k] = dma("scalar", lsem,
                          lambda e, wdram=wdram, sl_=sl_, dst_=dst_, s=s:
                          e.dma_start(dst_(pchunk, s), sl_(wdram.ap())),
                          waits=[("cAC", prep2_ac.get(k - 2)),
                                 ("cAC", prep1_ac[len(prep_chunks) - 2 + s])])

    def emit_prep2_signs(k):
        wi, wdram, sl_, dst_ = prep_chunks[k]
        thn = ("thg", "thu", "thd")[wi]
        nthn = ("nthg", "nthu", "nthd")[wi]
        s = k % 2
        lsem = ("dWP", "dWP1")[s]
        ssem = ("dWS", "dWS1")[s]
        SGP = ppb[:, s * 4 * D:(s + 1) * 4 * D]
        PWQ = pwq[:, s * 4 * D:(s + 1) * 4 * D]
        ac(lambda e, s=s, thn=thn, SGP=SGP: e.activation(
            SGP, pchunk[:, s * 4 * D:(s + 1) * 4 * D], ACTF.Sign,
            bias=bcast[:, GLO[thn]:GLO[thn] + 1], scale=1.0),
            waits=[(lsem, prep2_ld[k]), ("cDV", fin_state["vBC"]),
                   ("cGP", prep2_dv.get(k - 2))], drain=True)
        vSM = ac(lambda e, s=s, nthn=nthn, PWQ=PWQ: e.activation(
            PWQ, pchunk[:, s * 4 * D:(s + 1) * 4 * D], ACTF.Sign,
            bias=bcast[:, GLO[nthn]:GLO[nthn] + 1], scale=1.0),
            waits=[(ssem, prep2_st.get(k - 2))], drain=True)
        prep2_ac[k] = vSM

    def emit_prep2_add(k):
        wi, wdram, sl_, dst_ = prep_chunks[k]
        sdram = (s_q["g"], s_q["u"], s_q["d"])[wi]
        s = k % 2
        ssem = ("dWS", "dWS1")[s]
        SGP = ppb[:, s * 4 * D:(s + 1) * 4 * D]
        PWQ = pwq[:, s * 4 * D:(s + 1) * 4 * D]
        vW = p.op("gpsimd", lambda e, SGP=SGP, PWQ=PWQ: e.tensor_tensor(
            PWQ, SGP, PWQ, op=ALU.add),
            [("cAC", prep2_ac[k])], "cGP", 1)
        prep2_dv[k] = vW
        prep2_st[k] = dma("sync", ssem,
                          lambda e, sdram=sdram, sl_=sl_, dst_=dst_, s=s:
                          e.dma_start(sl_(sdram.ap()), dst_(pwq, s)),
                          waits=[("cGP", vW)])

    p2s_hist = [0]

    def prep2_drain_adds(upto=None):
        tgt = p2s[0] if upto is None else upto
        while p2a[0] < tgt:
            emit_prep2_add(p2a[0])
            p2a[0] += 1

    xr_buf = sb(st_sweep, "xr_buf", [128, 2 * D], BF16)
    xrT = sb(st_sweep, "xrT", [128, 2 * D], BF16)
    ek_buf = sb(st_sweep, "ek_buf", [128, 2 * D], BF16)
    vstage = sb(st_sweep, "vstage", [128, 2 * NPAIR * PSTR], BF16)
    eq_buf = sb(st_sweep, "eq_buf", [128, D], BF16)
    nmax = sb(st_sweep, "nmax", [128, 4], F32)
    zkb = sb(st_sweep, "zkb", [128, 2], F32)
    rzkb = sb(st_sweep, "rzkb", [128, 2], F32)
    eqt_st = sb(st_sweep, "eqt_st", [128, 2 * D], BF16)

    vZst = p.op("gpsimd", lambda e: e.memset(vstage[...], 0.0),
                [("cDV", vRows)], "cGP", 1)

    def emit_proj(pb, XRT, wT, sum_off, bias_off, t, xw):
        vlast = None
        for h in range(2):
            for c in range(CH):
                vlast = pe(lambda e, h=h, c=c, pb=pb, XRT=XRT, wT=wT: e.matmul(
                    pb[:, 512 * h:512 * h + 512],
                    XRT[:, c * 128:(c + 1) * 128],
                    wT[:, c * D + 512 * h: c * D + 512 * h + 512],
                    start=(c == 0), stop=False),
                    waits=xw if (h == 0 and c == 0) else (), inc=False)
            vlast = pe(lambda e, h=h, t=t, sum_off=sum_off, pb=pb: e.matmul(
                pb[:, 512 * h:512 * h + 512],
                lhsT2[0:2, t * 128:(t + 1) * 128],
                rows2[0:2, sum_off + 512 * h: sum_off + 512 * h + 512],
                start=False, stop=True),
                waits=[("dX", vRows2)], inc=(h == 1))
        return vlast

    xr_dv = {}
    pbA_free = []
    pbV_free = []
    kv_last = None
    prep_i = 0
    vEK_last = None
    eqt_store = {}

    x_loads = {}

    def emit_xload(t):
        s = t % 2
        xw_extra = ([("cDV", xr_dv[t - 2]), ("cAC", stats_marks[t - 2][1])]
                    if t >= 2 else [])
        xw_sem = ("dX", "dX1")[s]
        vL = dma("sync", xw_sem, lambda e, t=t, s=s: e.dma_start(
            xt_buf[:, s * D:(s + 1) * D], x_tile_ap(t)),
            waits=xw_extra + dbg_w())
        x_loads[t] = (vL, xw_sem)
        return vL, xw_sem

    vL0, sem0 = emit_xload(0)
    emit_stats(0, vL0, sem0, xt_buf[:, 0:D])

    def emit_next_tile_stats(t):
        if t + 1 < TKV:
            vLn_, semn = emit_xload(t + 1)
            emit_stats(t + 1, vLn_, semn,
                       xt_buf[:, ((t + 1) % 2) * D:(((t + 1) % 2) + 1) * D])

    for t in range(TKV):
        s = t % 2
        vL, xw_sem = x_loads[t]
        xap = xt_buf[:, s * D:(s + 1) * D]
        XR = xr_buf[:, s * D:(s + 1) * D]
        XRT = xrT[:, s * D:(s + 1) * D]
        EK = ek_buf[:, s * D:(s + 1) * D]
        VST = vstage[:, s * NPAIR * PSTR:(s + 1) * NPAIR * PSTR]

        xr_dv[t] = dv(lambda e, xap=xap, t=t, XR=XR: e.tensor_single_scalar(
            XR, xap, stat[:, STT["r"] + t:STT["r"] + t + 1], op=ALU.mult),
            waits=[(xw_sem, vL), ("cAC", stats_marks[t][1])])

        vT = None
        for j in range(CH):
            vT = pe(lambda e, j=j, XR=XR: e.transpose(
                pbtr[:, j * 128:(j + 1) * 128], XR[:, j * 128:(j + 1) * 128],
                identb[...]),
                waits=[("cDV", xr_dv[t]), ("cDV", evtr_state[0])]
                if j == 0 else (),
                inc=(j == CH - 1))
        evtr_state[0] = dv(lambda e, XRT=XRT: e.tensor_copy(XRT, pbtr[...]),
                           waits=[("cPE", vT)])
        vXRT = evtr_state[0]

        vK = emit_proj(pbA, XRT, wckT, ROW["wcks"], ROW["bfk"], t,
                       [("cDV", vXRT)] + pbA_free)
        vNM = dv(lambda e, s=s: e.reduce_max(
            nmax[:, s:s + 1], pbA[...], axis=AX, negate=True),
            waits=[("cPE", vK)])
        vEK = ac(lambda e, s=s, EK=EK: e.activation(
            EK, pbA[...], ACTF.Exp, bias=nmax[:, s:s + 1], scale=1.0,
            accum_out=zkb[:, s:s + 1]), waits=[("cPE", vK), ("cDV", vNM)])
        pbA_free = [("cAC", vEK), ("cDV", vNM)]
        vEK_last = vEK
        vRZ = dv(lambda e, s=s: e.reciprocal(rzkb[:, s:s + 1], zkb[:, s:s + 1]),
                 waits=[("cAC", vEK)])
        vV = emit_proj(pbV, XRT, wvT, ROW["wvs"], ROW["bv"], t, pbV_free)
        vS1 = dv(lambda e, s=s, VST=VST: e.tensor_scalar(
            VST.rearrange("p (c k) -> p c k", k=PSTR)[:, :, 0:128],
            pbV[...].rearrange("p (c k) -> p c k", k=128),
            rzkb[:, s:s + 1], None, op0=ALU.mult),
            waits=[("cPE", vV), ("cGP", vZst)], drain=True)
        vS2 = dv(lambda e, s=s, VST=VST: e.tensor_copy(
            VST.rearrange("p (c k) -> p c k", k=PSTR)[:, :, 128:129],
            rzkb[:, s:s + 1].rearrange("p (c k) -> p c k", c=1)
            .broadcast_to([128, NPAIR, 1])))
        pbV_free = [("cDV", vS1)]
        if t == 0:
            dbg_store("ek0", EK, [("cAC", vEK)])
            dbg_store("zk0", zkb[:, 0:1], [("cAC", vEK)])
            dbg_store("vst0", VST, [("cDV", vS2)])

        vQ = None
        if t < TQ:
            vQ = emit_proj(pbA, XRT, wcqT, ROW["wcqs"], ROW["bfq"], t, pbA_free)
        for pr in range(NPAIR):
            b, slot = pr // 3, PSTR * (pr % 3)
            kv_last = pe(lambda e, pr=pr, b=b, slot=slot, EK=EK, VST=VST,
                         st_=(t == 0 and pr % 3 == 0),
                         sp_=(t == TKV - 1 and (pr % 3 == 2 or pr == NPAIR - 1)):
                         e.matmul(
                pbK[b][:, slot:slot + 129], EK[:, pr * 128:(pr + 1) * 128],
                VST[:, pr * PSTR:pr * PSTR + 129],
                start=st_, stop=sp_),
                waits=[("cAC", vEK), ("cDV", vS2)] if pr == 0 else (),
                inc=(pr == NPAIR - 1))

        emit_next_tile_stats(t)

        if t < TQ:
            vNMq = dv(lambda e, s=s: e.reduce_max(
                nmax[:, 2 + s:3 + s], pbA[...], axis=AX, negate=True),
                waits=[("cPE", vQ)])
            vEQ = ac(lambda e, s=s: e.activation(
                eq_buf[...], pbA[...], ACTF.Exp, bias=nmax[:, 2 + s:3 + s],
                scale=1.0), waits=[("cPE", vQ), ("cDV", vNMq)])
            pbA_free = [("cAC", vEQ), ("cDV", vNMq)]
            vTq = None
            for j in range(CH):
                vTq = pe(lambda e, j=j: e.transpose(
                    pbtr[:, j * 128:(j + 1) * 128],
                    eq_buf[:, j * 128:(j + 1) * 128], identb[...]),
                    waits=[("cAC", vEQ), ("cDV", evtr_state[0])]
                    if j == 0 else (),
                    inc=(j == CH - 1))
            evtr_state[0] = dv(lambda e, t=t, s=s: e.tensor_copy(
                eqt_st[:, s * D:(s + 1) * D], pbtr[...]),
                waits=[("cPE", vTq), ("dS2", eqt_store.get(t - 2))]
                + dbg_w())
            eqt_store[t] = dma("sync", "dS2", lambda e, t=t, s=s: e.dma_start(
                s_eqT.ap()[t * 128:(t + 1) * 128, :],
                eqt_st[:, s * D:(s + 1) * D]),
                waits=[("cDV", evtr_state[0])])
            if t == 0:
                dbg_store("eqT0", eqt_st[:, 0:D], [("cDV", evtr_state[0])])

        prep2_drain_adds(upto=p2s[0])
        for _ in range(2):
            if p1a[0] < p1l[0]:
                emit_prep1_abs(p1a[0])
                p1a[0] += 1
            elif p1a[0] >= len(prep_chunks) and not fin_done[0]:
                emit_finalize()
            elif fin_done[0] and p2s[0] < p2l[0]:
                emit_prep2_signs(p2s[0])
                p2s[0] += 1
        for _ in range(2):
            if prep_i < len(prep_chunks) and prep_i - p1a[0] < 2:
                emit_prep1_load(prep_i)
                prep_i += 1
                p1l[0] += 1
        if fin_done[0]:
            for _ in range(2):
                if p2l[0] < NP2 and p2l[0] - p2s[0] < 2:
                    emit_prep2_load(p2l[0])
                    p2l[0] += 1

    while p1a[0] < len(prep_chunks):
        if prep_i < len(prep_chunks) and prep_i - p1a[0] < 2:
            emit_prep1_load(prep_i)
            prep_i += 1
            p1l[0] += 1
        emit_prep1_abs(p1a[0])
        p1a[0] += 1
    if not fin_done[0]:
        emit_finalize()
    while p2a[0] < NP2:
        prep2_drain_adds()
        if p2s[0] < p2l[0]:
            emit_prep2_signs(p2s[0])
            p2s[0] += 1
        elif p2l[0] < NP2 and p2l[0] - p2s[0] < 2:
            emit_prep2_load(p2l[0])
            p2l[0] += 1
    vEQT_done = evtr_state[0]

    vKZ = p.op("gpsimd", lambda e: e.memset(kvs[...], 0.0),
               [("cDV", vRows)], "cGP", 1)
    kv_evac = None
    for pr in range(NPAIR):
        b, slot = pr // 3, PSTR * (pr % 3)
        w0 = [("cPE", kv_last), ("cGP", vKZ)] if pr == 0 else []
        dv(lambda e, pr=pr, b=b, slot=slot: e.tensor_copy(
            kvs[0:64, pr * PSTR:pr * PSTR + 64], pbK[b][0:64, slot:slot + 64]),
            waits=w0)
        dv(lambda e, pr=pr, b=b, slot=slot: e.tensor_copy(
            kvs[64:128, pr * PSTR + 64:pr * PSTR + 128],
            pbK[b][64:128, slot + 64:slot + 128]))
        dv(lambda e, pr=pr, b=b, slot=slot: e.tensor_copy(
            kvs[0:64, pr * PSTR + 128:pr * PSTR + 129],
            pbK[b][0:64, slot + 128:slot + 129]))
        kv_evac = dv(lambda e, pr=pr, b=b, slot=slot: e.tensor_copy(
            kvs[64:128, pr * PSTR + 129:pr * PSTR + 130],
            pbK[b][64:128, slot + 128:slot + 129]))
    dbg_store("kvs", kvs[...], [("cDV", kv_evac)])

    st_sweep.close()
    st_attn.close()

    wgT = sb(st_ffnA, "wgT", [128, CH * DI], BF16)
    wuT = sb(st_ffnA, "wuT", [128, CH * DI], BF16)
    ln2b = sb(st_ffnA, "ln2b", [128, D], F32)
    tq16 = sb(st_ffnA, "tq16", [128, 2 * D], F16)
    aq2 = sb(st_ffnA, "aq2", [128, D], BF16)
    aq2T = sb(st_ffnA, "aq2T", [128, 2 * D], BF16)
    silu_b = sb(st_ffnA, "silu_b", [128, 2 * 512], F32)
    hmid = sb(st_ffnA, "hmid", [128, DI], F32)
    hqb = sb(st_ffnA, "hqb", [128, DI], BF16)
    smal = sb(st_ffnA, "smal", [128, 32], F32)
    hmx = sb(st_ffnA, "hmx", [128, 16], F32)

    fA = fence()

    def _wdone(last):
        out = []
        for par, sem in enumerate(("dWS", "dWS1")):
            ks = [k for k in range(last + 1) if k % 2 == par and k in prep2_st]
            if ks:
                out.append((sem, prep2_st[max(ks)]))
        return out

    wg_done = _wdone(UPW - 1)
    wu_done = _wdone(2 * UPW - 1)
    wd_done = _wdone(3 * UPW - 1)

    eqt_loads = {}
    ctx_x_loads = {}
    ctx_pe = {}
    x2_st = {}
    ln2_ac = {}
    st1v = {}
    lnA_dv = {}

    def emit_eqt_load(t):
        if t >= TQ:
            return
        s = t % 2
        eqt_loads[t] = dma("sync", ("dLA", "dLA1")[s],
                           lambda e, t=t, s=s: e.dma_start(
            eqt_ld[:, s * D:(s + 1) * D], s_eqT.ap()[t * 128:(t + 1) * 128, :]),
            waits=[("dS2", eqt_store[t]), ("cPE", ctx_pe.get(t - 2))]
            + dbg_w())

    def emit_ctx_xload(t):
        if t >= TQ:
            return
        s = t % 2
        xw = ([("cDV", xr_dv[TKV - 1]), ("cAC", stats_marks[TKV - 1][1])]
              if t < 2 else
              [("cDV", ln2_ac.get(t - 2)), ("dS2", x2_st.get(t - 2)),
               ("cDV", lnA_dv.get(t - 2))])
        ctx_x_loads[t] = dma("sync", ("dX", "dX1")[s],
                             lambda e, t=t, s=s: e.dma_start(
            xt_buf[:, s * D:(s + 1) * D], x_tile_ap(t)), waits=xw + dbg_w())

    emit_ctx_xload(0)
    emit_eqt_load(0)
    emit_eqt_load(1)
    emit_ctx_xload(1)

    wgld, wuld = {}, {}
    for c in range(CH):
        wgld[c] = (("dWL", "dWL1")[c % 2], dma(
            "sync", ("dWL", "dWL1")[c % 2], lambda e, c=c: e.dma_start(
                wgT[:, c * DI:(c + 1) * DI],
                s_q["g"].ap()[:, c * 128:(c + 1) * 128],
                transpose=True), waits=wg_done + fA))
    for c in range(CH):
        wuld[c] = (("dWL", "dWL1")[c % 2], dma(
            "sync", ("dWL", "dWL1")[c % 2], lambda e, c=c: e.dma_start(
                wuT[:, c * DI:(c + 1) * DI],
                s_q["u"].ap()[:, c * 128:(c + 1) * 128],
                transpose=True), waits=wu_done + fA))

    GB = [pbA[:, 0:512], pbA[:, 512:D], pbK[2]]
    UB = [pbV[:, 0:512], pbV[:, 512:D]]
    gb_free = [pbA_free[:], pbA_free[:], [("cDV", kv_evac)]]
    ub_free = [pbV_free[:], pbV_free[:]]
    silu_free = [[], []]
    tr_free = [("cDV", vEQT_done)]
    pbtr32m = pbtr[...].bitcast(F32)
    CTXB = (pbK[0], pbK[1], pbtr32m)
    hq_st = {}
    vHQ_last = None
    ctx_banks_free = [("cDV", kv_evac)]
    ctx_vm = {}

    def emit_ctx(t):
        nonlocal ctx_banks_free
        s = t % 2
        X2 = xt_buf[:, s * D:(s + 1) * D]
        vC = None
        for pr in range(NPAIR):
            b, slot = pr // 3, PSTR * (pr % 3)
            vC = pe(lambda e, pr=pr, b=b, slot=slot, t=t, s=s: e.matmul(
                CTXB[b][:, slot:slot + 130],
                eqt_ld[:, s * D + pr * 128:s * D + (pr + 1) * 128],
                kvs[:, pr * PSTR:pr * PSTR + 130],
                start=(pr % 3 == 0), stop=(pr % 3 == 2 or pr == NPAIR - 1)),
                waits=(ctx_banks_free + [("cDV", vEQT_done),
                                         (("dLA", "dLA1")[s], eqt_loads[t])]
                       + ([("cDV", st1v[t - 1]["aqT_ev"])] if t >= 1 else []))
                if pr == 0 else (),
                inc=(pr == NPAIR - 1))
        ctx_pe[t] = vC
        vRD = None
        for b in range(3):
            npr = 3 if b < 2 else 2
            vRD = dv(lambda e, b=b, npr=npr: e.reciprocal(
                recden[:, 6 * b:6 * b + 2 * npr]
                .rearrange("p (c k) -> p c k", k=2),
                CTXB[b][:, 0:npr * PSTR].rearrange("p (c k) -> p c k", k=PSTR)
                [:, :, 128:130]),
                waits=[("cPE", vC)] if b == 0 else ())
        vM = None
        for b in range(3):
            npr = 3 if b < 2 else 2
            vM = dv(lambda e, b=b, npr=npr: e.tensor_tensor(
                tmp_ctx[:, b * 384:b * 384 + npr * 128]
                .rearrange("p (c h k) -> p c h k", h=2, k=64),
                CTXB[b][:, 0:npr * PSTR].rearrange("p (c k) -> p c k", k=PSTR)
                [:, :, 0:128].rearrange("p c (h k) -> p c h k", k=64),
                recden[:, 6 * b:6 * b + 2 * npr]
                .rearrange("p (c h k) -> p c h k", h=2, k=1)
                .broadcast_to([128, npr, 2, 64]),
                op=ALU.mult),
                waits=([("cDV", vRD)] + ([("cDV", ln2_ac[t - 1])]
                                         if t >= 1 else []))
                if b == 0 else (),
                drain=(b == 0))
        ctx_vm[t] = vM
        vAdd = dv(lambda e, X2=X2: e.tensor_add(X2, X2, tmp_ctx[...]),
                  waits=[(("dX", "dX1")[s], ctx_x_loads[t])], drain=True)
        ctx_banks_free = [("cDV", vM)]
        x2_st[t] = dma("sync", "dS2", lambda e, t=t, X2=X2: e.dma_start(
            s_x2.ap()[t * 128:(t + 1) * 128, :], X2), waits=[("cDV", vAdd)])
        dv(lambda e, t=t, X2=X2: e.reduce_sum(
            m2r2[:, M2["sum"] + t:M2["sum"] + t + 1], X2, axis=AX),
            drain=True)
        ln2_ac[t] = dv(lambda e, t=t, X2=X2: e.scalar_tensor_tensor(
            tmp_ctx[...], X2, 0.0, X2, op0=ALU.add, op1=ALU.mult,
            accum_out=m2r2[:, M2["ssq"] + t:M2["ssq"] + t + 1]),
            drain=True)
        if t == 0:
            dbg_store("x20", X2, [("cDV", vAdd)])

    def emit_ln2fin(t):
        c2 = M2
        dv(lambda e, t=t: e.tensor_single_scalar(
            m2r2[:, c2["m"] + t:c2["m"] + t + 1],
            m2r2[:, c2["sum"] + t:c2["sum"] + t + 1],
            1.0 / D, op=ALU.mult), waits=[("cDV", ln2_ac[t])], drain=True)
        dv(lambda e, t=t: e.tensor_tensor(
            m2r2[:, c2["sp"] + t:c2["sp"] + t + 1],
            m2r2[:, c2["m"] + t:c2["m"] + t + 1],
            m2r2[:, c2["m"] + t:c2["m"] + t + 1], op=ALU.mult), drain=True)
        dv(lambda e, t=t: e.scalar_tensor_tensor(
            m2r2[:, c2["sp2"] + t:c2["sp2"] + t + 1],
            m2r2[:, c2["ssq"] + t:c2["ssq"] + t + 1], 1.0 / D,
            m2r2[:, c2["sp"] + t:c2["sp"] + t + 1],
            op0=ALU.mult, op1=ALU.subtract), drain=True)
        dv(lambda e, t=t: e.tensor_single_scalar(
            m2r2[:, c2["sp"] + t:c2["sp"] + t + 1],
            m2r2[:, c2["sp2"] + t:c2["sp2"] + t + 1],
            EPS, op=ALU.add), drain=True)
        u5 = dv(lambda e, t=t: e.reciprocal(
            m2r2[:, c2["sp2"] + t:c2["sp2"] + t + 1],
            m2r2[:, c2["sp"] + t:c2["sp"] + t + 1]), drain=True)
        st1v[t] = dict(vR2=ac(lambda e, t=t: e.activation(
            m2r2[:, c2["r"] + t:c2["r"] + t + 1],
            m2r2[:, c2["sp2"] + t:c2["sp2"] + t + 1],
            ACTF.Sqrt, bias=0.0, scale=1.0), waits=[("cDV", u5)], drain=True))

    def stage1_front_a(t):
        s = t % 2
        sm = 16 * s
        X2 = xt_buf[:, s * D:(s + 1) * D]
        vLN = dv(lambda e, X2=X2, t=t: e.tensor_scalar(
            ln2b[...], X2, m2r2[:, M2["m"] + t:M2["m"] + t + 1],
            m2r2[:, M2["r"] + t:M2["r"] + t + 1],
            op0=ALU.subtract, op1=ALU.mult),
            waits=[("cAC", st1v[t]["vR2"])] + dbg_w(), drain=True)
        lnA_dv[t] = vLN
        dv(lambda e, sm=sm: e.reduce_max(smal[:, sm:sm + 1], ln2b[...],
                                         axis=AX, apply_absolute_value=True),
           drain=True)
        dv(lambda e, sm=sm: e.tensor_single_scalar(
            smal[:, sm + 1:sm + 2], smal[:, sm:sm + 1], EPS, op=ALU.max),
            drain=True)
        dv(lambda e, sm=sm: e.reciprocal(smal[:, sm + 2:sm + 3],
                                         smal[:, sm + 1:sm + 2]), drain=True)
        vS127 = dv(lambda e, sm=sm: e.tensor_single_scalar(
            smal[:, sm + 3:sm + 4], smal[:, sm + 2:sm + 3], 127.0,
            op=ALU.mult), drain=True)
        st1v[t].update(sm=sm, vS127=vS127, vLN=vLN)

    def stage1_front_b(t):
        s = t % 2
        sm = st1v[t]["sm"]
        dv(lambda e, sm=sm: e.tensor_single_scalar(
            smal[:, sm + 4:sm + 5], smal[:, sm + 1:sm + 2], 1.0 / 127.0,
            op=ALU.mult), drain=True)
        vGS = dv(lambda e, sm=sm: e.tensor_tensor(
            smal[:, sm + 5:sm + 6], smal[:, sm + 4:sm + 5],
            bcast[:, GLO["ngg"]:GLO["ngg"] + 1], op=ALU.mult),
            waits=[("cDV", fin_state["vBC"])], drain=True)
        vUS = dv(lambda e, sm=sm: e.tensor_tensor(
            smal[:, sm + 6:sm + 7], smal[:, sm + 4:sm + 5],
            bcast[:, GLO["ngu"]:GLO["ngu"] + 1], op=ALU.mult), drain=True)
        vTQ = ac(lambda e, s=s, sm=sm: e.activation(
            tq16[:, s * D:(s + 1) * D], ln2b[...], ACTF.Copy,
            bias=1536.0, scale=smal[:, sm + 3:sm + 4]),
            waits=[("cDV", st1v[t]["vS127"])], drain=True)
        vAQ = ac(lambda e, s=s: e.activation(
            aq2[...], tq16[:, s * D:(s + 1) * D], ACTF.Copy,
            bias=-1536.0, scale=1.0),
            waits=([("cPE", last_vTA[0])]
                   if last_vTA[0] is not None else [])
            + dbg_w(), drain=True)
        if t == 0:
            dbg_store("ln20", ln2b[...], [("cDV", st1v[t]["vLN"])])
            dbg_store("aq20", aq2[...], [("cAC", vAQ)])
        st1v[t].update(vAQ=vAQ, vGS=vGS, vUS=vUS)

    last_vTA = [None]

    def stage1_pe(t):
        nonlocal_tr = st1v[t]
        s = t % 2
        vTA = None
        for j in range(CH):
            vTA = pe(lambda e, j=j: e.transpose(
                pbtr[:, j * 128:(j + 1) * 128], aq2[:, j * 128:(j + 1) * 128],
                identb[...]),
                waits=([("cAC", nonlocal_tr["vAQ"]), ("cDV", ctx_vm[t])]
                       + tr_free[:])
                if j == 0 else (),
                inc=(j == CH - 1))
        last_vTA[0] = vTA
        ev = dv(lambda e, s=s: e.tensor_copy(aq2T[:, s * D:(s + 1) * D],
                                             pbtr[...]),
                waits=[("cPE", vTA)])
        nonlocal_tr["aqT_ev"] = ev
        tr_free[:] = [("cDV", ev)]

    def main_mm(t, nxt=None):
        nonlocal vHQ_last
        s = t % 2
        sm = st1v[t]["sm"]
        aqT_ev = st1v[t]["aqT_ev"]
        vGS, vUS = st1v[t]["vGS"], st1v[t]["vUS"]
        if t == 0:
            dbg_store("sgq", s_q["g"].ap()[:, 0:128], wg_done)
            dbg_store("wgT0", wgT[:, 0:DI], [wgld[c] for c in range(CH)])
            dbg_store("wuT0", wuT[:, 0:DI], [wuld[c] for c in range(CH)])
            dbg_store("aqT0", aq2T[:, 0:D], [("cDV", aqT_ev)])
        for oc in range(CH):
            if nxt is not None:
                if oc == 2:
                    emit_ln2fin(nxt)
                elif oc == 4:
                    stage1_front_a(nxt)
                elif oc == 5:
                    stage1_front_b(nxt)
            gp, up_ = GB[oc % 3], UB[oc % 2]
            gbi = oc % 3
            ss = oc % 2
            vG = None
            for c in range(CH):
                vG = pe(lambda e, c=c, oc=oc, gp=gp, s=s: e.matmul(
                    gp[:, 0:512], aq2T[:, s * D + c * 128:s * D + (c + 1) * 128],
                    wgT[:, c * DI + 512 * oc: c * DI + 512 * oc + 512],
                    start=(c == 0), stop=(c == CH - 1)),
                    waits=(([("cDV", aqT_ev)] + gb_free[gbi])
                           if c == 0 else []) + [wgld[c]],
                    inc=(c == CH - 1))
            vU = None
            for c in range(CH):
                vU = pe(lambda e, c=c, oc=oc, up_=up_, s=s: e.matmul(
                    up_[:, 0:512], aq2T[:, s * D + c * 128:s * D + (c + 1) * 128],
                    wuT[:, c * DI + 512 * oc: c * DI + 512 * oc + 512],
                    start=(c == 0), stop=(c == CH - 1)),
                    waits=(ub_free[ss] if c == 0 else []) + [wuld[c]],
                    inc=(c == CH - 1))
            vSI = ac(lambda e, ss=ss, gp=gp, sm=sm: e.activation(
                silu_b[:, ss * 512:(ss + 1) * 512], gp[:, 0:512], ACTF.Silu,
                bias=0.0, scale=smal[:, sm + 5:sm + 6]),
                waits=[("cPE", vG), ("cDV", vGS)] + silu_free[ss], drain=True)
            gb_free[gbi] = [("cAC", vSI)]
            if t == 0 and oc == 0:
                dbg_store("silu00", silu_b[:, 0:512], [("cAC", vSI)])
            vH = dv(lambda e, oc=oc, ss=ss, up_=up_, sm=sm:
                    e.scalar_tensor_tensor(
                hmid[:, oc * 512:(oc + 1) * 512], up_[:, 0:512],
                smal[:, sm + 6:sm + 7], silu_b[:, ss * 512:(ss + 1) * 512],
                op0=ALU.mult, op1=ALU.mult),
                waits=[("cPE", vU), ("cAC", vSI), ("cDV", vUS)]
                + (dbg_w() if oc == 0 else []), drain=True)
            dv(lambda e, oc=oc, s=s: e.reduce_max(
                hmx[:, s * 8 + oc:s * 8 + oc + 1],
                hmid[:, oc * 512:(oc + 1) * 512], axis=AX,
                apply_absolute_value=True), drain=True)
            ub_free[ss] = [("cDV", vH)]
            silu_free[ss] = [("cDV", vH)]
        if nxt is not None:
            stage1_pe(nxt)
        vHM_final = dv(lambda e, sm=sm, s=s: e.reduce_max(
            smal[:, sm + 7:sm + 8], hmx[:, s * 8:(s + 1) * 8], axis=AX),
            drain=True)
        dv(lambda e, sm=sm: e.tensor_single_scalar(
            smal[:, sm + 8:sm + 9], smal[:, sm + 7:sm + 8], EPS, op=ALU.max),
            drain=True)
        dv(lambda e, sm=sm: e.reciprocal(smal[:, sm + 9:sm + 10],
                                         smal[:, sm + 8:sm + 9]), drain=True)
        vS127h = dv(lambda e, sm=sm: e.tensor_single_scalar(
            smal[:, sm + 7:sm + 8], smal[:, sm + 9:sm + 10], 127.0,
            op=ALU.mult), drain=True)
        vDS_last = dv(lambda e, t=t, sm=sm: e.tensor_single_scalar(
            dscale[:, t:t + 1], smal[:, sm + 8:sm + 9], 1.0 / 127.0,
            op=ALU.mult), drain=True)
        vHQ = None
        for sl_ in range(4):
            vTQh = ac(lambda e, sl_=sl_, s=s, sm=sm: e.activation(
                tq16[:, s * D:(s + 1) * D], hmid[:, sl_ * D:(sl_ + 1) * D],
                ACTF.Copy, bias=1536.0, scale=smal[:, sm + 7:sm + 8]),
                waits=[("cDV", vS127h)] if sl_ == 0 else (), drain=True)
            vHQ = ac(lambda e, sl_=sl_, s=s: e.activation(
                hqb[:, sl_ * D:(sl_ + 1) * D], tq16[:, s * D:(s + 1) * D],
                ACTF.Copy, bias=-1536.0, scale=1.0),
                waits=(([("dS2", hq_st.get(t - 1))] + dbg_w())
                       if sl_ == 0 else ()), drain=True)
        vHQ_last = vHQ
        ffnA_last_ac[0] = vHQ
        ffnA_last_dv[0] = vDS_last
        if t == 0:
            dbg_store("hmid0", hmid[...], [("cDV", vHM_final)])
            dbg_store("hq0", hqb[...], [("cAC", vHQ)])
        hq_st[t] = dma("sync", "dS2", lambda e, t=t: e.dma_start(
            s_hq4.ap()[t * DI:(t + 1) * DI, :]
            .rearrange("(c n) k -> n c k", n=128),
            hqb[...].rearrange("p (c k) -> p c k", k=128),
            allow_slow_non_contiguous=True),
            waits=[("cAC", vHQ)])
        ffnA_pe_last[0] = vU

    ffnA_pe_last = [None]
    ffnA_last_dv = [None]
    ffnA_last_ac = [None]
    for t in range(TQ):
        emit_ctx(t)
        if t >= 1:
            main_mm(t - 1, nxt=t)
        else:
            emit_ln2fin(0)
            stage1_front_a(0)
            stage1_front_b(0)
            stage1_pe(0)
        emit_eqt_load(t + 2)
        emit_ctx_xload(t + 2)
    main_mm(TQ - 1)

    st_ffnA.close()
    st_long.close()
    wdT = sb(st_ffnB, "wdT", [128, CHI * D], BF16)
    hqT = sb(st_ffnB, "hqT", [128, 2 * DI], BF16)
    x2tB = sb(st_ffnB, "x2tB", [128, 2 * D], F32)
    ybuf = sb(st_ffnB, "ybuf", [128, 2 * D], F32)
    dsct = sb(st_ffnB, "dsct", [128, 2], F32)

    fBpe = [("cPE", ffnA_pe_last[0])]
    fBfree = fBpe + [("cAC", ffnA_last_ac[0]), ("cDV", ffnA_last_dv[0])]
    fBwd = fBfree + [("dS2", x2_st[TQ - 1])]

    hq_ld = {}
    hq_cons = {}

    def emit_hq_load(t):
        if t >= TQ:
            return
        s = t % 2
        lbsem = ("dLB", "dLB1")[s]
        hq_ld[t] = dma("scalar", lbsem, lambda e, t=t, s=s: e.dma_start(
            hqT[:, s * DI:(s + 1) * DI],
            s_hq4.ap()[t * DI:(t + 1) * DI, :], transpose=True),
            waits=[("dS2", hq_st[t]), ("cPE", hq_cons.get(t - 2))] + fBpe
            + dbg_w())

    wdld = {}

    def emit_wd_loads(c0, c1):
        for c in range(c0, c1):
            sem = ("dWL2", "dWL3")[c % 2]
            wdld[c] = (sem, dma("scalar", sem, lambda e, c=c: e.dma_start(
                wdT[:, c * D:(c + 1) * D],
                s_q["d"].ap()[:, c * 128:(c + 1) * 128],
                transpose=True), waits=wd_done + fBwd))

    emit_hq_load(0)
    emit_wd_loads(0, 8)
    emit_hq_load(1)
    emit_wd_loads(8, CHI)

    DB = [pbA, pbV]
    db_free = [fBfree, fBfree]
    y_st = {}
    x2_cons = {}
    for t in range(TQ):
        s = t % 2
        HQT = hqT[:, s * DI:(s + 1) * DI]
        X2B = x2tB[:, s * D:(s + 1) * D]
        YB = ybuf[:, s * D:(s + 1) * D]
        lbsem = ("dLB", "dLB1")[s]
        lxsem = ("dLX", "dLX1")[s]
        vLH = hq_ld[t]
        vLX = dma("sync", lxsem, lambda e, t=t, X2B=X2B: e.dma_start(
            X2B, s_x2.ap()[t * 128:(t + 1) * 128, :]),
            waits=[("cDV", x2_cons.get(t - 2)), ("dS2", x2_st[t])] + fBpe)
        vDSC = dv(lambda e, t=t, s=s: e.tensor_tensor(
            dsct[:, s:s + 1], dscale[:, t:t + 1],
            bcast[:, GLO["ngd"]:GLO["ngd"] + 1], op=ALU.mult),
            waits=[("cDV", fin_state["vBC"])] + fBpe, drain=True)
        pb = DB[t % 2]
        vTlast = None
        for jh in range(2):
            for c in range(CHI):
                vTlast = pe(lambda e, c=c, jh=jh, pb=pb, HQT=HQT, s=s:
                            e.matmul(
                    pb[:, 512 * jh:512 * jh + 512],
                    HQT[:, c * 128:(c + 1) * 128],
                    wdT[:, c * D + 512 * jh: c * D + 512 * jh + 512],
                    start=(c == 0), stop=(c == CHI - 1)),
                    waits=(([(lbsem, vLH)] + db_free[t % 2])
                           if (c == 0 and jh == 0) else [])
                    + ([wdld[c]] if jh == 0 else []),
                    inc=(jh == 1 and c == CHI - 1))
        hq_cons[t] = vTlast
        emit_hq_load(t + 2)
        vY = None
        for jh in range(2):
            vY = dv(lambda e, jh=jh, pb=pb, X2B=X2B, YB=YB, s=s:
                    e.scalar_tensor_tensor(
                        YB[:, 512 * jh:512 * jh + 512],
                        pb[:, 512 * jh:512 * jh + 512], dsct[:, s:s + 1],
                        X2B[:, 512 * jh:512 * jh + 512],
                        op0=ALU.mult, op1=ALU.add),
                    waits=[("cPE", vTlast), (lxsem, vLX), ("cDV", vDSC),
                           ("dY", y_st.get(t - 2))], drain=True)
        x2_cons[t] = vY
        db_free[t % 2] = [("cDV", vY)]
        y_st[t] = dma("sync", "dY", lambda e, t=t, YB=YB: e.dma_start(
            y_out.ap()[t * 128:(t + 1) * 128, :], YB), waits=[("cDV", vY)])

    p.op("sync", lambda e: e.nop(), waits=[("dY", y_st[TQ - 1])])

    sems = {}
    sem_stack = ExitStack()
    for name in ("cPE", "cDV", "cAC", "cGP", "dX", "dX1", "dS2", "dY", "dWP",
                 "dWP1", "dWP2", "dWS", "dWS1", "dWS2", "dWL", "dWL1", "dWL2",
                 "dWL3", "dGC", "dLA", "dLA1", "dLB", "dLB1", "dLX", "dLX1"):
        sems[name] = sem_stack.enter_context(nc.semaphore(name))
    p.materialize(nc, sems)
    sem_stack.close()
    st_ffnB.close()
    st_base.close()
    return nc, sorted(dbg_t)


def _shard_inputs(inputs):
    hs = np.asarray(inputs["L0_attn_hidden_states"], np.float32)
    B, S, _ = hs.shape
    half = S // 2
    base = {
        "wq": np.ascontiguousarray(inputs["L0_attn_Wq"], dtype=np.float32),
        "wk": np.ascontiguousarray(inputs["L0_attn_Wk"], dtype=np.float32),
        "wv": np.ascontiguousarray(inputs["L0_attn_Wv"], dtype=np.float32),
        "wfq": np.ascontiguousarray(inputs["L0_attn_Wfq"], dtype=np.float32),
        "wfk": np.ascontiguousarray(inputs["L0_attn_Wfk"], dtype=np.float32),
        "bq": np.ascontiguousarray(inputs["L0_attn_bq"], dtype=np.float32),
        "bk": np.ascontiguousarray(inputs["L0_attn_bk"], dtype=np.float32),
        "bv": np.ascontiguousarray(inputs["L0_attn_bv"], dtype=np.float32),
        "bfq": np.ascontiguousarray(inputs["L0_attn_bfq"], dtype=np.float32),
        "bfk": np.ascontiguousarray(inputs["L0_attn_bfk"], dtype=np.float32),
        "wg": np.ascontiguousarray(inputs["L0_ffn_w_gate"], dtype=np.float32),
        "wu": np.ascontiguousarray(inputs["L0_ffn_w_up"], dtype=np.float32),
        "wd": np.ascontiguousarray(inputs["L0_ffn_w_down"], dtype=np.float32),
        "id_b": np.eye(128, dtype=bfd),
        "id_f": np.eye(128, dtype=np.float32),
    }
    in_maps = []
    for c in range(2 * B):
        b, h = c // 2, c % 2
        m = dict(base)
        m["x_own"] = np.ascontiguousarray(hs[b, h * half:(h + 1) * half])
        m["x_oth"] = np.ascontiguousarray(hs[b, (1 - h) * half:(2 - h) * half])
        in_maps.append(m)
    return in_maps


def kernel(**inputs):
    nc, _ = build_nc(16, 32)
    in_maps = _shard_inputs(inputs)
    res = run_bass_kernel_spmd(nc, in_maps, core_ids=list(range(8)))
    hs = inputs["L0_attn_hidden_states"]
    B, S, E = hs.shape
    out = np.empty((B, S, E), np.float32)
    half = S // 2
    for c in range(2 * B):
        b, h = c // 2, c % 2
        out[b, h * half:(h + 1) * half] = res.results[c]["y"]
    return out
